# version 12
# baseline (speedup 1.0000x reference)
import numpy as np
from contextlib import ExitStack

import concourse.bass as bass
import concourse.mybir as mybir
from concourse.bass_utils import run_bass_kernel_spmd

F32 = mybir.dt.float32
BF16 = mybir.dt.bfloat16
AF = mybir.ActivationFunctionType
OP = mybir.AluOpType

D = 1024
T = 2048
NCH = 8
NBUF = 4
EPS_LN = 1e-5
MIN_DECAY = 1e-3
SCAN_EPS = 1e-7
CLAMP_A = 1.0 - 1e-5


def _weight_loads():
    WL = []
    for tau in (0, 1):
        for mg in range(16):
            for part in (0, 1):
                WL.append(("s1", tau, mg, part))
        for st in ("s2", "s3"):
            for mg in range(16):
                for kg in range(4):
                    WL.append((st, tau, mg, kg))
        for mg in range(4):
            for kg in range(4):
                WL.append(("s4", tau, mg, kg))
    G = {k: g for g, k in enumerate(WL)}
    return WL, G


WL, GIDX = _weight_loads()
NLOADS = len(WL)
QID = [0 for g in range(NLOADS)]
QORD = []
_cnt = [0, 0]
for _g in range(NLOADS):
    QORD.append(_cnt[QID[_g]])
    _cnt[QID[_g]] += 1
B2_LEVEL = 6
_L_A_END = NBUF
_L_B1_0_END = GIDX[("s4", 0, 0, 0)]
_L_B2_0_END = GIDX[("s1", 1, 0, 0)]
_L_B1_1_END = GIDX[("s4", 1, 0, 0)]
_L_B2_1_END = NLOADS


class _MockEng:
    def __getattr__(self, name):
        return lambda *a, **k: None


class _MockTensor:
    def __getitem__(self, *a):
        return self

    def rearrange(self, *a, **k):
        return self

    def ap(self):
        return self


class Ctx:

    def __init__(self):
        self.counting = True
        self.cnt = {}
        self.M = {}
        self.refs = set()
        self.sems = None
        self.t = None
        self._ssn = 0

    def inc(self, ins, sem, n=1):
        amt = 16 * n if sem.startswith("s_") else n
        if self.counting:
            self.cnt[sem] = self.cnt.get(sem, 0) + amt
        else:
            ins.then_inc(self.sems[sem], amt)

    def mark(self, name, sem):
        if self.counting:
            assert name not in self.M, f"duplicate mark {name}"
            self.M[name] = (sem, self.cnt.get(sem, 0))

    def wt(self, eng, name):
        if self.counting:
            self.refs.add(name)
            return
        semname, v = self.M[name]
        if v > 0:
            eng.wait_ge(self.sems[semname], v)

    def wt_raw(self, eng, sem, v):
        if not self.counting and v > 0:
            eng.wait_ge(self.sems[sem], v)

    def ss(self, ins, eng, sem):
        name = f"_ss_{sem}_{self._ssn}"
        self._ssn += 1
        self.inc(ins, sem)
        self.mark(name, sem)
        self.wt(eng, name)



N_CONST = 10
C1 = float(np.log1p(-1e-5))
LCL = float(np.log(1e-7))


def em_S0_sync(c, e):
    t = c.t
    for name, dst in [("wgin_h", "wg_in"), ("wga_h", "wg_a"),
                      ("bcin_h", "bcol_in"), ("bca_h", "bcol_a"),
                      ("acol_h", "acol"), ("s1c_h", "s1c"), ("s2c_h", "s2c"),
                      ("hs1c_h", "hs1c"), ("hs2c_h", "hs2c"), ("s3c_h", "s3c")]:
        if name in ("wgin_h", "wga_h"):
            src = t[name].ap().rearrange("h i j -> i h j")
        else:
            src = t[name].ap()
        c.inc(e.dma_start(t[dst][:], src), "s_c")


def em_S0_vector(c, e):
    t = c.t
    e.memset(t["ones_row"][:], 1.0)
    e.memset(t["cst_col"][:], 1.0 / D)
    c.wt_raw(e, "s_c", 16 * N_CONST)
    tr = t["tiny"][:, 0:8]
    tb = t["tiny"][:, 8:16]
    tl = t["tiny"][:, 24:32]
    ins = e.tensor_single_scalar(tr, t["acol"][:], 0.0, OP.max)
    c.inc(ins, "c_v")
    c.mark("v_pro1", "c_v")
    c.wt(e, "a_pro1")
    c.ss(e.scalar_tensor_tensor(tb, tl, MIN_DECAY, tr, OP.add, OP.add), e, "c_v")
    ins = e.tensor_single_scalar(t["nspcol"][:], tb, -1.0, OP.mult)
    c.inc(ins, "c_v")
    c.mark("v_pro2", "c_v")


def em_S0_scalar(c, e):
    t = c.t
    tb = t["tiny"][:, 8:16]
    te = t["tiny"][:, 16:24]
    tl = t["tiny"][:, 24:32]
    c.wt_raw(e, "s_c", 16 * N_CONST)
    c.ss(e.activation(tb, t["acol"][:], AF.Abs), e, "c_a")
    c.ss(e.activation(te, tb, AF.Exp, scale=-1.0), e, "c_a")
    ins = e.activation(tl, te, AF.Ln, bias=1.0)
    c.inc(ins, "c_a")
    c.mark("a_pro1", "c_a")


def em_S0_gpsimd(c, e):
    pass


def em_S0_tensor(c, e):
    pass




def _pt(c, name, ch):
    return c.t[f"p{name}{ch % 2}"]


def _xload(c, e, ch, sem):
    src = c.t["x_h"].ap()[ch * 128:(ch + 1) * 128, :]
    c.inc(e.dma_start(c.t["xc2"][:, ch % 2, :], src), sem)


XQ = {0: ("s_x", 1), 3: ("s_x", 2), 6: ("s_x", 3),
      1: ("s_x2", 1), 4: ("s_x2", 2), 7: ("s_x2", 3),
      2: ("s_x3", 1), 5: ("s_x3", 2)}


def _wt_x1r(c, e, ch):
    sem, o = XQ[ch]
    c.wt_raw(e, sem, 16 * o)


def em_A_sync(c, e):
    for ch in (0, 3, 6):
        if ch >= 2:
            c.wt(e, f"t_A0st_{ch - 2}")
        _xload(c, e, ch, "s_x")
    for ch in range(8):
        c.wt(e, [f"t_A0st_{6 + ch}", f"p_xr_{ch - 2}"][ch >= 2])
        _xload(c, e, ch, "s_x")
    for ch in range(8):
        c.wt(e, [f"p_xr_{6 + ch}", f"v_x1_{ch - 2}"][ch >= 2])
        _xload(c, e, ch, "s_x")


def em_A_pool(c, e, dbg=False):
    t = c.t
    dbg2 = dbg == 2

    def dump(ch, name, tile, markname):
        if not dbg2:
            return
        c.wt(e, markname)
        c.inc(e.dma_start(t[name].ap()[ch * 128:(ch + 1) * 128, :],
                          _pt(c, tile, ch)[:]), "s_d3v")
        c.mark(f"d3_{name}_{ch}", "s_d3v")

    def gated(ch, hh):
        sl = slice(hh * 1024, (hh + 1) * 1024)
        if hh == 0:
            dump(ch, "dbg_ln1", "B", f"v_ln1s_{ch}")
            c.wt(e, f"a_gxh0_{ch}")
            c.wt(e, f"v_ln1s_{ch}")
        else:
            c.wt(e, f"a_gx_{ch}")
        A = _pt(c, "A", ch)
        ins = e.tensor_mul(A[:, sl], A[:, sl], _pt(c, "B", ch)[:, sl])
        c.inc(ins, "c_p")
        c.mark([f"p_g0_{ch}", f"p_gated_{ch}"][hh], "c_p")
        if hh == 1:
            dump(ch, "dbg_gx", "A", f"p_gated_{ch}")

    def xt(ch):
        dump(ch, "dbg_mu", "C", f"a_mult_{ch}")
        dump(ch, "dbg_nrm", "D", f"v_normed_{ch}")
        c.wt(e, f"v_normed_{ch}")
        ins = e.tensor_scalar(_pt(c, "C", ch)[:], _pt(c, "D", ch)[:],
                              SCAN_EPS, 1.0, OP.add, OP.mult)
        c.inc(ins, "c_p")
        c.mark(f"p_xt_{ch}", "c_p")

    def xr(ch):
        c.wt_raw(e, "s_x", 16 * (4 + ch))
        if ch < 2:
            c.wt(e, "v_A0bc")
        else:
            c.wt(e, f"v_normed_{ch - 2}")
        ins = e.tensor_mul(_pt(c, "A", ch)[:], t["xc2"][:, ch % 2, :],
                           t["r_b"][:])
        c.inc(ins, "c_p")
        c.mark(f"p_xr_{ch}", "c_p")

    c.wt(e, "t_A0st_0")
    _xload(c, e, 2, "s_x3")
    c.wt(e, "t_A0st_3")
    _xload(c, e, 5, "s_x3")
    xr(0)
    xr(1)
    for q in range(NCH // 2):
        c0, c1 = 2 * q, 2 * q + 1
        gated(c0, 0)
        gated(c1, 0)
        gated(c0, 1)
        gated(c1, 1)
        if dbg2:
            dump(c0, "dbg_a", "C", f"v_lac_{c0}")
            dump(c0, "dbg_cum", "B", f"v_lcum_{c0}")
            dump(c1, "dbg_a", "C", f"v_lac_{c1}")
            dump(c1, "dbg_cum", "B", f"v_lcum_{c1}")
        xt(c0)
        if c1 + 1 < NCH:
            xr(c1 + 1)
        xt(c1)
        if c1 + 2 < NCH:
            xr(c1 + 2)
    if dbg:
        for ch in range(NCH):
            c.wt(e, f"v_h_{ch}")
            c.inc(e.dma_start(t["dbg_h"].ap()[ch * 128:(ch + 1) * 128, :],
                              t["h_all"][:, ch, :]), "s_dbg")
    for ch in range(NCH):
        if ch == 0:
            c.wt(e, "v_A2bc")
        elif ch >= 2:
            c.wt(e, f"v_x1_{ch - 2}")
        ins = e.tensor_mul(_pt(c, "B", ch)[:], t["h_all"][:, ch, :],
                           t["r_b"][:])
        c.inc(ins, "c_p")
        c.mark(f"p_x1m_{ch}", "c_p")
    for ch in range(NCH):
        c.wt(e, f"v_x1_{ch}")
        c.inc(e.dma_start(t["x1sp"].ap()[ch * 128:(ch + 1) * 128, :],
                          t["h_all"][:, ch, :]), "s_sp")
    def spill_ln2(ch):
        c.wt(e, f"v_ln2_{ch}")
        c.inc(e.dma_start(t["ln2sp"].ap()[ch * 128:(ch + 1) * 128, :],
                          t["l2c2"][:, ch % 2, :]), "s_sp")

    for ch in range(NCH):
        if ch == 0:
            c.wt(e, "v_A3bc")
        elif ch >= 2:
            c.wt(e, f"v_ln2_{ch - 2}")
        ins = e.tensor_mul(_pt(c, "B", ch)[:], t["h_all"][:, ch, :],
                           t["r_b"][:])
        c.inc(ins, "c_p")
        c.mark(f"p_l2m_{ch}", "c_p")
        if ch >= 1:
            spill_ln2(ch - 1)
    spill_ln2(7)
    for g in range(_L_A_END):
        _em_wload(c, e, g)


def _wsrc(c, g):
    t = c.t
    st, tau, mg, x = WL[g]
    if st == "s1":
        wname = "w1t_h" if x == 0 else "w2t_h"
        src = c.t[wname].ap()[0:1024, mg * 256:(mg + 1) * 256]
    elif st == "s2":
        src = t["h1t_h"].ap()[x * 1024:(x + 1) * 1024, mg * 256:(mg + 1) * 256]
    elif st == "s3":
        src = t["h2t_h"].ap()[x * 1024:(x + 1) * 1024, mg * 256:(mg + 1) * 256]
    else:
        src = t["w3t_h"].ap()[x * 1024:(x + 1) * 1024, mg * 256:(mg + 1) * 256]
    return src.rearrange("(a p) n -> p a n", p=128)


def _em_wload(c, e, g):
    if g >= NBUF:
        c.wt_raw(e, "c_w", g - (NBUF - 1))
    ins = e.dma_start(c.t["wbuf"][:, g % NBUF, :, :], _wsrc(c, g))
    c.inc(ins, "s_w2" if QID[g] else "s_w")


def _wt_w(c, e, g):
    c.wt_raw(e, "s_w2" if QID[g] else "s_w", 16 * (QORD[g] + 1))


def _rows(c, name, n2048=True):
    t = c.t
    return t[name][0:1, :]


def _em_rows_dve_1(c, e, pref):
    t = c.t
    r0, r1, r2, r3 = (t["T0"][0:1, :], t["T1"][0:1, :],
                      t["T2"][0:1, :], t["T3"][0:1, :])
    c.ss(e.tensor_copy(r0.rearrange("p (a n) -> p a n", a=4),
                       t["ps"][0:1, 0:4, :]), e, "c_v")
    c.ss(e.tensor_copy(r1.rearrange("p (a n) -> p a n", a=4),
                       t["ps"][0:1, 4:8, :]), e, "c_v")
    c.ss(e.tensor_mul(r2, r0, r0), e, "c_v")
    c.ss(e.tensor_sub(r3, r1, r2), e, "c_v")
    c.ss(e.tensor_single_scalar(r2, r3, 0.0, OP.max), e, "c_v")
    ins = e.tensor_scalar_add(r3, r2, EPS_LN)
    c.inc(ins, "c_v")
    c.mark(f"v_{pref}r1", "c_v")


def _em_rows_act(c, e, pref):
    t = c.t
    r2, r3, r4 = t["T2"][0:1, :], t["T3"][0:1, :], t["T4"][0:1, :]
    c.wt(e, f"v_{pref}r1")
    c.ss(e.activation(r2, r3, AF.Ln), e, "c_a")
    ins = e.activation(r4, r2, AF.Exp, scale=-0.5)
    c.inc(ins, "c_a")
    c.mark(f"a_{pref}r", "c_a")


def _em_rows_dve_2(c, e, pref):
    t = c.t
    r0, r4, r5 = t["T0"][0:1, :], t["T4"][0:1, :], t["T5"][0:1, :]
    c.wt(e, f"a_{pref}r")
    ins = e.tensor_mul(r5, r0, r4)
    c.inc(ins, "c_v")
    c.mark(f"v_{pref}r2", "c_v")


def _em_bcast_pe(c, e, pref):
    t = c.t
    c.wt(e, f"v_{pref}r2")
    r4, r5 = t["T4"][0:1, :], t["T5"][0:1, :]
    ins = None
    for n in range(4):
        ins = e.matmul(t["ps"][:, n, :], t["ones_row"][:],
                       r4[0:1, n * 512:(n + 1) * 512], start=True, stop=True)
    for n in range(4):
        ins = e.matmul(t["ps"][:, 4 + n, :], t["ones_row"][:],
                       r5[0:1, n * 512:(n + 1) * 512], start=True, stop=True)
    c.inc(ins, "c_t")
    c.mark(f"t_{pref}bc", "c_t")


def _em_bcast_evac_dve(c, e, pref):
    t = c.t
    c.wt(e, f"t_{pref}bc")
    e.tensor_copy(t["r_b"][:].rearrange("p (a n) -> p a n", a=4),
                  t["ps"][:, 0:4, :])
    ins = e.tensor_copy(t["mr_b"][:].rearrange("p (a n) -> p a n", a=4),
                        t["ps"][:, 4:8, :])
    c.inc(ins, "c_v")
    c.mark(f"v_{pref}bc", "c_v")


def em_A_scalar(c, e):
    t = c.t
    ps = t["ps"]
    for ch in range(NCH):
        if ch in (0, 3, 6):
            och = ch + 1
            if och >= 4:
                c.wt(e, f"t_A0st_{och - 2}")
            _xload(c, e, och, "s_x2")
        _wt_x1r(c, e, ch)
        if ch >= 2:
            c.wt(e, f"t_A0st_{ch - 2}")
        ins = e.activation(_pt(c, "A", ch)[:], t["xc2"][:, ch % 2, :],
                           AF.Square)
        c.inc(ins, "c_a")
        c.mark(f"a_A0sq_{ch}", "c_a")
    _em_rows_act(c, e, "A0")
    def sig(ch, hh, which):
        bk = 4 * (ch % 2) + (0 if which == "gx" else 2)
        dst = _pt(c, "A" if which == "gx" else "C", ch)
        sl = slice(hh * 1024, (hh + 1) * 1024)
        if which == "gx":
            c.wt(e, f"t_g{hh}_{ch}")
        elif hh == 0 and ch >= 2:
            c.wt(e, f"v_h_{ch - 2}")
        col = t["bcol_in" if which == "gx" else "bcol_a"][:, ch:ch + 1]
        ins = e.activation(dst[:, sl].rearrange("p (a n) -> p a n", a=2),
                           ps[:, bk:bk + 2, :], AF.Sigmoid, bias=col)
        c.inc(ins, "c_a")
        mk = {("gx", 0): f"a_gxh0_{ch}", ("ga", 0): f"a_gah0_{ch}",
              ("gx", 1): f"a_gx_{ch}", ("ga", 1): f"a_ga_{ch}"}[(which, hh)]
        c.mark(mk, "c_a")

    for q in range(NCH // 2):
        pair = (2 * q, 2 * q + 1)
        for ch in pair:
            sig(ch, 0, "gx")
        for ch in pair:
            sig(ch, 0, "ga")
        for ch in pair:
            sig(ch, 1, "gx")
        for ch in pair:
            sig(ch, 1, "ga")
        for ch in pair:
            c.wt(e, f"v_lac_{ch}")
            if ch >= 2:
                c.wt(e, f"p_xt_{ch - 2}")
            ins = e.activation(_pt(c, "D", ch)[:], _pt(c, "C", ch)[:],
                               AF.Exp, scale=2.0)
            c.inc(ins, "c_a")
            c.mark(f"a_aa_{ch}", "c_a")
        for ch in pair:
            c.wt(e, f"v_lcum_{ch}")
            if c.dbg == 2:
                c.wt(e, f"d3_dbg_a_{ch}")
            ins = e.activation(_pt(c, "C", ch)[:], _pt(c, "D", ch)[:],
                               AF.Sqrt, scale=-1.0, bias=1.0)
            c.inc(ins, "c_a")
            c.mark(f"a_mult_{ch}", "c_a")
        for ch in pair:
            B, Dd = _pt(c, "B", ch), _pt(c, "D", ch)
            c.wt(e, f"v_m1_{ch}")
            e.activation(B[:, 0:1], Dd[:, 0:1], AF.Copy, scale=0.0)
            ins = e.activation(B[:, 1:2], Dd[:, 0:1], AF.Copy)
            c.inc(ins, "c_a")
            c.mark(f"a_ed_{ch}", "c_a")
            c.wt(e, f"v_d_{ch}")
            ins = e.activation(t["h_all"][:, ch, :], B[:], AF.Exp)
            c.inc(ins, "c_a")
            c.mark(f"a_at_{ch}", "c_a")
    for ch in range(NCH):
        c.wt(e, f"v_h_{ch}")
        if ch >= 2:
            c.wt(e, f"t_A2st_{ch - 2}")
        else:
            c.wt(e, f"v_h_{6 + ch}")
        ins = e.activation(_pt(c, "A", ch)[:], t["h_all"][:, ch, :],
                           AF.Square)
        c.inc(ins, "c_a")
        c.mark(f"a_hsq_{ch}", "c_a")
    _em_rows_act(c, e, "A2")
    for ch in range(NCH):
        c.wt(e, f"v_x1_{ch}")
        if ch >= 2:
            c.wt(e, f"t_lnst_{ch - 2}")
        else:
            c.wt(e, f"t_A2st_{6 + ch}")
        ins = e.activation(_pt(c, "A", ch)[:], t["h_all"][:, ch, :],
                           AF.Square)
        c.inc(ins, "c_a")
        c.mark(f"a_x1sq_{ch}", "c_a")
    _em_rows_act(c, e, "A3")


def em_A_vector(c, e):
    t = c.t
    c.wt(e, "t_A0st_7")
    _em_rows_dve_1(c, e, "A0")
    _em_rows_dve_2(c, e, "A0")
    _em_bcast_evac_dve(c, e, "A0")
    def ln1s(ch):
        A, B = _pt(c, "A", ch), _pt(c, "B", ch)
        c.wt(e, f"p_xr_{ch}")
        if ch >= 2:
            c.wt(e, f"a_at_{ch - 2}")
        ins = e.tensor_sub(B[:], A[:], t["mr_b"][:])
        c.inc(ins, "c_v")
        c.mark(f"v_ln1s_{ch}", "c_v")

    def lac(ch):
        Cc = _pt(c, "C", ch)
        c.wt(e, f"a_ga_{ch}")
        ins = e.tensor_scalar(Cc[:], Cc[:],
                              t["nspcol"][:, ch:ch + 1], C1, OP.mult, OP.min)
        c.inc(ins, "c_v")
        c.mark(f"v_lac_{ch}", "c_v")

    def lcum(ch):
        B, C = _pt(c, "B", ch), _pt(c, "C", ch)
        c.wt(e, f"t_g1_{ch}")
        c.wt(e, f"p_gated_{ch}")
        if c.dbg == 2:
            c.wt(e, f"d3_dbg_ln1_{ch}")
        ins = e.tensor_tensor_scan(B[:], C[:], C[:], 0.0, OP.add, OP.bypass)
        c.inc(ins, "c_v")
        c.mark(f"v_lcum_{ch}", "c_v")

    def m1(ch):
        c.wt(e, f"a_mult_{ch}")
        ins = e.tensor_scalar(_pt(c, "D", ch)[:], _pt(c, "B", ch)[:],
                              0.0, LCL, OP.min, OP.max)
        c.inc(ins, "c_v")
        c.mark(f"v_m1_{ch}", "c_v")

    def d_normed(ch):
        A, B, C, D = (_pt(c, "A", ch), _pt(c, "B", ch),
                      _pt(c, "C", ch), _pt(c, "D", ch))
        if c.dbg == 2:
            c.wt(e, f"d3_dbg_cum_{ch}")
        ins = e.tensor_sub(B[:, 2:T], D[:, 1:T - 1], D[:, 0:T - 2])
        c.inc(ins, "c_v")
        c.mark(f"v_d_{ch}", "c_v")
        c.wt(e, f"a_ed_{ch}")
        ins = e.tensor_mul(D[:], A[:], C[:])
        c.inc(ins, "c_v")
        c.mark(f"v_normed_{ch}", "c_v")

    ln1s(0)
    ln1s(1)
    for q in range(NCH // 2):
        c0, c1 = 2 * q, 2 * q + 1
        lac(c0)
        lac(c1)
        lcum(c0)
        lcum(c1)
        m1(c0)
        m1(c1)
        d_normed(c0)
        d_normed(c1)
        if c1 + 2 < NCH:
            ln1s(c1 + 1)
            ln1s(c1 + 2)
        _em_hscan(c, e, c0)
        _em_hscan(c, e, c1)
    c.wt(e, "t_A2st_7")
    _em_rows_dve_1(c, e, "A2")
    _em_rows_dve_2(c, e, "A2")
    _em_bcast_evac_dve(c, e, "A2")
    for ch in range(NCH):
        p = ch % 2
        c.wt_raw(e, "s_x", 16 * (12 + ch))
        if getattr(c, "dbg", False):
            c.wt_raw(e, "s_dbg", 16 * (ch + 1))
        B, Cc = _pt(c, "B", ch), _pt(c, "C", ch)
        c.wt(e, f"p_x1m_{ch}")
        e.tensor_sub(Cc[:], B[:], t["mr_b"][:])
        ins = e.tensor_add(t["h_all"][:, ch, :], t["xc2"][:, p, :], Cc[:])
        c.inc(ins, "c_v")
        c.mark(f"v_x1_{ch}", "c_v")
    c.wt(e, "t_lnst_7")
    _em_rows_dve_1(c, e, "A3")
    _em_rows_dve_2(c, e, "A3")
    _em_bcast_evac_dve(c, e, "A3")
    for ch in range(NCH):
        if ch >= 2:
            c.wt_raw(e, "s_sp", 16 * (7 + ch))
        B = _pt(c, "B", ch)
        c.wt(e, f"p_l2m_{ch}")
        e.tensor_sub(t["ln2_t"][:, ch, :], B[:, 0:1024], t["mr_b"][:, 0:1024])
        ins = e.tensor_sub(t["l2c2"][:, ch % 2, :], B[:, 1024:T],
                           t["mr_b"][:, 1024:T])
        c.inc(ins, "c_v")
        c.mark(f"v_ln2_{ch}", "c_v")


def _em_hscan(c, e, ch):
    c.wt(e, f"a_at_{ch}")
    c.wt(e, f"p_xt_{ch}")
    h = c.t["h_all"][:, ch, :]
    ins = e.tensor_tensor_scan(h, h, _pt(c, "C", ch)[:], 0.0, OP.mult, OP.add)
    c.inc(ins, "c_v")
    c.mark(f"v_h_{ch}", "c_v")


def em_A_tensor(c, e):
    t = c.t
    ps = t["ps"]

    def stats8(ch, markpref, rhs_x):
        ins = None
        for n in range(4):
            ins = e.matmul(ps[0:1, n, :], t["cst_col"][:],
                           rhs_x[:, n * 512:(n + 1) * 512],
                           start=(ch == 0), stop=(ch == 7),
                           skip_group_check=True)
        sq = _pt(c, "A", ch)
        for n in range(4):
            ins = e.matmul(ps[0:1, 4 + n, :], t["cst_col"][:],
                           sq[:, n * 512:(n + 1) * 512],
                           start=(ch == 0), stop=(ch == 7),
                           skip_group_check=True)
        c.inc(ins, "c_t")
        c.mark(f"{markpref}_{ch}", "c_t")

    for ch in range(NCH):
        _wt_x1r(c, e, ch)
        c.wt(e, f"a_A0sq_{ch}")
        stats8(ch, "t_A0st", t["xc2"][:, ch % 2, :])
    _em_bcast_pe(c, e, "A0")
    def gates_half(ch, hh):
        bk = 4 * (ch % 2)
        B = _pt(c, "B", ch)
        if hh == 0:
            c.wt(e, f"v_ln1s_{ch}")
            if ch >= 2:
                c.wt(e, f"a_ga_{ch - 2}")
            else:
                c.wt(e, "v_A0bc")
        else:
            c.wt(e, f"a_gah0_{ch}")
        ins = None
        for n in range(2):
            nn = 2 * hh + n
            ins = e.matmul(ps[:, bk + n, :], t["wg_in"][:, ch, :],
                           B[:, nn * 512:(nn + 1) * 512],
                           start=True, stop=True)
        for n in range(2):
            nn = 2 * hh + n
            ins = e.matmul(ps[:, bk + 2 + n, :], t["wg_a"][:, ch, :],
                           B[:, nn * 512:(nn + 1) * 512],
                           start=True, stop=True)
        c.inc(ins, "c_t")
        c.mark(f"t_g{hh}_{ch}", "c_t")

    for q in range(NCH // 2):
        for hh in (0, 1):
            gates_half(2 * q, hh)
            gates_half(2 * q + 1, hh)
    for ch in range(NCH):
        c.wt(e, f"a_hsq_{ch}")
        if ch == 0:
            c.wt(e, "a_ga_7")
        stats8(ch, "t_A2st", t["h_all"][:, ch, :])
    _em_bcast_pe(c, e, "A2")
    for ch in range(NCH):
        c.wt(e, f"a_x1sq_{ch}")
        if ch == 0:
            c.wt(e, "v_A2bc")
        stats8(ch, "t_lnst", t["h_all"][:, ch, :])
    _em_bcast_pe(c, e, "A3")



def em_B1_sync(c, e, tau):
    t = c.t
    if tau == 1:
        c.wt_raw(e, "c_w", 32)
        c.wt_raw(e, "s_sp", 16 * 16)
        src = t["ln2sp"].ap().rearrange("(a p) n -> p a n", p=128)
        c.inc(e.dma_start(t["ln2_t"][:], src), "s_l2")
    em_B1_sync_wloads(c, e, tau)


def em_B1_gpsimd(c, e, tau):
    lo = _L_A_END if tau == 0 else _L_B2_0_END
    hi = _L_B1_0_END if tau == 0 else _L_B1_1_END
    for g in range(lo, hi):
        if QID[g] == 0:
            _em_wload(c, e, g)


def em_B1_sync_wloads(c, e, tau):
    lo = _L_A_END if tau == 0 else _L_B2_0_END
    hi = _L_B1_0_END if tau == 0 else _L_B1_1_END
    for g in range(lo, hi):
        if QID[g] == 1:
            _em_wload(c, e, g)


def em_B1_tensor(c, e, tau):
    t = c.t
    ps = t["ps"]
    if tau == 0:
        c.wt(e, "v_ln2_7")
    else:
        c.wt_raw(e, "s_l2", 16)
        c.wt(e, "v_bc4_0")
        c.wt(e, "ev_s4_0_7")
    for mg in range(16):
        for part in (0, 1):
            g = GIDX[("s1", tau, mg, part)]
            if mg >= 1:
                c.wt(e, [f"ev_s1a_{tau}_{mg - 1}", f"ev_s1b_{tau}_{mg - 1}"][part])
            _wt_w(c, e, g)
            ins = None
            for kk in range(8):
                for mi in range(2):
                    lhsT = t["wbuf"][:, g % NBUF, kk, mi * 128:(mi + 1) * 128]
                    for n2 in range(2):
                        ins = e.matmul(ps[:, 4 * part + 2 * mi + n2, :], lhsT,
                                       t["ln2_t"][:, kk, n2 * 512:(n2 + 1) * 512],
                                       start=(kk == 0), stop=(kk == 7))
            c.inc(ins, "c_w")
            c.mark(f"pe_s1{'ab'[part]}_{tau}_{mg}", "c_w")
    for st, rhs in (("s2", "h1_t"), ("s3", "h2_t")):
        prev_ev = f"ev_s1b_{tau}_15" if st == "s2" else f"ev_s2_{tau}_15"
        c.wt(e, prev_ev)
        for mg in range(16):
            se = mg % 2
            if mg >= 2:
                c.wt(e, f"ev_{st}_{tau}_{mg - 2}")
            for kg in range(4):
                g = GIDX[(st, tau, mg, kg)]
                _wt_w(c, e, g)
                ins = None
                for kk in range(8):
                    k = kg * 8 + kk
                    for mi in range(2):
                        lhsT = t["wbuf"][:, g % NBUF, kk, mi * 128:(mi + 1) * 128]
                        for n2 in range(2):
                            ins = e.matmul(
                                ps[:, 4 * se + 2 * mi + n2, :], lhsT,
                                t[rhs][:, k, n2 * 512:(n2 + 1) * 512],
                                start=(k == 0), stop=(k == 31))
                c.inc(ins, "c_w")
            c.mark(f"pe_{st}_{tau}_{mg}", "c_w")


def em_B1_scalar(c, e, tau):
    t = c.t
    ps = t["ps"]
    if tau == 0:
        c.wt_raw(e, "s_sp", 16 * 16)
    else:
        c.wt(e, "v_C2_0")
    for mg in range(16):
        c.wt(e, f"pe_s1a_{tau}_{mg}")
        if mg >= 1:
            c.wt(e, f"ev_s1b_{tau}_{mg - 1}")
        ins = None
        for mi in range(2):
            m = 2 * mg + mi
            for n2 in range(2):
                ins = e.activation(t["t1_buf"][:, mi, n2 * 512:(n2 + 1) * 512],
                                   ps[:, 2 * mi + n2, :], AF.Silu,
                                   scale=t["s1c"][:, m:m + 1])
        c.inc(ins, "c_a")
        c.mark(f"ev_s1a_{tau}_{mg}", "c_a")
    for st, dst, scol in (("s2", "h2_t", "hs1c"), ("s3", "h1_t", "hs2c")):
        for mg in range(16):
            se = mg % 2
            c.wt(e, f"pe_{st}_{tau}_{mg}")
            ins = None
            for mi in range(2):
                m = 2 * mg + mi
                for n2 in range(2):
                    ins = e.activation(t[dst][:, m, n2 * 512:(n2 + 1) * 512],
                                       ps[:, 4 * se + 2 * mi + n2, :], AF.Gelu,
                                       scale=t[scol][:, m:m + 1])
            c.inc(ins, "c_a")
            c.mark(f"ev_{st}_{tau}_{mg}", "c_a")


def em_B1_vector(c, e, tau):
    t = c.t
    ps = t["ps"]
    if tau == 0:
        c.wt_raw(e, "s_sp", 16 * 16)
    else:
        c.wt(e, "pe_s4_0_7")
    for mg in range(16):
        c.wt(e, f"pe_s1b_{tau}_{mg}")
        c.wt(e, f"ev_s1a_{tau}_{mg}")
        ins = None
        for mi in range(2):
            m = 2 * mg + mi
            for n2 in range(2):
                ins = e.scalar_tensor_tensor(
                    t["h1_t"][:, m, n2 * 512:(n2 + 1) * 512],
                    ps[:, 4 + 2 * mi + n2, :], t["s2c"][:, m:m + 1],
                    t["t1_buf"][:, mi, n2 * 512:(n2 + 1) * 512],
                    OP.mult, OP.mult)
        c.inc(ins, "c_v")
        c.mark(f"ev_s1b_{tau}_{mg}", "c_v")



def em_B2_sync(c, e, tau):
    t = c.t
    em_B2_sync_wloads(c, e, tau)
    c.wt_raw(e, "s_sp", 16 * 8)
    if B2_LEVEL < 4:
        return
    for ch in range(NCH):
        if ch >= 2 and B2_LEVEL >= 6:
            c.wt(e, f"v_yc_{tau}_{ch - 2}")
            c.inc(e.dma_start(
                t["y_h"].ap()[(ch - 2) * 128:(ch - 1) * 128,
                              tau * 1024:(tau + 1) * 1024],
                t["y_t"][:, ch % 2, :]), "s_y")
        if tau == 1 and ch < 2:
            c.wt(e, f"v_yc_0_{6 + ch}")
        src = t["x1sp"].ap()[ch * 128:(ch + 1) * 128,
                             tau * 1024:(tau + 1) * 1024]
        c.inc(e.dma_start(t["x1r"][:, ch % 2, :], src), "s_x1r")
    if B2_LEVEL >= 6:
        for ch in (6, 7):
            c.wt(e, f"v_yc_{tau}_{ch}")
            c.inc(e.dma_start(
                t["y_h"].ap()[ch * 128:(ch + 1) * 128,
                              tau * 1024:(tau + 1) * 1024],
                t["y_t"][:, ch % 2, :]), "s_y")
        if tau == 1:
            c.wt_raw(e, "s_y", 16 * 16)


def em_B2_gpsimd(c, e, tau):
    lo = _L_B1_0_END if tau == 0 else _L_B1_1_END
    hi = _L_B2_0_END if tau == 0 else _L_B2_1_END
    for g in range(lo, hi):
        if QID[g] == 0:
            _em_wload(c, e, g)


def em_B2_sync_wloads(c, e, tau):
    lo = _L_B1_0_END if tau == 0 else _L_B1_1_END
    hi = _L_B2_0_END if tau == 0 else _L_B2_1_END
    for g in range(lo, hi):
        if QID[g] == 1:
            _em_wload(c, e, g)


def em_B2_tensor(c, e, tau):
    t = c.t
    ps = t["ps"]
    c.wt(e, f"ev_s3_{tau}_15")

    def stats(m):
        c.wt(e, f"ev_s4sq_{tau}_{m}")
        ins = None
        for n2 in range(2):
            ins = e.matmul(ps[0:1, 4 + n2, :], t["cst_col"][:],
                           t["ffn_t"][:, m, n2 * 512:(n2 + 1) * 512],
                           start=(m == 0), stop=(m == 7),
                           skip_group_check=True)
            ins = e.matmul(ps[0:1, 6 + n2, :], t["cst_col"][:],
                           t["sqf"][:, m % 2, n2 * 512:(n2 + 1) * 512],
                           start=(m == 0), stop=(m == 7),
                           skip_group_check=True)
        c.inc(ins, "c_t")
        c.mark(f"pe_s4stats_{tau}_{m}", "c_t")

    for m in range(8):
        if m >= 2:
            c.wt(e, f"ev_s4_{tau}_{m - 2}")
        for kg in range(4):
            g = GIDX[("s4", tau, m // 2, kg)]
            _wt_w(c, e, g)
            ins = None
            for kk in range(8):
                k = kg * 8 + kk
                lhsT = t["wbuf"][:, g % NBUF, kk,
                                 (m % 2) * 128:(m % 2 + 1) * 128]
                for n2 in range(2):
                    ins = e.matmul(ps[:, 2 * (m % 2) + n2, :], lhsT,
                                   t["h1_t"][:, k, n2 * 512:(n2 + 1) * 512],
                                   start=(k == 0), stop=(k == 31))
            if m % 2 == 1:
                c.inc(ins, "c_w")
            elif kg == 3:
                c.inc(ins, "c_t")
        c.mark(f"pe_s4_{tau}_{m}", "c_w" if m % 2 == 1 else "c_t")
        if m >= 1 and B2_LEVEL >= 2:
            stats(m - 1)
    if B2_LEVEL >= 2:
        stats(7)
    if B2_LEVEL < 3:
        return
    c.wt(e, f"v_C2_{tau}")
    r4 = t["y_t"][0:1, 0, :]
    r5 = t["y_t"][0:1, 1, :]
    ins = None
    for n2 in range(2):
        ins = e.matmul(ps[:, 4 + n2, :], t["ones_row"][:],
                       r4[0:1, n2 * 512:(n2 + 1) * 512], start=True, stop=True)
    for n2 in range(2):
        ins = e.matmul(ps[:, 6 + n2, :], t["ones_row"][:],
                       r5[0:1, n2 * 512:(n2 + 1) * 512], start=True, stop=True)
    c.inc(ins, "c_t")
    c.mark(f"t_bc4_{tau}", "c_t")


def em_B2_scalar(c, e, tau):
    t = c.t
    ps = t["ps"]
    if tau == 1:
        c.wt(e, "v_yc_0_7")
    for m in range(8):
        c.wt(e, f"pe_s4_{tau}_{m}")
        if m >= 2 and B2_LEVEL >= 2:
            c.wt(e, f"pe_s4stats_{tau}_{m - 2}")
        ins = None
        for n2 in range(2):
            ins = e.activation(t["ffn_t"][:, m, n2 * 512:(n2 + 1) * 512],
                               ps[:, 2 * (m % 2) + n2, :], AF.Copy,
                               scale=t["s3c"][:, m:m + 1])
        c.inc(ins, "c_a")
        c.mark(f"ev_s4_{tau}_{m}", "c_a")
        for n2 in range(2):
            ins = e.activation(t["sqf"][:, m % 2, n2 * 512:(n2 + 1) * 512],
                               t["ffn_t"][:, m, n2 * 512:(n2 + 1) * 512],
                               AF.Square)
        c.inc(ins, "c_a")
        c.mark(f"ev_s4sq_{tau}_{m}", "c_a")
    if B2_LEVEL < 3:
        return
    c.wt(e, f"v_C1_{tau}")
    tmp2 = t["sqf"][0:1, 1, :]
    r4 = t["y_t"][0:1, 0, :]
    s4 = t["t1_buf"][0:1, 1, :]
    c.ss(e.activation(tmp2, s4, AF.Ln), e, "c_a")
    ins = e.activation(r4, tmp2, AF.Exp, scale=-0.5)
    c.inc(ins, "c_a")
    c.mark(f"a_C_{tau}", "c_a")


def em_B2_vector(c, e, tau):
    t = c.t
    ps = t["ps"]
    if B2_LEVEL < 3:
        return
    m4 = t["t1_buf"][0:1, 0, :]
    s4 = t["t1_buf"][0:1, 1, :]
    tmp = t["sqf"][0:1, 0, :]
    r5 = t["y_t"][0:1, 1, :]
    c.wt(e, f"pe_s4stats_{tau}_7")
    c.ss(e.tensor_copy(m4.rearrange("p (a n) -> p a n", a=2),
                       ps[0:1, 4:6, :]), e, "c_v")
    c.ss(e.tensor_copy(s4.rearrange("p (a n) -> p a n", a=2),
                       ps[0:1, 6:8, :]), e, "c_v")
    c.ss(e.tensor_mul(tmp, m4, m4), e, "c_v")
    c.ss(e.tensor_sub(s4, s4, tmp), e, "c_v")
    c.ss(e.tensor_single_scalar(tmp, s4, 0.0, OP.max), e, "c_v")
    ins = e.tensor_scalar_add(s4, tmp, EPS_LN)
    c.inc(ins, "c_v")
    c.mark(f"v_C1_{tau}", "c_v")
    c.wt(e, f"a_C_{tau}")
    r4 = t["y_t"][0:1, 0, :]
    ins = e.tensor_mul(r5, m4, r4)
    c.inc(ins, "c_v")
    c.mark(f"v_C2_{tau}", "c_v")
    c.wt(e, f"t_bc4_{tau}")
    e.tensor_copy(t["r3_b"][:].rearrange("p (a n) -> p a n", a=2),
                  ps[:, 4:6, :])
    ins = e.tensor_copy(t["mr3_b"][:].rearrange("p (a n) -> p a n", a=2),
                        ps[:, 6:8, :])
    c.inc(ins, "c_v")
    c.mark(f"v_bc4_{tau}", "c_v")
    if B2_LEVEL < 4:
        return
    for ch in range(NCH):
        c.wt_raw(e, "s_x1r", 16 * (8 * tau + ch + 1))
        if B2_LEVEL >= 6:
            c.wt_raw(e, "s_y", 16 * (8 * tau + ch - 1))
        e.tensor_mul(t["sqf"][:, 0, :], t["ffn_t"][:, ch, :], t["r3_b"][:])
        e.tensor_sub(t["sqf"][:, 1, :], t["sqf"][:, 0, :], t["mr3_b"][:])
        ins = e.tensor_add(t["y_t"][:, ch % 2, :], t["x1r"][:, ch % 2, :],
                           t["sqf"][:, 1, :])
        c.inc(ins, "c_v")
        c.mark(f"v_yc_{tau}_{ch}", "c_v")



def build_nc(dbg=False, n_blocks=6):
    nc = bass.Bass("TRN2", target_bir_lowering=False)
    tn = {}
    tn["x_h"] = nc.dram_tensor("x", [D, T], F32, kind="ExternalInput")
    tn["wgin_h"] = nc.dram_tensor("wgin", [8, 128, 128], F32, kind="ExternalInput")
    tn["wga_h"] = nc.dram_tensor("wga", [8, 128, 128], F32, kind="ExternalInput")
    tn["bcin_h"] = nc.dram_tensor("bcin", [128, 8], F32, kind="ExternalInput")
    tn["bca_h"] = nc.dram_tensor("bca", [128, 8], F32, kind="ExternalInput")
    tn["acol_h"] = nc.dram_tensor("acol", [128, 8], F32, kind="ExternalInput")
    tn["w1t_h"] = nc.dram_tensor("w1t", [1024, 4096], F32, kind="ExternalInput")
    tn["w2t_h"] = nc.dram_tensor("w2t", [1024, 4096], F32, kind="ExternalInput")
    tn["h1t_h"] = nc.dram_tensor("h1t", [4096, 4096], F32, kind="ExternalInput")
    tn["h2t_h"] = nc.dram_tensor("h2t", [4096, 4096], F32, kind="ExternalInput")
    tn["w3t_h"] = nc.dram_tensor("w3t", [4096, 1024], F32, kind="ExternalInput")
    tn["s1c_h"] = nc.dram_tensor("s1c", [128, 32], F32, kind="ExternalInput")
    tn["s2c_h"] = nc.dram_tensor("s2c", [128, 32], F32, kind="ExternalInput")
    tn["hs1c_h"] = nc.dram_tensor("hs1c", [128, 32], F32, kind="ExternalInput")
    tn["hs2c_h"] = nc.dram_tensor("hs2c", [128, 32], F32, kind="ExternalInput")
    tn["s3c_h"] = nc.dram_tensor("s3c", [128, 8], F32, kind="ExternalInput")
    tn["x1sp"] = nc.dram_tensor("x1sp", [D, T], F32,
                                kind="ExternalOutput" if dbg else "Internal")
    tn["ln2sp"] = nc.dram_tensor("ln2sp", [D, 1024], BF16,
                                 kind="ExternalOutput" if dbg else "Internal")
    tn["y_h"] = nc.dram_tensor("y", [D, T], F32, kind="ExternalOutput")
    if dbg:
        tn["dbg_h"] = nc.dram_tensor("dbg_h", [D, T], F32, kind="ExternalOutput")
    if dbg == 2:
        for nm in ("dbg_ln1", "dbg_a", "dbg_nrm", "dbg_cum", "dbg_gx", "dbg_mu"):
            tn[nm] = nc.dram_tensor(nm, [D, T], F32, kind="ExternalOutput")

    tn["wg_in"] = nc.alloc_sbuf_tensor("wg_in", [128, 8, 128], F32)
    tn["wg_a"] = nc.alloc_sbuf_tensor("wg_a", [128, 8, 128], F32)
    tn["bcol_in"] = nc.alloc_sbuf_tensor("bcol_in", [128, 8], F32)
    tn["bcol_a"] = nc.alloc_sbuf_tensor("bcol_a", [128, 8], F32)
    tn["acol"] = nc.alloc_sbuf_tensor("acol_sb", [128, 8], F32)
    tn["nspcol"] = nc.alloc_sbuf_tensor("nspcol", [128, 8], F32)
    tn["tiny"] = nc.alloc_sbuf_tensor("tiny", [128, 32], F32)
    tn["s1c"] = nc.alloc_sbuf_tensor("s1c_sb", [128, 32], F32)
    tn["s2c"] = nc.alloc_sbuf_tensor("s2c_sb", [128, 32], F32)
    tn["hs1c"] = nc.alloc_sbuf_tensor("hs1c_sb", [128, 32], F32)
    tn["hs2c"] = nc.alloc_sbuf_tensor("hs2c_sb", [128, 32], F32)
    tn["s3c"] = nc.alloc_sbuf_tensor("s3c_sb", [128, 8], F32)
    tn["ones_row"] = nc.alloc_sbuf_tensor("ones_row", [1, 128], F32)
    tn["cst_col"] = nc.alloc_sbuf_tensor("cst_col", [128, 1], F32)
    tn["wbuf"] = nc.alloc_sbuf_tensor("wbuf", [128, NBUF, 8, 256], BF16)
    tn["ln2_t"] = nc.alloc_sbuf_tensor("ln2_t", [128, 8, 1024], BF16)
    tn["ps"] = nc.alloc_psum_tensor("ps", [128, 8, 512], F32)

    semnames = ["s_c", "s_x", "s_x2", "s_x3", "s_w", "s_w2", "s_sp", "s_l2", "s_x1r", "s_y", "s_dbg",
                "s_d3v", "s_d3a", "c_v", "c_a", "c_p", "c_t", "c_w", "bar"]
    sems = {n: nc.alloc_semaphore(n) for n in semnames}

    ctx = Ctx()
    ctx.sems = sems
    ctx.dbg = dbg

    def emit_all(c):
        blocks = []
        blocks.append({
            "sync": lambda e: em_S0_sync(c, e),
            "vector": lambda e: em_S0_vector(c, e),
            "scalar": lambda e: em_S0_scalar(c, e),
            "gpsimd": lambda e: None,
            "tensor": lambda e: None,
        })
        blocks.append({
            "sync": lambda e: em_A_sync(c, e),
            "gpsimd": lambda e: em_A_pool(c, e, dbg),
            "scalar": lambda e: em_A_scalar(c, e),
            "vector": lambda e: em_A_vector(c, e),
            "tensor": lambda e: em_A_tensor(c, e),
        })
        for tau in (0, 1):
            blocks.append({
                "sync": lambda e, tau=tau: em_B1_sync(c, e, tau),
                "gpsimd": lambda e, tau=tau: em_B1_gpsimd(c, e, tau),
                "tensor": lambda e, tau=tau: em_B1_tensor(c, e, tau),
                "scalar": lambda e, tau=tau: em_B1_scalar(c, e, tau),
                "vector": lambda e, tau=tau: em_B1_vector(c, e, tau),
            })
            blocks.append({
                "sync": lambda e, tau=tau: em_B2_sync(c, e, tau),
                "gpsimd": lambda e, tau=tau: em_B2_gpsimd(c, e, tau),
                "tensor": lambda e, tau=tau: em_B2_tensor(c, e, tau),
                "scalar": lambda e, tau=tau: em_B2_scalar(c, e, tau),
                "vector": lambda e, tau=tau: em_B2_vector(c, e, tau),
            })
        return blocks

    mock = _MockEng()
    mockt = _MockTensor()
    ctx.t = {k: mockt for k in list(tn.keys()) + [
        "xc2", "pA0", "pB0", "pC0", "pD0", "pA1", "pB1", "pC1", "pD1",
        "T0", "T1", "T2", "T3", "T4", "T5",
        "r_b", "mr_b", "h_all", "l2c2", "ln2_t", "h1_t", "h2_t", "t1_buf",
        "ffn_t", "sqf", "x1r", "y_t", "r3_b", "mr3_b"]}
    for blk in emit_all(ctx)[:n_blocks]:
        for en in ("sync", "gpsimd", "scalar", "vector", "tensor"):
            blk[en](mock)
    missing = ctx.refs - set(ctx.M.keys())
    assert not missing, f"unresolved marks: {sorted(missing)}"

    ctx.counting = False
    ctx.t = tn
    ctx._ssn = 0
    blocks = emit_all(ctx)
    engs = ("sync", "gpsimd", "scalar", "vector", "tensor")

    def emit_block(bi, blk):
        with nc.Block() as block:
            for en in engs:
                def body(eng, en=en, blk=blk, bi=bi):
                    if 0 < bi < 2:
                        eng.wait_ge(sems["bar"], 5 * bi)
                    blk[en](eng)
                    if bi < 1:
                        eng.sem_inc(sems["bar"], 1)
                getattr(block, en)(body)

    emit_block(0, blocks[0])
    if n_blocks >= 2:
        sa = _ScopeA(nc, tn)
        sa.__enter__()
        emit_block(1, blocks[1])
        sa.__exit__(None, None, None)
    if n_blocks >= 3:
        _ScopeBOuter(nc, tn).__enter__()
        for tau in (0, 1):
            if n_blocks >= 2 + 2 * tau + 1:
                b1 = _ScopeB1(nc, tn)
                b1.__enter__()
                emit_block(2 + 2 * tau, blocks[2 + 2 * tau])
                b1.__exit__(None, None, None)
            if n_blocks >= 3 + 2 * tau + 1:
                b2 = _ScopeB2(nc, tn)
                b2.__enter__()
                emit_block(3 + 2 * tau, blocks[3 + 2 * tau])
                b2.__exit__(None, None, None)
    return nc


class _ScopeA:
    def __init__(self, nc, tn):
        self.nc, self.tn = nc, tn
        self.keys = []

    def __enter__(self):
        nc, tn = self.nc, self.tn
        self.cms = []
        def add(key, shape, dt):
            cm = nc.sbuf_tensor(key, shape, dt)
            tn[key] = cm.__enter__()
            self.cms.append(cm)
            self.keys.append(key)
        add("xc2", [128, 2, T], F32)
        for p in (0, 1):
            for nm in "ABCD":
                add(f"p{nm}{p}", [128, T], F32)
        add("r_b", [128, T], F32)
        add("mr_b", [128, T], F32)
        add("h_all", [128, 8, T], F32)
        add("l2c2", [128, 2, 1024], BF16)
        for alias, real in (("T0", "pB0"), ("T1", "pC0"), ("T2", "pD0"),
                            ("T3", "pB1"), ("T4", "pC1"), ("T5", "pD1")):
            tn[alias] = tn[real]
        return self

    def __exit__(self, *a):
        for cm in reversed(self.cms):
            cm.__exit__(None, None, None)
        return False


class _ScopeBOuter:

    def __init__(self, nc, tn):
        self.nc, self.tn = nc, tn

    def __enter__(self):
        nc, tn = self.nc, self.tn
        tn["h1_t"] = nc.alloc_sbuf_tensor("h1_t", [128, 32, 1024], BF16)
        tn["t1_buf"] = nc.alloc_sbuf_tensor("t1_buf", [128, 2, 1024], F32)
        return self

    def __exit__(self, *a):
        return False


class _ScopeB1:
    _n = [0]

    def __init__(self, nc, tn):
        self.nc, self.tn = nc, tn

    def __enter__(self):
        self._n[0] += 1
        self.cm = self.nc.sbuf_tensor(f"h2_t_{self._n[0]}", [128, 32, 1024], BF16)
        self.tn["h2_t"] = self.cm.__enter__()
        return self

    def __exit__(self, *a):
        self.cm.__exit__(None, None, None)
        return False


class _ScopeB2:
    _n = [0]

    def __init__(self, nc, tn):
        self.nc, self.tn = nc, tn

    def __enter__(self):
        nc, tn = self.nc, self.tn
        self._n[0] += 1
        self.cms = []
        for key, shape, dt in [("ffn_t", [128, 8, 1024], F32),
                               ("sqf", [128, 2, 1024], F32),
                               ("x1r", [128, 2, 1024], F32),
                               ("y_t", [128, 2, 1024], F32),
                               ("r3_b", [128, 1024], F32),
                               ("mr3_b", [128, 1024], F32)]:
            cm = nc.sbuf_tensor(f"{key}_{self._n[0]}", shape, dt)
            tn[key] = cm.__enter__()
            self.cms.append(cm)
        return self

    def __exit__(self, *a):
        for cm in reversed(self.cms):
            cm.__exit__(None, None, None)
        return False



_NC_CACHE = {}


def _get_nc(dbg=False):
    if dbg not in _NC_CACHE:
        _NC_CACHE[dbg] = build_nc(dbg)
    return _NC_CACHE[dbg]


def _layout_inputs(inputs):
    f = np.ascontiguousarray
    x = np.asarray(inputs["L0_attn_x"], dtype=np.float32)
    common = {
        "wgin": f(np.asarray(inputs["L0_attn_w_in"], np.float32)),
        "wga": f(np.asarray(inputs["L0_attn_w_a"], np.float32)),
        "bcin": f(np.asarray(inputs["L0_attn_b_in"], np.float32).T),
        "bca": f(np.asarray(inputs["L0_attn_b_a"], np.float32).T),
        "acol": f(np.asarray(inputs["L0_attn_a_param"], np.float32)
                  .reshape(8, 128).T),
        "w1t": f(np.asarray(inputs["L0_ffn_w1_w"], np.float32).T),
        "w2t": f(np.asarray(inputs["L0_ffn_w2_w"], np.float32).T),
        "h1t": f(np.asarray(inputs["L0_ffn_h1_w"], np.float32).T),
        "h2t": f(np.asarray(inputs["L0_ffn_h2_w"], np.float32).T),
        "w3t": f(np.asarray(inputs["L0_ffn_w3_w"], np.float32).T),
        "s1c": f(np.asarray(inputs["L0_ffn_w1_s"], np.float32)
                 .reshape(32, 128).T),
        "s2c": f(np.asarray(inputs["L0_ffn_w2_s"], np.float32)
                 .reshape(32, 128).T),
        "hs1c": f(np.asarray(inputs["L0_ffn_h1_s"], np.float32)
                  .reshape(32, 128).T),
        "hs2c": f(np.asarray(inputs["L0_ffn_h2_s"], np.float32)
                  .reshape(32, 128).T),
        "s3c": f(np.asarray(inputs["L0_ffn_w3_s"], np.float32)
                 .reshape(8, 128).T),
    }
    in_maps = []
    for b in range(8):
        m = dict(common)
        m["x"] = f(x[b].T)
        in_maps.append(m)
    return in_maps


def kernel(**inputs):
    nc = _get_nc(dbg=False)
    in_maps = _layout_inputs(inputs)
    res = run_bass_kernel_spmd(nc, in_maps, core_ids=list(range(8)))
    out = np.stack([np.asarray(r["y"]).T for r in res.results], axis=0)
    return np.ascontiguousarray(out, dtype=np.float32)


# version 14
# speedup vs baseline: 1.0106x; 1.0106x over previous
import numpy as np
from contextlib import ExitStack

import concourse.bass as bass
import concourse.mybir as mybir
from concourse.bass_utils import run_bass_kernel_spmd

F32 = mybir.dt.float32
BF16 = mybir.dt.bfloat16
AF = mybir.ActivationFunctionType
OP = mybir.AluOpType

D = 1024
T = 2048
NCH = 8
NBUF = 4
EPS_LN = 1e-5
MIN_DECAY = 1e-3
SCAN_EPS = 1e-7
CLAMP_A = 1.0 - 1e-5


def _weight_loads():
    WL = []
    for tau in (0, 1):
        for mg in range(16):
            for part in (0, 1):
                WL.append(("s1", tau, mg, part))
        for st in ("s2", "s3"):
            for mg in range(16):
                for kg in range(4):
                    WL.append((st, tau, mg, kg))
        for mg in range(4):
            for kg in range(4):
                WL.append(("s4", tau, mg, kg))
    G = {k: g for g, k in enumerate(WL)}
    return WL, G


WL, GIDX = _weight_loads()
NLOADS = len(WL)
QID = [0 for g in range(NLOADS)]
QORD = []
_cnt = [0, 0]
for _g in range(NLOADS):
    QORD.append(_cnt[QID[_g]])
    _cnt[QID[_g]] += 1
B2_LEVEL = 6
_L_A_END = NBUF
_L_B1_0_END = GIDX[("s4", 0, 0, 0)]
_L_B2_0_END = GIDX[("s1", 1, 0, 0)]
_L_B1_1_END = GIDX[("s4", 1, 0, 0)]
_L_B2_1_END = NLOADS


class _MockEng:
    def __getattr__(self, name):
        return lambda *a, **k: None


class _MockTensor:
    def __getitem__(self, *a):
        return self

    def rearrange(self, *a, **k):
        return self

    def ap(self):
        return self


class Ctx:

    def __init__(self):
        self.counting = True
        self.cnt = {}
        self.M = {}
        self.refs = set()
        self.sems = None
        self.t = None
        self._ssn = 0

    def inc(self, ins, sem, n=1):
        amt = 16 * n if sem.startswith("s_") else n
        if self.counting:
            self.cnt[sem] = self.cnt.get(sem, 0) + amt
        else:
            ins.then_inc(self.sems[sem], amt)

    def mark(self, name, sem):
        if self.counting:
            assert name not in self.M, f"duplicate mark {name}"
            self.M[name] = (sem, self.cnt.get(sem, 0))

    def wt(self, eng, name):
        if self.counting:
            self.refs.add(name)
            return
        semname, v = self.M[name]
        if v > 0:
            eng.wait_ge(self.sems[semname], v)

    def wt_raw(self, eng, sem, v):
        if not self.counting and v > 0:
            eng.wait_ge(self.sems[sem], v)

    def ss(self, ins, eng, sem):
        name = f"_ss_{sem}_{self._ssn}"
        self._ssn += 1
        self.inc(ins, sem)
        self.mark(name, sem)
        self.wt(eng, name)



N_CONST = 10
C1 = float(np.log1p(-1e-5))
LCL = float(np.log(1e-7))


def em_S0_sync(c, e):
    t = c.t
    for name, dst in [("wgin_h", "wg_in"), ("wga_h", "wg_a"),
                      ("bcin_h", "bcol_in"), ("bca_h", "bcol_a"),
                      ("acol_h", "acol"), ("s1c_h", "s1c"), ("s2c_h", "s2c"),
                      ("hs1c_h", "hs1c"), ("hs2c_h", "hs2c"), ("s3c_h", "s3c")]:
        if name in ("wgin_h", "wga_h"):
            src = t[name].ap().rearrange("h i j -> i h j")
        else:
            src = t[name].ap()
        c.inc(e.dma_start(t[dst][:], src), "s_c")


def em_S0_vector(c, e):
    t = c.t
    e.memset(t["ones_row"][:], 1.0)
    e.memset(t["cst_col"][:], 1.0 / D)
    c.wt_raw(e, "s_c", 16 * N_CONST)
    tr = t["tiny"][:, 0:8]
    tb = t["tiny"][:, 8:16]
    tl = t["tiny"][:, 24:32]
    ins = e.tensor_single_scalar(tr, t["acol"][:], 0.0, OP.max)
    c.inc(ins, "c_v")
    c.mark("v_pro1", "c_v")
    c.wt(e, "a_pro1")
    c.ss(e.scalar_tensor_tensor(tb, tl, MIN_DECAY, tr, OP.add, OP.add), e, "c_v")
    ins = e.tensor_single_scalar(t["nspcol"][:], tb, -1.0, OP.mult)
    c.inc(ins, "c_v")
    c.mark("v_pro2", "c_v")


def em_S0_scalar(c, e):
    t = c.t
    tb = t["tiny"][:, 8:16]
    te = t["tiny"][:, 16:24]
    tl = t["tiny"][:, 24:32]
    c.wt_raw(e, "s_c", 16 * N_CONST)
    c.ss(e.activation(tb, t["acol"][:], AF.Abs), e, "c_a")
    c.ss(e.activation(te, tb, AF.Exp, scale=-1.0), e, "c_a")
    ins = e.activation(tl, te, AF.Ln, bias=1.0)
    c.inc(ins, "c_a")
    c.mark("a_pro1", "c_a")


def em_S0_gpsimd(c, e):
    pass


def em_S0_tensor(c, e):
    pass




def _pt(c, name, ch):
    return c.t[f"p{name}{ch % 2}"]


def _xload(c, e, ch, sem):
    src = c.t["x_h"].ap()[ch * 128:(ch + 1) * 128, :]
    c.inc(e.dma_start(c.t["xc2"][:, ch % 2, :], src), sem)


XQ = {0: ("s_x", 1), 3: ("s_x", 2), 6: ("s_x", 3),
      1: ("s_x2", 1), 4: ("s_x2", 2), 7: ("s_x2", 3),
      2: ("s_x3", 1), 5: ("s_x3", 2)}


def _wt_x1r(c, e, ch):
    sem, o = XQ[ch]
    c.wt_raw(e, sem, 16 * o)


def em_A_sync(c, e):
    for ch in (0, 3, 6):
        if ch >= 2:
            c.wt(e, f"t_A0st_{ch - 2}")
        _xload(c, e, ch, "s_x")
    for ch in range(8):
        c.wt(e, [f"t_A0st_{6 + ch}", f"p_xr_{ch - 2}"][ch >= 2])
        _xload(c, e, ch, "s_x")
    for ch in range(8):
        c.wt(e, [f"p_xr_{6 + ch}", f"v_x1_{ch - 2}"][ch >= 2])
        _xload(c, e, ch, "s_x")


def em_A_pool(c, e, dbg=False):
    t = c.t
    dbg2 = dbg == 2

    def dump(ch, name, tile, markname):
        if not dbg2:
            return
        c.wt(e, markname)
        c.inc(e.dma_start(t[name].ap()[ch * 128:(ch + 1) * 128, :],
                          _pt(c, tile, ch)[:]), "s_d3v")
        c.mark(f"d3_{name}_{ch}", "s_d3v")

    def gated(ch, hh):
        sl = slice(hh * 1024, (hh + 1) * 1024)
        if hh == 0:
            dump(ch, "dbg_ln1", "B", f"v_ln1s_{ch}")
            c.wt(e, f"a_gxh0_{ch}")
            c.wt(e, f"v_ln1s_{ch}")
        else:
            c.wt(e, f"a_gx_{ch}")
        A = _pt(c, "A", ch)
        ins = e.tensor_mul(A[:, sl], A[:, sl], _pt(c, "B", ch)[:, sl])
        c.inc(ins, "c_p")
        c.mark([f"p_g0_{ch}", f"p_gated_{ch}"][hh], "c_p")
        if hh == 1:
            dump(ch, "dbg_gx", "A", f"p_gated_{ch}")

    def xt(ch):
        dump(ch, "dbg_mu", "C", f"a_mult_{ch}")
        dump(ch, "dbg_nrm", "D", f"v_normed_{ch}")
        c.wt(e, f"v_normed_{ch}")
        ins = e.tensor_scalar(_pt(c, "C", ch)[:], _pt(c, "D", ch)[:],
                              SCAN_EPS, 1.0, OP.add, OP.mult)
        c.inc(ins, "c_p")
        c.mark(f"p_xt_{ch}", "c_p")

    def xr(ch):
        c.wt_raw(e, "s_x", 16 * (4 + ch))
        if ch < 2:
            c.wt(e, "v_A0bc")
        else:
            c.wt(e, f"v_normed_{ch - 2}")
        ins = e.tensor_mul(_pt(c, "A", ch)[:], t["xc2"][:, ch % 2, :],
                           t["r_b"][:])
        c.inc(ins, "c_p")
        c.mark(f"p_xr_{ch}", "c_p")

    c.wt(e, "t_A0st_0")
    _xload(c, e, 2, "s_x3")
    c.wt(e, "t_A0st_3")
    _xload(c, e, 5, "s_x3")
    xr(0)
    xr(1)
    for q in range(NCH // 2):
        c0, c1 = 2 * q, 2 * q + 1
        gated(c0, 0)
        gated(c1, 0)
        gated(c0, 1)
        gated(c1, 1)
        if dbg2:
            dump(c0, "dbg_a", "C", f"v_lac_{c0}")
            dump(c0, "dbg_cum", "B", f"v_lcum_{c0}")
            dump(c1, "dbg_a", "C", f"v_lac_{c1}")
            dump(c1, "dbg_cum", "B", f"v_lcum_{c1}")
        xt(c0)
        if c1 + 1 < NCH:
            xr(c1 + 1)
        xt(c1)
        if c1 + 2 < NCH:
            xr(c1 + 2)
    if dbg:
        for ch in range(NCH):
            c.wt(e, f"v_h_{ch}")
            c.inc(e.dma_start(t["dbg_h"].ap()[ch * 128:(ch + 1) * 128, :],
                              t["h_all"][:, ch, :]), "s_dbg")
    for ch in range(NCH):
        if ch == 0:
            c.wt(e, "v_A2bc")
        elif ch >= 2:
            c.wt(e, f"v_x1_{ch - 2}")
        ins = e.tensor_mul(_pt(c, "B", ch)[:], t["h_all"][:, ch, :],
                           t["r_b"][:])
        c.inc(ins, "c_p")
        c.mark(f"p_x1m_{ch}", "c_p")
    for ch in range(NCH):
        c.wt(e, f"v_x1_{ch}")
        c.inc(e.dma_start(t["x1sp"].ap()[ch * 128:(ch + 1) * 128, :],
                          t["h_all"][:, ch, :]), "s_sp")
    def spill_ln2(ch):
        c.wt(e, f"v_ln2_{ch}")
        c.inc(e.dma_start(t["ln2sp"].ap()[ch * 128:(ch + 1) * 128, :],
                          t["l2c2"][:, ch % 2, :]), "s_sp")

    for ch in range(NCH):
        if ch == 0:
            c.wt(e, "v_A3bc")
        elif ch >= 2:
            c.wt(e, f"v_ln2_{ch - 2}")
        ins = e.tensor_mul(_pt(c, "B", ch)[:], t["h_all"][:, ch, :],
                           t["r_b"][:])
        c.inc(ins, "c_p")
        c.mark(f"p_l2m_{ch}", "c_p")
        if ch == 0:
            for g in range(_L_A_END):
                _em_wload(c, e, g)
        if ch >= 1:
            spill_ln2(ch - 1)
    spill_ln2(7)


def _wsrc(c, g):
    t = c.t
    st, tau, mg, x = WL[g]
    if st == "s1":
        wname = "w1t_h" if x == 0 else "w2t_h"
        src = c.t[wname].ap()[0:1024, mg * 256:(mg + 1) * 256]
    elif st == "s2":
        src = t["h1t_h"].ap()[x * 1024:(x + 1) * 1024, mg * 256:(mg + 1) * 256]
    elif st == "s3":
        src = t["h2t_h"].ap()[x * 1024:(x + 1) * 1024, mg * 256:(mg + 1) * 256]
    else:
        src = t["w3t_h"].ap()[x * 1024:(x + 1) * 1024, mg * 256:(mg + 1) * 256]
    return src.rearrange("(a p) n -> p a n", p=128)


def _em_wload(c, e, g):
    if g >= NBUF:
        c.wt_raw(e, "c_w", g - (NBUF - 1))
    ins = e.dma_start(c.t["wbuf"][:, g % NBUF, :, :], _wsrc(c, g))
    c.inc(ins, "s_w2" if QID[g] else "s_w")


def _wt_w(c, e, g):
    c.wt_raw(e, "s_w2" if QID[g] else "s_w", 16 * (QORD[g] + 1))


def _rows(c, name, n2048=True):
    t = c.t
    return t[name][0:1, :]


def _em_rows_dve_1(c, e, pref):
    t = c.t
    r0, r1, r2, r3 = (t["T0"][0:1, :], t["T1"][0:1, :],
                      t["T2"][0:1, :], t["T3"][0:1, :])
    c.ss(e.tensor_copy(r0.rearrange("p (a n) -> p a n", a=4),
                       t["ps"][0:1, 0:4, :]), e, "c_v")
    c.ss(e.tensor_copy(r1.rearrange("p (a n) -> p a n", a=4),
                       t["ps"][0:1, 4:8, :]), e, "c_v")
    c.ss(e.tensor_mul(r2, r0, r0), e, "c_v")
    c.ss(e.tensor_sub(r3, r1, r2), e, "c_v")
    c.ss(e.tensor_single_scalar(r2, r3, 0.0, OP.max), e, "c_v")
    ins = e.tensor_scalar_add(r3, r2, EPS_LN)
    c.inc(ins, "c_v")
    c.mark(f"v_{pref}r1", "c_v")


def _em_rows_act(c, e, pref):
    t = c.t
    r2, r3, r4 = t["T2"][0:1, :], t["T3"][0:1, :], t["T4"][0:1, :]
    c.wt(e, f"v_{pref}r1")
    c.ss(e.activation(r2, r3, AF.Ln), e, "c_a")
    ins = e.activation(r4, r2, AF.Exp, scale=-0.5)
    c.inc(ins, "c_a")
    c.mark(f"a_{pref}r", "c_a")


def _em_rows_dve_2(c, e, pref):
    t = c.t
    r0, r4, r5 = t["T0"][0:1, :], t["T4"][0:1, :], t["T5"][0:1, :]
    c.wt(e, f"a_{pref}r")
    ins = e.tensor_mul(r5, r0, r4)
    c.inc(ins, "c_v")
    c.mark(f"v_{pref}r2", "c_v")


def _em_bcast_pe(c, e, pref):
    t = c.t
    c.wt(e, f"v_{pref}r2")
    r4, r5 = t["T4"][0:1, :], t["T5"][0:1, :]
    ins = None
    for n in range(4):
        ins = e.matmul(t["ps"][:, n, :], t["ones_row"][:],
                       r4[0:1, n * 512:(n + 1) * 512], start=True, stop=True)
    for n in range(4):
        ins = e.matmul(t["ps"][:, 4 + n, :], t["ones_row"][:],
                       r5[0:1, n * 512:(n + 1) * 512], start=True, stop=True)
    c.inc(ins, "c_t")
    c.mark(f"t_{pref}bc", "c_t")


def _em_bcast_evac_dve(c, e, pref):
    t = c.t
    c.wt(e, f"t_{pref}bc")
    e.tensor_copy(t["r_b"][:].rearrange("p (a n) -> p a n", a=4),
                  t["ps"][:, 0:4, :])
    ins = e.tensor_copy(t["mr_b"][:].rearrange("p (a n) -> p a n", a=4),
                        t["ps"][:, 4:8, :])
    c.inc(ins, "c_v")
    c.mark(f"v_{pref}bc", "c_v")


def em_A_scalar(c, e):
    t = c.t
    ps = t["ps"]
    for ch in range(NCH):
        if ch in (0, 3, 6):
            och = ch + 1
            if och >= 4:
                c.wt(e, f"t_A0st_{och - 2}")
            _xload(c, e, och, "s_x2")
        _wt_x1r(c, e, ch)
        if ch >= 2:
            c.wt(e, f"t_A0st_{ch - 2}")
        ins = e.activation(_pt(c, "A", ch)[:], t["xc2"][:, ch % 2, :],
                           AF.Square)
        c.inc(ins, "c_a")
        c.mark(f"a_A0sq_{ch}", "c_a")
    _em_rows_act(c, e, "A0")
    def sig(ch, hh, which):
        bk = 4 * (ch % 2) + (0 if which == "gx" else 2)
        dst = _pt(c, "A" if which == "gx" else "C", ch)
        sl = slice(hh * 1024, (hh + 1) * 1024)
        if which == "gx":
            c.wt(e, f"t_g{hh}_{ch}")
        elif hh == 0 and ch >= 2:
            c.wt(e, f"v_h_{ch - 2}")
        col = t["bcol_in" if which == "gx" else "bcol_a"][:, ch:ch + 1]
        ins = e.activation(dst[:, sl].rearrange("p (a n) -> p a n", a=2),
                           ps[:, bk:bk + 2, :], AF.Sigmoid, bias=col)
        c.inc(ins, "c_a")
        mk = {("gx", 0): f"a_gxh0_{ch}", ("ga", 0): f"a_gah0_{ch}",
              ("gx", 1): f"a_gx_{ch}", ("ga", 1): f"a_ga_{ch}"}[(which, hh)]
        c.mark(mk, "c_a")

    for q in range(NCH // 2):
        pair = (2 * q, 2 * q + 1)
        for ch in pair:
            sig(ch, 0, "gx")
        for ch in pair:
            sig(ch, 0, "ga")
        for ch in pair:
            sig(ch, 1, "gx")
        for ch in pair:
            sig(ch, 1, "ga")
        for ch in pair:
            c.wt(e, f"v_lac_{ch}")
            if ch >= 2:
                c.wt(e, f"p_xt_{ch - 2}")
            ins = e.activation(_pt(c, "D", ch)[:], _pt(c, "C", ch)[:],
                               AF.Exp, scale=2.0)
            c.inc(ins, "c_a")
            c.mark(f"a_aa_{ch}", "c_a")
        for ch in pair:
            c.wt(e, f"v_lcum_{ch}")
            if c.dbg == 2:
                c.wt(e, f"d3_dbg_a_{ch}")
            ins = e.activation(_pt(c, "C", ch)[:], _pt(c, "D", ch)[:],
                               AF.Sqrt, scale=-1.0, bias=1.0)
            c.inc(ins, "c_a")
            c.mark(f"a_mult_{ch}", "c_a")
        for ch in pair:
            B, Dd = _pt(c, "B", ch), _pt(c, "D", ch)
            c.wt(e, f"v_m1_{ch}")
            e.activation(B[:, 0:1], Dd[:, 0:1], AF.Copy, scale=0.0)
            ins = e.activation(B[:, 1:2], Dd[:, 0:1], AF.Copy)
            c.inc(ins, "c_a")
            c.mark(f"a_ed_{ch}", "c_a")
            c.wt(e, f"v_d_{ch}")
            ins = e.activation(t["h_all"][:, ch, :], B[:], AF.Exp)
            c.inc(ins, "c_a")
            c.mark(f"a_at_{ch}", "c_a")
    for ch in range(NCH):
        c.wt(e, f"v_h_{ch}")
        if ch >= 2:
            c.wt(e, f"t_A2st_{ch - 2}")
        else:
            c.wt(e, f"v_h_{6 + ch}")
        ins = e.activation(_pt(c, "A", ch)[:], t["h_all"][:, ch, :],
                           AF.Square)
        c.inc(ins, "c_a")
        c.mark(f"a_hsq_{ch}", "c_a")
    _em_rows_act(c, e, "A2")
    for ch in range(NCH):
        c.wt(e, f"v_x1_{ch}")
        if ch >= 2:
            c.wt(e, f"t_lnst_{ch - 2}")
        else:
            c.wt(e, f"t_A2st_{6 + ch}")
        ins = e.activation(_pt(c, "A", ch)[:], t["h_all"][:, ch, :],
                           AF.Square)
        c.inc(ins, "c_a")
        c.mark(f"a_x1sq_{ch}", "c_a")
    _em_rows_act(c, e, "A3")


def em_A_vector(c, e):
    t = c.t
    c.wt(e, "t_A0st_7")
    _em_rows_dve_1(c, e, "A0")
    _em_rows_dve_2(c, e, "A0")
    _em_bcast_evac_dve(c, e, "A0")
    def ln1s(ch):
        A, B = _pt(c, "A", ch), _pt(c, "B", ch)
        c.wt(e, f"p_xr_{ch}")
        if ch >= 2:
            c.wt(e, f"a_at_{ch - 2}")
        ins = e.tensor_sub(B[:], A[:], t["mr_b"][:])
        c.inc(ins, "c_v")
        c.mark(f"v_ln1s_{ch}", "c_v")

    def lac(ch):
        Cc = _pt(c, "C", ch)
        c.wt(e, f"a_ga_{ch}")
        ins = e.tensor_scalar(Cc[:], Cc[:],
                              t["nspcol"][:, ch:ch + 1], C1, OP.mult, OP.min)
        c.inc(ins, "c_v")
        c.mark(f"v_lac_{ch}", "c_v")

    def lcum(ch):
        B, C = _pt(c, "B", ch), _pt(c, "C", ch)
        c.wt(e, f"t_g1_{ch}")
        c.wt(e, f"p_gated_{ch}")
        if c.dbg == 2:
            c.wt(e, f"d3_dbg_ln1_{ch}")
        ins = e.tensor_tensor_scan(B[:], C[:], C[:], 0.0, OP.add, OP.bypass)
        c.inc(ins, "c_v")
        c.mark(f"v_lcum_{ch}", "c_v")

    def m1(ch):
        c.wt(e, f"a_mult_{ch}")
        ins = e.tensor_scalar(_pt(c, "D", ch)[:], _pt(c, "B", ch)[:],
                              0.0, LCL, OP.min, OP.max)
        c.inc(ins, "c_v")
        c.mark(f"v_m1_{ch}", "c_v")

    def d_normed(ch):
        A, B, C, D = (_pt(c, "A", ch), _pt(c, "B", ch),
                      _pt(c, "C", ch), _pt(c, "D", ch))
        if c.dbg == 2:
            c.wt(e, f"d3_dbg_cum_{ch}")
        ins = e.tensor_sub(B[:, 2:T], D[:, 1:T - 1], D[:, 0:T - 2])
        c.inc(ins, "c_v")
        c.mark(f"v_d_{ch}", "c_v")
        c.wt(e, f"a_ed_{ch}")
        ins = e.tensor_mul(D[:], A[:], C[:])
        c.inc(ins, "c_v")
        c.mark(f"v_normed_{ch}", "c_v")

    ln1s(0)
    ln1s(1)
    for q in range(NCH // 2):
        c0, c1 = 2 * q, 2 * q + 1
        lac(c0)
        lac(c1)
        lcum(c0)
        lcum(c1)
        m1(c0)
        m1(c1)
        d_normed(c0)
        d_normed(c1)
        if c1 + 2 < NCH:
            ln1s(c1 + 1)
            ln1s(c1 + 2)
        _em_hscan(c, e, c0)
        _em_hscan(c, e, c1)
    c.wt(e, "t_A2st_7")
    _em_rows_dve_1(c, e, "A2")
    _em_rows_dve_2(c, e, "A2")
    _em_bcast_evac_dve(c, e, "A2")
    for ch in range(NCH):
        p = ch % 2
        c.wt_raw(e, "s_x", 16 * (12 + ch))
        if getattr(c, "dbg", False):
            c.wt_raw(e, "s_dbg", 16 * (ch + 1))
        B, Cc = _pt(c, "B", ch), _pt(c, "C", ch)
        c.wt(e, f"p_x1m_{ch}")
        e.tensor_sub(Cc[:], B[:], t["mr_b"][:])
        ins = e.tensor_add(t["h_all"][:, ch, :], t["xc2"][:, p, :], Cc[:])
        c.inc(ins, "c_v")
        c.mark(f"v_x1_{ch}", "c_v")
    c.wt(e, "t_lnst_7")
    _em_rows_dve_1(c, e, "A3")
    _em_rows_dve_2(c, e, "A3")
    _em_bcast_evac_dve(c, e, "A3")
    for ch in range(NCH):
        if ch >= 2:
            c.wt_raw(e, "s_sp", 16 * (7 + ch))
        B = _pt(c, "B", ch)
        c.wt(e, f"p_l2m_{ch}")
        e.tensor_sub(t["ln2_t"][:, ch, :], B[:, 0:1024], t["mr_b"][:, 0:1024])
        ins = e.tensor_sub(t["l2c2"][:, ch % 2, :], B[:, 1024:T],
                           t["mr_b"][:, 1024:T])
        c.inc(ins, "c_v")
        c.mark(f"v_ln2_{ch}", "c_v")


def _em_hscan(c, e, ch):
    c.wt(e, f"a_at_{ch}")
    c.wt(e, f"p_xt_{ch}")
    h = c.t["h_all"][:, ch, :]
    ins = e.tensor_tensor_scan(h, h, _pt(c, "C", ch)[:], 0.0, OP.mult, OP.add)
    c.inc(ins, "c_v")
    c.mark(f"v_h_{ch}", "c_v")


def em_A_tensor(c, e):
    t = c.t
    ps = t["ps"]

    def stats8(ch, markpref, rhs_x):
        ins = None
        for n in range(4):
            ins = e.matmul(ps[0:1, n, :], t["cst_col"][:],
                           rhs_x[:, n * 512:(n + 1) * 512],
                           start=(ch == 0), stop=(ch == 7),
                           skip_group_check=True)
        sq = _pt(c, "A", ch)
        for n in range(4):
            ins = e.matmul(ps[0:1, 4 + n, :], t["cst_col"][:],
                           sq[:, n * 512:(n + 1) * 512],
                           start=(ch == 0), stop=(ch == 7),
                           skip_group_check=True)
        c.inc(ins, "c_t")
        c.mark(f"{markpref}_{ch}", "c_t")

    for ch in range(NCH):
        _wt_x1r(c, e, ch)
        c.wt(e, f"a_A0sq_{ch}")
        stats8(ch, "t_A0st", t["xc2"][:, ch % 2, :])
    _em_bcast_pe(c, e, "A0")
    def gates_half(ch, hh):
        bk = 4 * (ch % 2)
        B = _pt(c, "B", ch)
        if hh == 0:
            c.wt(e, f"v_ln1s_{ch}")
            if ch >= 2:
                c.wt(e, f"a_ga_{ch - 2}")
            else:
                c.wt(e, "v_A0bc")
        else:
            c.wt(e, f"a_gah0_{ch}")
        ins = None
        for n in range(2):
            nn = 2 * hh + n
            ins = e.matmul(ps[:, bk + n, :], t["wg_in"][:, ch, :],
                           B[:, nn * 512:(nn + 1) * 512],
                           start=True, stop=True)
        for n in range(2):
            nn = 2 * hh + n
            ins = e.matmul(ps[:, bk + 2 + n, :], t["wg_a"][:, ch, :],
                           B[:, nn * 512:(nn + 1) * 512],
                           start=True, stop=True)
        c.inc(ins, "c_t")
        c.mark(f"t_g{hh}_{ch}", "c_t")

    for q in range(NCH // 2):
        for hh in (0, 1):
            gates_half(2 * q, hh)
            gates_half(2 * q + 1, hh)
    for ch in range(NCH):
        c.wt(e, f"a_hsq_{ch}")
        if ch == 0:
            c.wt(e, "a_ga_7")
        stats8(ch, "t_A2st", t["h_all"][:, ch, :])
    _em_bcast_pe(c, e, "A2")
    for ch in range(NCH):
        c.wt(e, f"a_x1sq_{ch}")
        if ch == 0:
            c.wt(e, "v_A2bc")
        stats8(ch, "t_lnst", t["h_all"][:, ch, :])
    _em_bcast_pe(c, e, "A3")



def em_B1_sync(c, e, tau):
    em_B1_sync_wloads(c, e, tau)


def em_B1_gpsimd(c, e, tau):
    lo = _L_A_END if tau == 0 else _L_B2_0_END
    hi = _L_B1_0_END if tau == 0 else _L_B1_1_END
    for g in range(lo, hi):
        if QID[g] == 0:
            _em_wload(c, e, g)


def em_B1_sync_wloads(c, e, tau):
    lo = _L_A_END if tau == 0 else _L_B2_0_END
    hi = _L_B1_0_END if tau == 0 else _L_B1_1_END
    for g in range(lo, hi):
        if QID[g] == 1:
            _em_wload(c, e, g)


def em_B1_tensor(c, e, tau):
    t = c.t
    ps = t["ps"]
    if tau == 0:
        pass
    else:
        c.wt_raw(e, "s_l2", 16)
        c.wt(e, "v_bc4_0")
        c.wt(e, "ev_s4_0_7")
    for mg in range(16):
        for part in (0, 1):
            g = GIDX[("s1", tau, mg, part)]
            if mg >= 1:
                c.wt(e, [f"ev_s1a_{tau}_{mg - 1}", f"ev_s1b_{tau}_{mg - 1}"][part])
            _wt_w(c, e, g)
            ins = None
            for kk in range(8):
                if tau == 0 and mg == 0 and part == 0:
                    c.wt(e, f"v_ln2_{kk}")
                for mi in range(2):
                    lhsT = t["wbuf"][:, g % NBUF, kk, mi * 128:(mi + 1) * 128]
                    for n2 in range(2):
                        ins = e.matmul(ps[:, 4 * part + 2 * mi + n2, :], lhsT,
                                       t["ln2_t"][:, kk, n2 * 512:(n2 + 1) * 512],
                                       start=(kk == 0), stop=(kk == 7))
            c.inc(ins, "c_w")
            c.mark(f"pe_s1{'ab'[part]}_{tau}_{mg}", "c_w")
    for st, rhs in (("s2", "h1_t"), ("s3", "h2_t")):
        prev_ev = f"ev_s1b_{tau}_15" if st == "s2" else f"ev_s2_{tau}_15"
        c.wt(e, prev_ev)
        for mg in range(16):
            se = mg % 2
            if mg >= 2:
                c.wt(e, f"ev_{st}_{tau}_{mg - 2}")
            for kg in range(4):
                g = GIDX[(st, tau, mg, kg)]
                _wt_w(c, e, g)
                ins = None
                for kk in range(8):
                    k = kg * 8 + kk
                    for mi in range(2):
                        lhsT = t["wbuf"][:, g % NBUF, kk, mi * 128:(mi + 1) * 128]
                        for n2 in range(2):
                            ins = e.matmul(
                                ps[:, 4 * se + 2 * mi + n2, :], lhsT,
                                t[rhs][:, k, n2 * 512:(n2 + 1) * 512],
                                start=(k == 0), stop=(k == 31))
                c.inc(ins, "c_w")
            c.mark(f"pe_{st}_{tau}_{mg}", "c_w")


def em_B1_scalar(c, e, tau):
    t = c.t
    ps = t["ps"]
    if tau == 0:
        c.wt_raw(e, "s_sp", 16 * 16)
    else:
        c.wt(e, "v_C2_0")
    for mg in range(16):
        c.wt(e, f"pe_s1a_{tau}_{mg}")
        if mg >= 1:
            c.wt(e, f"ev_s1b_{tau}_{mg - 1}")
        ins = None
        for mi in range(2):
            m = 2 * mg + mi
            for n2 in range(2):
                ins = e.activation(t["t1_buf"][:, mi, n2 * 512:(n2 + 1) * 512],
                                   ps[:, 2 * mi + n2, :], AF.Silu,
                                   scale=t["s1c"][:, m:m + 1])
        c.inc(ins, "c_a")
        c.mark(f"ev_s1a_{tau}_{mg}", "c_a")
    for st, dst, scol in (("s2", "h2_t", "hs1c"), ("s3", "h1_t", "hs2c")):
        for mg in range(16):
            se = mg % 2
            c.wt(e, f"pe_{st}_{tau}_{mg}")
            if tau == 0 and st == "s2" and mg == 0:
                c.wt_raw(e, "s_sp", 16 * 16)
                src2 = t["ln2sp"].ap().rearrange("(a p) n -> p a n", p=128)
                c.inc(e.dma_start(t["ln2_t"][:], src2), "s_l2")
            ins = None
            for mi in range(2):
                m = 2 * mg + mi
                for n2 in range(2):
                    ins = e.activation(t[dst][:, m, n2 * 512:(n2 + 1) * 512],
                                       ps[:, 4 * se + 2 * mi + n2, :], AF.Gelu,
                                       scale=t[scol][:, m:m + 1])
            c.inc(ins, "c_a")
            c.mark(f"ev_{st}_{tau}_{mg}", "c_a")


def em_B1_vector(c, e, tau):
    t = c.t
    ps = t["ps"]
    if tau == 0:
        c.wt_raw(e, "s_sp", 16 * 16)
    else:
        c.wt(e, "pe_s4_0_7")
    for mg in range(16):
        c.wt(e, f"pe_s1b_{tau}_{mg}")
        c.wt(e, f"ev_s1a_{tau}_{mg}")
        ins = None
        for mi in range(2):
            m = 2 * mg + mi
            for n2 in range(2):
                ins = e.scalar_tensor_tensor(
                    t["h1_t"][:, m, n2 * 512:(n2 + 1) * 512],
                    ps[:, 4 + 2 * mi + n2, :], t["s2c"][:, m:m + 1],
                    t["t1_buf"][:, mi, n2 * 512:(n2 + 1) * 512],
                    OP.mult, OP.mult)
        c.inc(ins, "c_v")
        c.mark(f"ev_s1b_{tau}_{mg}", "c_v")



def em_B2_sync(c, e, tau):
    t = c.t
    em_B2_sync_wloads(c, e, tau)
    c.wt_raw(e, "s_sp", 16 * 8)
    if B2_LEVEL < 4:
        return
    for ch in range(NCH):
        if ch >= 2 and B2_LEVEL >= 6:
            c.wt(e, f"v_yc_{tau}_{ch - 2}")
            c.inc(e.dma_start(
                t["y_h"].ap()[(ch - 2) * 128:(ch - 1) * 128,
                              tau * 1024:(tau + 1) * 1024],
                t["y_t"][:, ch % 2, :]), "s_y")
        if tau == 1 and ch < 2:
            c.wt(e, f"v_yc_0_{6 + ch}")
        src = t["x1sp"].ap()[ch * 128:(ch + 1) * 128,
                             tau * 1024:(tau + 1) * 1024]
        c.inc(e.dma_start(t["x1r"][:, ch % 2, :], src), "s_x1r")
    if B2_LEVEL >= 6:
        for ch in (6, 7):
            c.wt(e, f"v_yc_{tau}_{ch}")
            c.inc(e.dma_start(
                t["y_h"].ap()[ch * 128:(ch + 1) * 128,
                              tau * 1024:(tau + 1) * 1024],
                t["y_t"][:, ch % 2, :]), "s_y")
        if tau == 1:
            c.wt_raw(e, "s_y", 16 * 16)


def em_B2_gpsimd(c, e, tau):
    lo = _L_B1_0_END if tau == 0 else _L_B1_1_END
    hi = _L_B2_0_END if tau == 0 else _L_B2_1_END
    for g in range(lo, hi):
        if QID[g] == 0:
            _em_wload(c, e, g)


def em_B2_sync_wloads(c, e, tau):
    lo = _L_B1_0_END if tau == 0 else _L_B1_1_END
    hi = _L_B2_0_END if tau == 0 else _L_B2_1_END
    for g in range(lo, hi):
        if QID[g] == 1:
            _em_wload(c, e, g)


def em_B2_tensor(c, e, tau):
    t = c.t
    ps = t["ps"]
    c.wt(e, f"ev_s3_{tau}_15")

    def stats(m):
        c.wt(e, f"ev_s4sq_{tau}_{m}")
        ins = None
        for n2 in range(2):
            ins = e.matmul(ps[0:1, 4 + n2, :], t["cst_col"][:],
                           t["ffn_t"][:, m, n2 * 512:(n2 + 1) * 512],
                           start=(m == 0), stop=(m == 7),
                           skip_group_check=True)
            ins = e.matmul(ps[0:1, 6 + n2, :], t["cst_col"][:],
                           t["sqf"][:, m % 2, n2 * 512:(n2 + 1) * 512],
                           start=(m == 0), stop=(m == 7),
                           skip_group_check=True)
        c.inc(ins, "c_t")
        c.mark(f"pe_s4stats_{tau}_{m}", "c_t")

    for m in range(8):
        if m >= 2:
            c.wt(e, f"ev_s4_{tau}_{m - 2}")
        for kg in range(4):
            g = GIDX[("s4", tau, m // 2, kg)]
            _wt_w(c, e, g)
            ins = None
            for kk in range(8):
                k = kg * 8 + kk
                lhsT = t["wbuf"][:, g % NBUF, kk,
                                 (m % 2) * 128:(m % 2 + 1) * 128]
                for n2 in range(2):
                    ins = e.matmul(ps[:, 2 * (m % 2) + n2, :], lhsT,
                                   t["h1_t"][:, k, n2 * 512:(n2 + 1) * 512],
                                   start=(k == 0), stop=(k == 31))
            if m % 2 == 1:
                c.inc(ins, "c_w")
            elif kg == 3:
                c.inc(ins, "c_t")
        c.mark(f"pe_s4_{tau}_{m}", "c_w" if m % 2 == 1 else "c_t")
        if m >= 1 and B2_LEVEL >= 2:
            stats(m - 1)
    if B2_LEVEL >= 2:
        stats(7)
    if B2_LEVEL < 3:
        return
    c.wt(e, f"v_C2_{tau}")
    r4 = t["y_t"][0:1, 0, :]
    r5 = t["y_t"][0:1, 1, :]
    ins = None
    for n2 in range(2):
        ins = e.matmul(ps[:, 4 + n2, :], t["ones_row"][:],
                       r4[0:1, n2 * 512:(n2 + 1) * 512], start=True, stop=True)
    for n2 in range(2):
        ins = e.matmul(ps[:, 6 + n2, :], t["ones_row"][:],
                       r5[0:1, n2 * 512:(n2 + 1) * 512], start=True, stop=True)
    c.inc(ins, "c_t")
    c.mark(f"t_bc4_{tau}", "c_t")


def em_B2_scalar(c, e, tau):
    t = c.t
    ps = t["ps"]
    if tau == 1:
        c.wt(e, "v_yc_0_7")
    for m in range(8):
        c.wt(e, f"pe_s4_{tau}_{m}")
        if m >= 2 and B2_LEVEL >= 2:
            c.wt(e, f"pe_s4stats_{tau}_{m - 2}")
        ins = None
        for n2 in range(2):
            ins = e.activation(t["ffn_t"][:, m, n2 * 512:(n2 + 1) * 512],
                               ps[:, 2 * (m % 2) + n2, :], AF.Copy,
                               scale=t["s3c"][:, m:m + 1])
        c.inc(ins, "c_a")
        c.mark(f"ev_s4_{tau}_{m}", "c_a")
        for n2 in range(2):
            ins = e.activation(t["sqf"][:, m % 2, n2 * 512:(n2 + 1) * 512],
                               t["ffn_t"][:, m, n2 * 512:(n2 + 1) * 512],
                               AF.Square)
        c.inc(ins, "c_a")
        c.mark(f"ev_s4sq_{tau}_{m}", "c_a")
    if B2_LEVEL < 3:
        return
    c.wt(e, f"v_C1_{tau}")
    tmp2 = t["sqf"][0:1, 1, :]
    r4 = t["y_t"][0:1, 0, :]
    s4 = t["t1_buf"][0:1, 1, :]
    c.ss(e.activation(tmp2, s4, AF.Ln), e, "c_a")
    ins = e.activation(r4, tmp2, AF.Exp, scale=-0.5)
    c.inc(ins, "c_a")
    c.mark(f"a_C_{tau}", "c_a")


def em_B2_vector(c, e, tau):
    t = c.t
    ps = t["ps"]
    if B2_LEVEL < 3:
        return
    m4 = t["t1_buf"][0:1, 0, :]
    s4 = t["t1_buf"][0:1, 1, :]
    tmp = t["sqf"][0:1, 0, :]
    r5 = t["y_t"][0:1, 1, :]
    c.wt(e, f"pe_s4stats_{tau}_7")
    c.ss(e.tensor_copy(m4.rearrange("p (a n) -> p a n", a=2),
                       ps[0:1, 4:6, :]), e, "c_v")
    c.ss(e.tensor_copy(s4.rearrange("p (a n) -> p a n", a=2),
                       ps[0:1, 6:8, :]), e, "c_v")
    c.ss(e.tensor_mul(tmp, m4, m4), e, "c_v")
    c.ss(e.tensor_sub(s4, s4, tmp), e, "c_v")
    c.ss(e.tensor_single_scalar(tmp, s4, 0.0, OP.max), e, "c_v")
    ins = e.tensor_scalar_add(s4, tmp, EPS_LN)
    c.inc(ins, "c_v")
    c.mark(f"v_C1_{tau}", "c_v")
    c.wt(e, f"a_C_{tau}")
    r4 = t["y_t"][0:1, 0, :]
    ins = e.tensor_mul(r5, m4, r4)
    c.inc(ins, "c_v")
    c.mark(f"v_C2_{tau}", "c_v")
    c.wt(e, f"t_bc4_{tau}")
    e.tensor_copy(t["r3_b"][:].rearrange("p (a n) -> p a n", a=2),
                  ps[:, 4:6, :])
    ins = e.tensor_copy(t["mr3_b"][:].rearrange("p (a n) -> p a n", a=2),
                        ps[:, 6:8, :])
    c.inc(ins, "c_v")
    c.mark(f"v_bc4_{tau}", "c_v")
    if B2_LEVEL < 4:
        return
    for ch in range(NCH):
        c.wt_raw(e, "s_x1r", 16 * (8 * tau + ch + 1))
        if B2_LEVEL >= 6:
            c.wt_raw(e, "s_y", 16 * (8 * tau + ch - 1))
        e.tensor_mul(t["sqf"][:, 0, :], t["ffn_t"][:, ch, :], t["r3_b"][:])
        e.tensor_sub(t["sqf"][:, 1, :], t["sqf"][:, 0, :], t["mr3_b"][:])
        ins = e.tensor_add(t["y_t"][:, ch % 2, :], t["x1r"][:, ch % 2, :],
                           t["sqf"][:, 1, :])
        c.inc(ins, "c_v")
        c.mark(f"v_yc_{tau}_{ch}", "c_v")



def build_nc(dbg=False, n_blocks=6):
    nc = bass.Bass("TRN2", target_bir_lowering=False)
    tn = {}
    tn["x_h"] = nc.dram_tensor("x", [D, T], F32, kind="ExternalInput")
    tn["wgin_h"] = nc.dram_tensor("wgin", [8, 128, 128], F32, kind="ExternalInput")
    tn["wga_h"] = nc.dram_tensor("wga", [8, 128, 128], F32, kind="ExternalInput")
    tn["bcin_h"] = nc.dram_tensor("bcin", [128, 8], F32, kind="ExternalInput")
    tn["bca_h"] = nc.dram_tensor("bca", [128, 8], F32, kind="ExternalInput")
    tn["acol_h"] = nc.dram_tensor("acol", [128, 8], F32, kind="ExternalInput")
    tn["w1t_h"] = nc.dram_tensor("w1t", [1024, 4096], F32, kind="ExternalInput")
    tn["w2t_h"] = nc.dram_tensor("w2t", [1024, 4096], F32, kind="ExternalInput")
    tn["h1t_h"] = nc.dram_tensor("h1t", [4096, 4096], F32, kind="ExternalInput")
    tn["h2t_h"] = nc.dram_tensor("h2t", [4096, 4096], F32, kind="ExternalInput")
    tn["w3t_h"] = nc.dram_tensor("w3t", [4096, 1024], F32, kind="ExternalInput")
    tn["s1c_h"] = nc.dram_tensor("s1c", [128, 32], F32, kind="ExternalInput")
    tn["s2c_h"] = nc.dram_tensor("s2c", [128, 32], F32, kind="ExternalInput")
    tn["hs1c_h"] = nc.dram_tensor("hs1c", [128, 32], F32, kind="ExternalInput")
    tn["hs2c_h"] = nc.dram_tensor("hs2c", [128, 32], F32, kind="ExternalInput")
    tn["s3c_h"] = nc.dram_tensor("s3c", [128, 8], F32, kind="ExternalInput")
    tn["x1sp"] = nc.dram_tensor("x1sp", [D, T], F32,
                                kind="ExternalOutput" if dbg else "Internal")
    tn["ln2sp"] = nc.dram_tensor("ln2sp", [D, 1024], BF16,
                                 kind="ExternalOutput" if dbg else "Internal")
    tn["y_h"] = nc.dram_tensor("y", [D, T], F32, kind="ExternalOutput")
    if dbg:
        tn["dbg_h"] = nc.dram_tensor("dbg_h", [D, T], F32, kind="ExternalOutput")
    if dbg == 2:
        for nm in ("dbg_ln1", "dbg_a", "dbg_nrm", "dbg_cum", "dbg_gx", "dbg_mu"):
            tn[nm] = nc.dram_tensor(nm, [D, T], F32, kind="ExternalOutput")

    tn["wg_in"] = nc.alloc_sbuf_tensor("wg_in", [128, 8, 128], F32)
    tn["wg_a"] = nc.alloc_sbuf_tensor("wg_a", [128, 8, 128], F32)
    tn["bcol_in"] = nc.alloc_sbuf_tensor("bcol_in", [128, 8], F32)
    tn["bcol_a"] = nc.alloc_sbuf_tensor("bcol_a", [128, 8], F32)
    tn["acol"] = nc.alloc_sbuf_tensor("acol_sb", [128, 8], F32)
    tn["nspcol"] = nc.alloc_sbuf_tensor("nspcol", [128, 8], F32)
    tn["tiny"] = nc.alloc_sbuf_tensor("tiny", [128, 32], F32)
    tn["s1c"] = nc.alloc_sbuf_tensor("s1c_sb", [128, 32], F32)
    tn["s2c"] = nc.alloc_sbuf_tensor("s2c_sb", [128, 32], F32)
    tn["hs1c"] = nc.alloc_sbuf_tensor("hs1c_sb", [128, 32], F32)
    tn["hs2c"] = nc.alloc_sbuf_tensor("hs2c_sb", [128, 32], F32)
    tn["s3c"] = nc.alloc_sbuf_tensor("s3c_sb", [128, 8], F32)
    tn["ones_row"] = nc.alloc_sbuf_tensor("ones_row", [1, 128], F32)
    tn["cst_col"] = nc.alloc_sbuf_tensor("cst_col", [128, 1], F32)
    tn["wbuf"] = nc.alloc_sbuf_tensor("wbuf", [128, NBUF, 8, 256], BF16)
    tn["ln2_t"] = nc.alloc_sbuf_tensor("ln2_t", [128, 8, 1024], BF16)
    tn["ps"] = nc.alloc_psum_tensor("ps", [128, 8, 512], F32)

    semnames = ["s_c", "s_x", "s_x2", "s_x3", "s_w", "s_w2", "s_sp", "s_l2", "s_x1r", "s_y", "s_dbg",
                "s_d3v", "s_d3a", "c_v", "c_a", "c_p", "c_t", "c_w", "bar"]
    sems = {n: nc.alloc_semaphore(n) for n in semnames}

    ctx = Ctx()
    ctx.sems = sems
    ctx.dbg = dbg

    def emit_all(c):
        blocks = []
        blocks.append({
            "sync": lambda e: em_S0_sync(c, e),
            "vector": lambda e: em_S0_vector(c, e),
            "scalar": lambda e: em_S0_scalar(c, e),
            "gpsimd": lambda e: None,
            "tensor": lambda e: None,
        })
        blocks.append({
            "sync": lambda e: em_A_sync(c, e),
            "gpsimd": lambda e: em_A_pool(c, e, dbg),
            "scalar": lambda e: em_A_scalar(c, e),
            "vector": lambda e: em_A_vector(c, e),
            "tensor": lambda e: em_A_tensor(c, e),
        })
        for tau in (0, 1):
            blocks.append({
                "sync": lambda e, tau=tau: em_B1_sync(c, e, tau),
                "gpsimd": lambda e, tau=tau: em_B1_gpsimd(c, e, tau),
                "tensor": lambda e, tau=tau: em_B1_tensor(c, e, tau),
                "scalar": lambda e, tau=tau: em_B1_scalar(c, e, tau),
                "vector": lambda e, tau=tau: em_B1_vector(c, e, tau),
            })
            blocks.append({
                "sync": lambda e, tau=tau: em_B2_sync(c, e, tau),
                "gpsimd": lambda e, tau=tau: em_B2_gpsimd(c, e, tau),
                "tensor": lambda e, tau=tau: em_B2_tensor(c, e, tau),
                "scalar": lambda e, tau=tau: em_B2_scalar(c, e, tau),
                "vector": lambda e, tau=tau: em_B2_vector(c, e, tau),
            })
        return blocks

    mock = _MockEng()
    mockt = _MockTensor()
    ctx.t = {k: mockt for k in list(tn.keys()) + [
        "xc2", "pA0", "pB0", "pC0", "pD0", "pA1", "pB1", "pC1", "pD1",
        "T0", "T1", "T2", "T3", "T4", "T5",
        "r_b", "mr_b", "h_all", "l2c2", "ln2_t", "h1_t", "h2_t", "t1_buf",
        "ffn_t", "sqf", "x1r", "y_t", "r3_b", "mr3_b"]}
    for blk in emit_all(ctx)[:n_blocks]:
        for en in ("sync", "gpsimd", "scalar", "vector", "tensor"):
            blk[en](mock)
    missing = ctx.refs - set(ctx.M.keys())
    assert not missing, f"unresolved marks: {sorted(missing)}"

    ctx.counting = False
    ctx.t = tn
    ctx._ssn = 0
    blocks = emit_all(ctx)
    engs = ("sync", "gpsimd", "scalar", "vector", "tensor")

    def emit_block(bi, blk):
        with nc.Block() as block:
            for en in engs:
                def body(eng, en=en, blk=blk, bi=bi):
                    if 0 < bi < 2:
                        eng.wait_ge(sems["bar"], 5 * bi)
                    blk[en](eng)
                    if bi < 1:
                        eng.sem_inc(sems["bar"], 1)
                getattr(block, en)(body)

    emit_block(0, blocks[0])
    if n_blocks >= 2:
        sa = _ScopeA(nc, tn)
        sa.__enter__()
        emit_block(1, blocks[1])
        sa.__exit__(None, None, None)
    if n_blocks >= 3:
        _ScopeBOuter(nc, tn).__enter__()
        for tau in (0, 1):
            if n_blocks >= 2 + 2 * tau + 1:
                b1 = _ScopeB1(nc, tn)
                b1.__enter__()
                emit_block(2 + 2 * tau, blocks[2 + 2 * tau])
                b1.__exit__(None, None, None)
            if n_blocks >= 3 + 2 * tau + 1:
                b2 = _ScopeB2(nc, tn)
                b2.__enter__()
                emit_block(3 + 2 * tau, blocks[3 + 2 * tau])
                b2.__exit__(None, None, None)
    return nc


class _ScopeA:
    def __init__(self, nc, tn):
        self.nc, self.tn = nc, tn
        self.keys = []

    def __enter__(self):
        nc, tn = self.nc, self.tn
        self.cms = []
        def add(key, shape, dt):
            cm = nc.sbuf_tensor(key, shape, dt)
            tn[key] = cm.__enter__()
            self.cms.append(cm)
            self.keys.append(key)
        add("xc2", [128, 2, T], F32)
        for p in (0, 1):
            for nm in "ABCD":
                add(f"p{nm}{p}", [128, T], F32)
        add("r_b", [128, T], F32)
        add("mr_b", [128, T], F32)
        add("h_all", [128, 8, T], F32)
        add("l2c2", [128, 2, 1024], BF16)
        for alias, real in (("T0", "pB0"), ("T1", "pC0"), ("T2", "pD0"),
                            ("T3", "pB1"), ("T4", "pC1"), ("T5", "pD1")):
            tn[alias] = tn[real]
        return self

    def __exit__(self, *a):
        for cm in reversed(self.cms):
            cm.__exit__(None, None, None)
        return False


class _ScopeBOuter:

    def __init__(self, nc, tn):
        self.nc, self.tn = nc, tn

    def __enter__(self):
        nc, tn = self.nc, self.tn
        tn["h1_t"] = nc.alloc_sbuf_tensor("h1_t", [128, 32, 1024], BF16)
        tn["t1_buf"] = nc.alloc_sbuf_tensor("t1_buf", [128, 2, 1024], F32)
        return self

    def __exit__(self, *a):
        return False


class _ScopeB1:
    _n = [0]

    def __init__(self, nc, tn):
        self.nc, self.tn = nc, tn

    def __enter__(self):
        self._n[0] += 1
        self.cm = self.nc.sbuf_tensor(f"h2_t_{self._n[0]}", [128, 32, 1024], BF16)
        self.tn["h2_t"] = self.cm.__enter__()
        return self

    def __exit__(self, *a):
        self.cm.__exit__(None, None, None)
        return False


class _ScopeB2:
    _n = [0]

    def __init__(self, nc, tn):
        self.nc, self.tn = nc, tn

    def __enter__(self):
        nc, tn = self.nc, self.tn
        self._n[0] += 1
        self.cms = []
        for key, shape, dt in [("ffn_t", [128, 8, 1024], F32),
                               ("sqf", [128, 2, 1024], F32),
                               ("x1r", [128, 2, 1024], F32),
                               ("y_t", [128, 2, 1024], F32),
                               ("r3_b", [128, 1024], F32),
                               ("mr3_b", [128, 1024], F32)]:
            cm = nc.sbuf_tensor(f"{key}_{self._n[0]}", shape, dt)
            tn[key] = cm.__enter__()
            self.cms.append(cm)
        return self

    def __exit__(self, *a):
        for cm in reversed(self.cms):
            cm.__exit__(None, None, None)
        return False



_NC_CACHE = {}


def _get_nc(dbg=False):
    if dbg not in _NC_CACHE:
        _NC_CACHE[dbg] = build_nc(dbg)
    return _NC_CACHE[dbg]


def _layout_inputs(inputs):
    f = np.ascontiguousarray
    x = np.asarray(inputs["L0_attn_x"], dtype=np.float32)
    common = {
        "wgin": f(np.asarray(inputs["L0_attn_w_in"], np.float32)),
        "wga": f(np.asarray(inputs["L0_attn_w_a"], np.float32)),
        "bcin": f(np.asarray(inputs["L0_attn_b_in"], np.float32).T),
        "bca": f(np.asarray(inputs["L0_attn_b_a"], np.float32).T),
        "acol": f(np.asarray(inputs["L0_attn_a_param"], np.float32)
                  .reshape(8, 128).T),
        "w1t": f(np.asarray(inputs["L0_ffn_w1_w"], np.float32).T),
        "w2t": f(np.asarray(inputs["L0_ffn_w2_w"], np.float32).T),
        "h1t": f(np.asarray(inputs["L0_ffn_h1_w"], np.float32).T),
        "h2t": f(np.asarray(inputs["L0_ffn_h2_w"], np.float32).T),
        "w3t": f(np.asarray(inputs["L0_ffn_w3_w"], np.float32).T),
        "s1c": f(np.asarray(inputs["L0_ffn_w1_s"], np.float32)
                 .reshape(32, 128).T),
        "s2c": f(np.asarray(inputs["L0_ffn_w2_s"], np.float32)
                 .reshape(32, 128).T),
        "hs1c": f(np.asarray(inputs["L0_ffn_h1_s"], np.float32)
                  .reshape(32, 128).T),
        "hs2c": f(np.asarray(inputs["L0_ffn_h2_s"], np.float32)
                  .reshape(32, 128).T),
        "s3c": f(np.asarray(inputs["L0_ffn_w3_s"], np.float32)
                 .reshape(8, 128).T),
    }
    in_maps = []
    for b in range(8):
        m = dict(common)
        m["x"] = f(x[b].T)
        in_maps.append(m)
    return in_maps


def kernel(**inputs):
    nc = _get_nc(dbg=False)
    in_maps = _layout_inputs(inputs)
    res = run_bass_kernel_spmd(nc, in_maps, core_ids=list(range(8)))
    out = np.stack([np.asarray(r["y"]).T for r in res.results], axis=0)
    return np.ascontiguousarray(out, dtype=np.float32)


# version 15
# speedup vs baseline: 1.0111x; 1.0004x over previous
import numpy as np
from contextlib import ExitStack

import concourse.bass as bass
import concourse.mybir as mybir
from concourse.bass_utils import run_bass_kernel_spmd

F32 = mybir.dt.float32
BF16 = mybir.dt.bfloat16
AF = mybir.ActivationFunctionType
OP = mybir.AluOpType

D = 1024
T = 2048
NCH = 8
NBUF = 4
EPS_LN = 1e-5
MIN_DECAY = 1e-3
SCAN_EPS = 1e-7
CLAMP_A = 1.0 - 1e-5


def _weight_loads():
    WL = []
    for tau in (0, 1):
        for mg in range(16):
            for part in (0, 1):
                WL.append(("s1", tau, mg, part))
        for st in ("s2", "s3"):
            for mg in range(16):
                for kg in range(4):
                    WL.append((st, tau, mg, kg))
        for mg in range(4):
            for kg in range(4):
                WL.append(("s4", tau, mg, kg))
    G = {k: g for g, k in enumerate(WL)}
    return WL, G


WL, GIDX = _weight_loads()
NLOADS = len(WL)
QID = [0 for g in range(NLOADS)]
QORD = []
_cnt = [0, 0]
for _g in range(NLOADS):
    QORD.append(_cnt[QID[_g]])
    _cnt[QID[_g]] += 1
B2_LEVEL = 6
_L_A_END = NBUF
_L_B1_0_END = GIDX[("s4", 0, 0, 0)]
_L_B2_0_END = GIDX[("s1", 1, 0, 0)]
_L_B1_1_END = GIDX[("s4", 1, 0, 0)]
_L_B2_1_END = NLOADS


class _MockEng:
    def __getattr__(self, name):
        return lambda *a, **k: None


class _MockTensor:
    def __getitem__(self, *a):
        return self

    def rearrange(self, *a, **k):
        return self

    def ap(self):
        return self


class Ctx:

    def __init__(self):
        self.counting = True
        self.cnt = {}
        self.M = {}
        self.refs = set()
        self.sems = None
        self.t = None
        self._ssn = 0

    def inc(self, ins, sem, n=1):
        amt = 16 * n if sem.startswith("s_") else n
        if self.counting:
            self.cnt[sem] = self.cnt.get(sem, 0) + amt
        else:
            ins.then_inc(self.sems[sem], amt)

    def mark(self, name, sem):
        if self.counting:
            assert name not in self.M, f"duplicate mark {name}"
            self.M[name] = (sem, self.cnt.get(sem, 0))

    def wt(self, eng, name):
        if self.counting:
            self.refs.add(name)
            return
        semname, v = self.M[name]
        if v > 0:
            eng.wait_ge(self.sems[semname], v)

    def wt_raw(self, eng, sem, v):
        if not self.counting and v > 0:
            eng.wait_ge(self.sems[sem], v)

    def ss(self, ins, eng, sem):
        name = f"_ss_{sem}_{self._ssn}"
        self._ssn += 1
        self.inc(ins, sem)
        self.mark(name, sem)
        self.wt(eng, name)



N_CONST = 10
C1 = float(np.log1p(-1e-5))
LCL = float(np.log(1e-7))


def em_S0_sync(c, e):
    t = c.t
    for name, dst in [("wgin_h", "wg_in"), ("wga_h", "wg_a"),
                      ("bcin_h", "bcol_in"), ("bca_h", "bcol_a"),
                      ("acol_h", "acol"), ("s1c_h", "s1c"), ("s2c_h", "s2c"),
                      ("hs1c_h", "hs1c"), ("hs2c_h", "hs2c"), ("s3c_h", "s3c")]:
        if name in ("wgin_h", "wga_h"):
            src = t[name].ap().rearrange("h i j -> i h j")
        else:
            src = t[name].ap()
        c.inc(e.dma_start(t[dst][:], src), "s_c")


def em_S0_vector(c, e):
    t = c.t
    e.memset(t["ones_row"][:], 1.0)
    ins = e.memset(t["cst_col"][:], 1.0 / D)
    c.inc(ins, "c_v")
    c.mark("v_ms", "c_v")
    c.wt_raw(e, "s_c", 16 * N_CONST)
    tr = t["tiny"][:, 0:8]
    tb = t["tiny"][:, 8:16]
    tl = t["tiny"][:, 24:32]
    ins = e.tensor_single_scalar(tr, t["acol"][:], 0.0, OP.max)
    c.inc(ins, "c_v")
    c.mark("v_pro1", "c_v")
    c.wt(e, "a_pro1")
    c.ss(e.scalar_tensor_tensor(tb, tl, MIN_DECAY, tr, OP.add, OP.add), e, "c_v")
    ins = e.tensor_single_scalar(t["nspcol"][:], tb, -1.0, OP.mult)
    c.inc(ins, "c_v")
    c.mark("v_pro2", "c_v")


def em_S0_scalar(c, e):
    t = c.t
    tb = t["tiny"][:, 8:16]
    te = t["tiny"][:, 16:24]
    tl = t["tiny"][:, 24:32]
    c.wt_raw(e, "s_c", 16 * N_CONST)
    c.ss(e.activation(tb, t["acol"][:], AF.Abs), e, "c_a")
    c.ss(e.activation(te, tb, AF.Exp, scale=-1.0), e, "c_a")
    ins = e.activation(tl, te, AF.Ln, bias=1.0)
    c.inc(ins, "c_a")
    c.mark("a_pro1", "c_a")


def em_S0_gpsimd(c, e):
    pass


def em_S0_tensor(c, e):
    pass




def _pt(c, name, ch):
    return c.t[f"p{name}{ch % 2}"]


def _xload(c, e, ch, sem):
    src = c.t["x_h"].ap()[ch * 128:(ch + 1) * 128, :]
    c.inc(e.dma_start(c.t["xc2"][:, ch % 2, :], src), sem)


XQ = {0: ("s_x", 1), 3: ("s_x", 2), 6: ("s_x", 3),
      1: ("s_x2", 1), 4: ("s_x2", 2), 7: ("s_x2", 3),
      2: ("s_x3", 1), 5: ("s_x3", 2)}


def _wt_x1r(c, e, ch):
    sem, o = XQ[ch]
    c.wt_raw(e, sem, 16 * o)


def em_A_sync(c, e):
    for ch in (0, 3, 6):
        if ch >= 2:
            c.wt(e, f"t_A0st_{ch - 2}")
        _xload(c, e, ch, "s_x")
    for ch in range(8):
        c.wt(e, [f"t_A0st_{6 + ch}", f"p_xr_{ch - 2}"][ch >= 2])
        _xload(c, e, ch, "s_x")
    for ch in range(8):
        c.wt(e, [f"p_xr_{6 + ch}", f"v_x1_{ch - 2}"][ch >= 2])
        _xload(c, e, ch, "s_x")


def em_A_pool(c, e, dbg=False):
    t = c.t
    dbg2 = dbg == 2

    def dump(ch, name, tile, markname):
        if not dbg2:
            return
        c.wt(e, markname)
        c.inc(e.dma_start(t[name].ap()[ch * 128:(ch + 1) * 128, :],
                          _pt(c, tile, ch)[:]), "s_d3v")
        c.mark(f"d3_{name}_{ch}", "s_d3v")

    def gated(ch, hh):
        sl = slice(hh * 1024, (hh + 1) * 1024)
        if hh == 0:
            dump(ch, "dbg_ln1", "B", f"v_ln1s_{ch}")
            c.wt(e, f"a_gxh0_{ch}")
            c.wt(e, f"v_ln1s_{ch}")
        else:
            c.wt(e, f"a_gx_{ch}")
        A = _pt(c, "A", ch)
        ins = e.tensor_mul(A[:, sl], A[:, sl], _pt(c, "B", ch)[:, sl])
        c.inc(ins, "c_p")
        c.mark([f"p_g0_{ch}", f"p_gated_{ch}"][hh], "c_p")
        if hh == 1:
            dump(ch, "dbg_gx", "A", f"p_gated_{ch}")

    def xt(ch):
        dump(ch, "dbg_mu", "C", f"a_mult_{ch}")
        dump(ch, "dbg_nrm", "D", f"v_normed_{ch}")
        c.wt(e, f"v_normed_{ch}")
        ins = e.tensor_scalar(_pt(c, "C", ch)[:], _pt(c, "D", ch)[:],
                              SCAN_EPS, 1.0, OP.add, OP.mult)
        c.inc(ins, "c_p")
        c.mark(f"p_xt_{ch}", "c_p")

    def xr(ch):
        c.wt_raw(e, "s_x", 16 * (4 + ch))
        if ch < 2:
            c.wt(e, "v_A0bc")
        else:
            c.wt(e, f"v_normed_{ch - 2}")
        ins = e.tensor_mul(_pt(c, "A", ch)[:], t["xc2"][:, ch % 2, :],
                           t["r_b"][:])
        c.inc(ins, "c_p")
        c.mark(f"p_xr_{ch}", "c_p")

    c.wt(e, "t_A0st_0")
    _xload(c, e, 2, "s_x3")
    c.wt(e, "t_A0st_3")
    _xload(c, e, 5, "s_x3")
    xr(0)
    xr(1)
    for q in range(NCH // 2):
        c0, c1 = 2 * q, 2 * q + 1
        gated(c0, 0)
        gated(c1, 0)
        gated(c0, 1)
        gated(c1, 1)
        if dbg2:
            dump(c0, "dbg_a", "C", f"v_lac_{c0}")
            dump(c0, "dbg_cum", "B", f"v_lcum_{c0}")
            dump(c1, "dbg_a", "C", f"v_lac_{c1}")
            dump(c1, "dbg_cum", "B", f"v_lcum_{c1}")
        xt(c0)
        if c1 + 1 < NCH:
            xr(c1 + 1)
        xt(c1)
        if c1 + 2 < NCH:
            xr(c1 + 2)
    if dbg:
        for ch in range(NCH):
            c.wt(e, f"v_h_{ch}")
            c.inc(e.dma_start(t["dbg_h"].ap()[ch * 128:(ch + 1) * 128, :],
                              t["h_all"][:, ch, :]), "s_dbg")
    for ch in range(NCH):
        if ch == 0:
            c.wt(e, "v_A2bc")
        elif ch >= 2:
            c.wt(e, f"v_x1_{ch - 2}")
        ins = e.tensor_mul(_pt(c, "B", ch)[:], t["h_all"][:, ch, :],
                           t["r_b"][:])
        c.inc(ins, "c_p")
        c.mark(f"p_x1m_{ch}", "c_p")
    for ch in range(NCH):
        c.wt(e, f"v_x1_{ch}")
        c.inc(e.dma_start(t["x1sp"].ap()[ch * 128:(ch + 1) * 128, :],
                          t["h_all"][:, ch, :]), "s_sp")
    def spill_ln2(ch):
        c.wt(e, f"v_ln2_{ch}")
        c.inc(e.dma_start(t["ln2sp"].ap()[ch * 128:(ch + 1) * 128, :],
                          t["l2c2"][:, ch % 2, :]), "s_sp")

    for ch in range(NCH):
        if ch == 0:
            c.wt(e, "v_A3bc")
        elif ch >= 2:
            c.wt(e, f"v_ln2_{ch - 2}")
        ins = e.tensor_mul(_pt(c, "B", ch)[:], t["h_all"][:, ch, :],
                           t["r_b"][:])
        c.inc(ins, "c_p")
        c.mark(f"p_l2m_{ch}", "c_p")
        if ch == 0:
            for g in range(_L_A_END):
                _em_wload(c, e, g)
        if ch >= 1:
            spill_ln2(ch - 1)
    spill_ln2(7)


def _wsrc(c, g):
    t = c.t
    st, tau, mg, x = WL[g]
    if st == "s1":
        wname = "w1t_h" if x == 0 else "w2t_h"
        src = c.t[wname].ap()[0:1024, mg * 256:(mg + 1) * 256]
    elif st == "s2":
        src = t["h1t_h"].ap()[x * 1024:(x + 1) * 1024, mg * 256:(mg + 1) * 256]
    elif st == "s3":
        src = t["h2t_h"].ap()[x * 1024:(x + 1) * 1024, mg * 256:(mg + 1) * 256]
    else:
        src = t["w3t_h"].ap()[x * 1024:(x + 1) * 1024, mg * 256:(mg + 1) * 256]
    return src.rearrange("(a p) n -> p a n", p=128)


def _em_wload(c, e, g):
    if g >= NBUF:
        c.wt_raw(e, "c_w", g - (NBUF - 1))
    ins = e.dma_start(c.t["wbuf"][:, g % NBUF, :, :], _wsrc(c, g))
    c.inc(ins, "s_w2" if QID[g] else "s_w")


def _wt_w(c, e, g):
    c.wt_raw(e, "s_w2" if QID[g] else "s_w", 16 * (QORD[g] + 1))


def _rows(c, name, n2048=True):
    t = c.t
    return t[name][0:1, :]


def _em_rows_dve_1(c, e, pref):
    t = c.t
    r0, r1, r2, r3 = (t["T0"][0:1, :], t["T1"][0:1, :],
                      t["T2"][0:1, :], t["T3"][0:1, :])
    c.ss(e.tensor_copy(r0.rearrange("p (a n) -> p a n", a=4),
                       t["ps"][0:1, 0:4, :]), e, "c_v")
    c.ss(e.tensor_copy(r1.rearrange("p (a n) -> p a n", a=4),
                       t["ps"][0:1, 4:8, :]), e, "c_v")
    c.ss(e.tensor_mul(r2, r0, r0), e, "c_v")
    c.ss(e.tensor_sub(r3, r1, r2), e, "c_v")
    c.ss(e.tensor_single_scalar(r2, r3, 0.0, OP.max), e, "c_v")
    ins = e.tensor_scalar_add(r3, r2, EPS_LN)
    c.inc(ins, "c_v")
    c.mark(f"v_{pref}r1", "c_v")


def _em_rows_act(c, e, pref):
    t = c.t
    r2, r3, r4 = t["T2"][0:1, :], t["T3"][0:1, :], t["T4"][0:1, :]
    c.wt(e, f"v_{pref}r1")
    c.ss(e.activation(r2, r3, AF.Ln), e, "c_a")
    ins = e.activation(r4, r2, AF.Exp, scale=-0.5)
    c.inc(ins, "c_a")
    c.mark(f"a_{pref}r", "c_a")


def _em_rows_dve_2(c, e, pref):
    t = c.t
    r0, r4, r5 = t["T0"][0:1, :], t["T4"][0:1, :], t["T5"][0:1, :]
    c.wt(e, f"a_{pref}r")
    ins = e.tensor_mul(r5, r0, r4)
    c.inc(ins, "c_v")
    c.mark(f"v_{pref}r2", "c_v")


def _em_bcast_pe(c, e, pref):
    t = c.t
    c.wt(e, f"v_{pref}r2")
    r4, r5 = t["T4"][0:1, :], t["T5"][0:1, :]
    ins = None
    for n in range(4):
        ins = e.matmul(t["ps"][:, n, :], t["ones_row"][:],
                       r4[0:1, n * 512:(n + 1) * 512], start=True, stop=True)
    for n in range(4):
        ins = e.matmul(t["ps"][:, 4 + n, :], t["ones_row"][:],
                       r5[0:1, n * 512:(n + 1) * 512], start=True, stop=True)
    c.inc(ins, "c_t")
    c.mark(f"t_{pref}bc", "c_t")


def _em_bcast_evac_dve(c, e, pref):
    t = c.t
    c.wt(e, f"t_{pref}bc")
    e.tensor_copy(t["r_b"][:].rearrange("p (a n) -> p a n", a=4),
                  t["ps"][:, 0:4, :])
    ins = e.tensor_copy(t["mr_b"][:].rearrange("p (a n) -> p a n", a=4),
                        t["ps"][:, 4:8, :])
    c.inc(ins, "c_v")
    c.mark(f"v_{pref}bc", "c_v")


def em_A_scalar(c, e):
    t = c.t
    ps = t["ps"]
    for ch in range(NCH):
        if ch in (0, 3, 6):
            och = ch + 1
            if och >= 4:
                c.wt(e, f"t_A0st_{och - 2}")
            _xload(c, e, och, "s_x2")
        _wt_x1r(c, e, ch)
        if ch >= 2:
            c.wt(e, f"t_A0st_{ch - 2}")
        ins = e.activation(_pt(c, "A", ch)[:], t["xc2"][:, ch % 2, :],
                           AF.Square)
        c.inc(ins, "c_a")
        c.mark(f"a_A0sq_{ch}", "c_a")
    _em_rows_act(c, e, "A0")
    def sig(ch, hh, which):
        bk = 4 * (ch % 2) + (0 if which == "gx" else 2)
        dst = _pt(c, "A" if which == "gx" else "C", ch)
        sl = slice(hh * 1024, (hh + 1) * 1024)
        if which == "gx":
            c.wt(e, f"t_g{hh}_{ch}")
        elif hh == 0 and ch >= 2:
            c.wt(e, f"v_h_{ch - 2}")
        col = t["bcol_in" if which == "gx" else "bcol_a"][:, ch:ch + 1]
        ins = e.activation(dst[:, sl].rearrange("p (a n) -> p a n", a=2),
                           ps[:, bk:bk + 2, :], AF.Sigmoid, bias=col)
        c.inc(ins, "c_a")
        mk = {("gx", 0): f"a_gxh0_{ch}", ("ga", 0): f"a_gah0_{ch}",
              ("gx", 1): f"a_gx_{ch}", ("ga", 1): f"a_ga_{ch}"}[(which, hh)]
        c.mark(mk, "c_a")

    for q in range(NCH // 2):
        pair = (2 * q, 2 * q + 1)
        for ch in pair:
            sig(ch, 0, "gx")
        for ch in pair:
            sig(ch, 0, "ga")
        for ch in pair:
            sig(ch, 1, "gx")
        for ch in pair:
            sig(ch, 1, "ga")
        for ch in pair:
            c.wt(e, f"v_lac_{ch}")
            if ch >= 2:
                c.wt(e, f"p_xt_{ch - 2}")
            ins = e.activation(_pt(c, "D", ch)[:], _pt(c, "C", ch)[:],
                               AF.Exp, scale=2.0)
            c.inc(ins, "c_a")
            c.mark(f"a_aa_{ch}", "c_a")
        for ch in pair:
            c.wt(e, f"v_lcum_{ch}")
            if c.dbg == 2:
                c.wt(e, f"d3_dbg_a_{ch}")
            ins = e.activation(_pt(c, "C", ch)[:], _pt(c, "D", ch)[:],
                               AF.Sqrt, scale=-1.0, bias=1.0)
            c.inc(ins, "c_a")
            c.mark(f"a_mult_{ch}", "c_a")
        for ch in pair:
            B, Dd = _pt(c, "B", ch), _pt(c, "D", ch)
            c.wt(e, f"v_m1_{ch}")
            e.activation(B[:, 0:1], Dd[:, 0:1], AF.Copy, scale=0.0)
            ins = e.activation(B[:, 1:2], Dd[:, 0:1], AF.Copy)
            c.inc(ins, "c_a")
            c.mark(f"a_ed_{ch}", "c_a")
            c.wt(e, f"v_d_{ch}")
            ins = e.activation(t["h_all"][:, ch, :], B[:], AF.Exp)
            c.inc(ins, "c_a")
            c.mark(f"a_at_{ch}", "c_a")
    for ch in range(NCH):
        c.wt(e, f"v_h_{ch}")
        if ch >= 2:
            c.wt(e, f"t_A2st_{ch - 2}")
        else:
            c.wt(e, f"v_h_{6 + ch}")
        ins = e.activation(_pt(c, "A", ch)[:], t["h_all"][:, ch, :],
                           AF.Square)
        c.inc(ins, "c_a")
        c.mark(f"a_hsq_{ch}", "c_a")
    _em_rows_act(c, e, "A2")
    for ch in range(NCH):
        c.wt(e, f"v_x1_{ch}")
        if ch >= 2:
            c.wt(e, f"t_lnst_{ch - 2}")
        else:
            c.wt(e, f"t_A2st_{6 + ch}")
        ins = e.activation(_pt(c, "A", ch)[:], t["h_all"][:, ch, :],
                           AF.Square)
        c.inc(ins, "c_a")
        c.mark(f"a_x1sq_{ch}", "c_a")
    _em_rows_act(c, e, "A3")


def em_A_vector(c, e):
    t = c.t
    c.wt(e, "t_A0st_7")
    _em_rows_dve_1(c, e, "A0")
    _em_rows_dve_2(c, e, "A0")
    _em_bcast_evac_dve(c, e, "A0")
    def ln1s(ch):
        A, B = _pt(c, "A", ch), _pt(c, "B", ch)
        c.wt(e, f"p_xr_{ch}")
        if ch >= 2:
            c.wt(e, f"a_at_{ch - 2}")
        ins = e.tensor_sub(B[:], A[:], t["mr_b"][:])
        c.inc(ins, "c_v")
        c.mark(f"v_ln1s_{ch}", "c_v")

    def lac(ch):
        Cc = _pt(c, "C", ch)
        c.wt(e, f"a_ga_{ch}")
        ins = e.tensor_scalar(Cc[:], Cc[:],
                              t["nspcol"][:, ch:ch + 1], C1, OP.mult, OP.min)
        c.inc(ins, "c_v")
        c.mark(f"v_lac_{ch}", "c_v")

    def lcum(ch):
        B, C = _pt(c, "B", ch), _pt(c, "C", ch)
        c.wt(e, f"t_g1_{ch}")
        c.wt(e, f"p_gated_{ch}")
        if c.dbg == 2:
            c.wt(e, f"d3_dbg_ln1_{ch}")
        ins = e.tensor_tensor_scan(B[:], C[:], C[:], 0.0, OP.add, OP.bypass)
        c.inc(ins, "c_v")
        c.mark(f"v_lcum_{ch}", "c_v")

    def m1(ch):
        c.wt(e, f"a_mult_{ch}")
        ins = e.tensor_scalar(_pt(c, "D", ch)[:], _pt(c, "B", ch)[:],
                              0.0, LCL, OP.min, OP.max)
        c.inc(ins, "c_v")
        c.mark(f"v_m1_{ch}", "c_v")

    def d_normed(ch):
        A, B, C, D = (_pt(c, "A", ch), _pt(c, "B", ch),
                      _pt(c, "C", ch), _pt(c, "D", ch))
        if c.dbg == 2:
            c.wt(e, f"d3_dbg_cum_{ch}")
        ins = e.tensor_sub(B[:, 2:T], D[:, 1:T - 1], D[:, 0:T - 2])
        c.inc(ins, "c_v")
        c.mark(f"v_d_{ch}", "c_v")
        c.wt(e, f"a_ed_{ch}")
        ins = e.tensor_mul(D[:], A[:], C[:])
        c.inc(ins, "c_v")
        c.mark(f"v_normed_{ch}", "c_v")

    ln1s(0)
    ln1s(1)
    for q in range(NCH // 2):
        c0, c1 = 2 * q, 2 * q + 1
        lac(c0)
        lac(c1)
        lcum(c0)
        lcum(c1)
        m1(c0)
        m1(c1)
        d_normed(c0)
        d_normed(c1)
        if c1 + 2 < NCH:
            ln1s(c1 + 1)
            ln1s(c1 + 2)
        _em_hscan(c, e, c0)
        _em_hscan(c, e, c1)
    c.wt(e, "t_A2st_7")
    _em_rows_dve_1(c, e, "A2")
    _em_rows_dve_2(c, e, "A2")
    _em_bcast_evac_dve(c, e, "A2")
    for ch in range(NCH):
        p = ch % 2
        c.wt_raw(e, "s_x", 16 * (12 + ch))
        if getattr(c, "dbg", False):
            c.wt_raw(e, "s_dbg", 16 * (ch + 1))
        B, Cc = _pt(c, "B", ch), _pt(c, "C", ch)
        c.wt(e, f"p_x1m_{ch}")
        e.tensor_sub(Cc[:], B[:], t["mr_b"][:])
        ins = e.tensor_add(t["h_all"][:, ch, :], t["xc2"][:, p, :], Cc[:])
        c.inc(ins, "c_v")
        c.mark(f"v_x1_{ch}", "c_v")
    c.wt(e, "t_lnst_7")
    _em_rows_dve_1(c, e, "A3")
    _em_rows_dve_2(c, e, "A3")
    _em_bcast_evac_dve(c, e, "A3")
    for ch in range(NCH):
        if ch >= 2:
            c.wt_raw(e, "s_sp", 16 * (7 + ch))
        B = _pt(c, "B", ch)
        c.wt(e, f"p_l2m_{ch}")
        e.tensor_sub(t["ln2_t"][:, ch, :], B[:, 0:1024], t["mr_b"][:, 0:1024])
        ins = e.tensor_sub(t["l2c2"][:, ch % 2, :], B[:, 1024:T],
                           t["mr_b"][:, 1024:T])
        c.inc(ins, "c_v")
        c.mark(f"v_ln2_{ch}", "c_v")


def _em_hscan(c, e, ch):
    c.wt(e, f"a_at_{ch}")
    c.wt(e, f"p_xt_{ch}")
    h = c.t["h_all"][:, ch, :]
    ins = e.tensor_tensor_scan(h, h, _pt(c, "C", ch)[:], 0.0, OP.mult, OP.add)
    c.inc(ins, "c_v")
    c.mark(f"v_h_{ch}", "c_v")


def em_A_tensor(c, e):
    t = c.t
    ps = t["ps"]

    def stats8(ch, markpref, rhs_x):
        ins = None
        for n in range(4):
            ins = e.matmul(ps[0:1, n, :], t["cst_col"][:],
                           rhs_x[:, n * 512:(n + 1) * 512],
                           start=(ch == 0), stop=(ch == 7),
                           skip_group_check=True)
        sq = _pt(c, "A", ch)
        for n in range(4):
            ins = e.matmul(ps[0:1, 4 + n, :], t["cst_col"][:],
                           sq[:, n * 512:(n + 1) * 512],
                           start=(ch == 0), stop=(ch == 7),
                           skip_group_check=True)
        c.inc(ins, "c_t")
        c.mark(f"{markpref}_{ch}", "c_t")

    c.wt(e, "v_ms")
    for ch in range(NCH):
        _wt_x1r(c, e, ch)
        c.wt(e, f"a_A0sq_{ch}")
        stats8(ch, "t_A0st", t["xc2"][:, ch % 2, :])
    _em_bcast_pe(c, e, "A0")
    def gates_half(ch, hh):
        bk = 4 * (ch % 2)
        B = _pt(c, "B", ch)
        if hh == 0:
            c.wt(e, f"v_ln1s_{ch}")
            if ch >= 2:
                c.wt(e, f"a_ga_{ch - 2}")
            else:
                c.wt(e, "v_A0bc")
        else:
            c.wt(e, f"a_gah0_{ch}")
        ins = None
        for n in range(2):
            nn = 2 * hh + n
            ins = e.matmul(ps[:, bk + n, :], t["wg_in"][:, ch, :],
                           B[:, nn * 512:(nn + 1) * 512],
                           start=True, stop=True)
        for n in range(2):
            nn = 2 * hh + n
            ins = e.matmul(ps[:, bk + 2 + n, :], t["wg_a"][:, ch, :],
                           B[:, nn * 512:(nn + 1) * 512],
                           start=True, stop=True)
        c.inc(ins, "c_t")
        c.mark(f"t_g{hh}_{ch}", "c_t")

    for q in range(NCH // 2):
        for hh in (0, 1):
            gates_half(2 * q, hh)
            gates_half(2 * q + 1, hh)
    for ch in range(NCH):
        c.wt(e, f"a_hsq_{ch}")
        if ch == 0:
            c.wt(e, "a_ga_7")
        stats8(ch, "t_A2st", t["h_all"][:, ch, :])
    _em_bcast_pe(c, e, "A2")
    for ch in range(NCH):
        c.wt(e, f"a_x1sq_{ch}")
        if ch == 0:
            c.wt(e, "v_A2bc")
        stats8(ch, "t_lnst", t["h_all"][:, ch, :])
    _em_bcast_pe(c, e, "A3")



def em_B1_sync(c, e, tau):
    em_B1_sync_wloads(c, e, tau)


def em_B1_gpsimd(c, e, tau):
    lo = _L_A_END if tau == 0 else _L_B2_0_END
    hi = _L_B1_0_END if tau == 0 else _L_B1_1_END
    for g in range(lo, hi):
        if QID[g] == 0:
            _em_wload(c, e, g)


def em_B1_sync_wloads(c, e, tau):
    lo = _L_A_END if tau == 0 else _L_B2_0_END
    hi = _L_B1_0_END if tau == 0 else _L_B1_1_END
    for g in range(lo, hi):
        if QID[g] == 1:
            _em_wload(c, e, g)


def em_B1_tensor(c, e, tau):
    t = c.t
    ps = t["ps"]
    if tau == 0:
        c.wt(e, "v_A3bc")
    else:
        c.wt_raw(e, "s_l2", 16)
        c.wt(e, "v_bc4_0")
        c.wt(e, "ev_s4_0_7")
    for mg in range(16):
        for part in (0, 1):
            g = GIDX[("s1", tau, mg, part)]
            if mg >= 1:
                c.wt(e, [f"ev_s1a_{tau}_{mg - 1}", f"ev_s1b_{tau}_{mg - 1}"][part])
            _wt_w(c, e, g)
            ins = None
            for kk in range(8):
                if tau == 0 and mg == 0 and part == 0:
                    c.wt(e, f"v_ln2_{kk}")
                for mi in range(2):
                    lhsT = t["wbuf"][:, g % NBUF, kk, mi * 128:(mi + 1) * 128]
                    for n2 in range(2):
                        ins = e.matmul(ps[:, 4 * part + 2 * mi + n2, :], lhsT,
                                       t["ln2_t"][:, kk, n2 * 512:(n2 + 1) * 512],
                                       start=(kk == 0), stop=(kk == 7))
            c.inc(ins, "c_w")
            c.mark(f"pe_s1{'ab'[part]}_{tau}_{mg}", "c_w")
    for st, rhs in (("s2", "h1_t"), ("s3", "h2_t")):
        prev_ev = f"ev_s1b_{tau}_15" if st == "s2" else f"ev_s2_{tau}_15"
        c.wt(e, prev_ev)
        for mg in range(16):
            se = mg % 2
            if mg >= 2:
                c.wt(e, f"ev_{st}_{tau}_{mg - 2}")
            for kg in range(4):
                g = GIDX[(st, tau, mg, kg)]
                _wt_w(c, e, g)
                ins = None
                for kk in range(8):
                    k = kg * 8 + kk
                    for mi in range(2):
                        lhsT = t["wbuf"][:, g % NBUF, kk, mi * 128:(mi + 1) * 128]
                        for n2 in range(2):
                            ins = e.matmul(
                                ps[:, 4 * se + 2 * mi + n2, :], lhsT,
                                t[rhs][:, k, n2 * 512:(n2 + 1) * 512],
                                start=(k == 0), stop=(k == 31))
                c.inc(ins, "c_w")
            c.mark(f"pe_{st}_{tau}_{mg}", "c_w")


def em_B1_scalar(c, e, tau):
    t = c.t
    ps = t["ps"]
    if tau == 0:
        c.wt_raw(e, "s_sp", 16 * 16)
    else:
        c.wt(e, "v_C2_0")
    for mg in range(16):
        c.wt(e, f"pe_s1a_{tau}_{mg}")
        if mg >= 1:
            c.wt(e, f"ev_s1b_{tau}_{mg - 1}")
        ins = None
        for mi in range(2):
            m = 2 * mg + mi
            for n2 in range(2):
                ins = e.activation(t["t1_buf"][:, mi, n2 * 512:(n2 + 1) * 512],
                                   ps[:, 2 * mi + n2, :], AF.Silu,
                                   scale=t["s1c"][:, m:m + 1])
        c.inc(ins, "c_a")
        c.mark(f"ev_s1a_{tau}_{mg}", "c_a")
    for st, dst, scol in (("s2", "h2_t", "hs1c"), ("s3", "h1_t", "hs2c")):
        for mg in range(16):
            se = mg % 2
            c.wt(e, f"pe_{st}_{tau}_{mg}")
            if tau == 0 and st == "s2" and mg == 0:
                c.wt_raw(e, "s_sp", 16 * 16)
                src2 = t["ln2sp"].ap().rearrange("(a p) n -> p a n", p=128)
                c.inc(e.dma_start(t["ln2_t"][:], src2), "s_l2")
            if tau == 1 and st == "s2" and mg == 0:
                c.wt(e, "v_yc_0_7")
                c.wt_raw(e, "s_y", 16 * 8)
            ins = None
            for mi in range(2):
                m = 2 * mg + mi
                for n2 in range(2):
                    ins = e.activation(t[dst][:, m, n2 * 512:(n2 + 1) * 512],
                                       ps[:, 4 * se + 2 * mi + n2, :], AF.Gelu,
                                       scale=t[scol][:, m:m + 1])
            c.inc(ins, "c_a")
            c.mark(f"ev_{st}_{tau}_{mg}", "c_a")


def em_B1_vector(c, e, tau):
    t = c.t
    ps = t["ps"]
    if tau == 0:
        c.wt_raw(e, "s_sp", 16 * 16)
    else:
        c.wt(e, "pe_s4_0_7")
    for mg in range(16):
        c.wt(e, f"pe_s1b_{tau}_{mg}")
        c.wt(e, f"ev_s1a_{tau}_{mg}")
        ins = None
        for mi in range(2):
            m = 2 * mg + mi
            for n2 in range(2):
                ins = e.scalar_tensor_tensor(
                    t["h1_t"][:, m, n2 * 512:(n2 + 1) * 512],
                    ps[:, 4 + 2 * mi + n2, :], t["s2c"][:, m:m + 1],
                    t["t1_buf"][:, mi, n2 * 512:(n2 + 1) * 512],
                    OP.mult, OP.mult)
        c.inc(ins, "c_v")
        c.mark(f"ev_s1b_{tau}_{mg}", "c_v")



def em_B2_sync(c, e, tau):
    t = c.t
    em_B2_sync_wloads(c, e, tau)
    c.wt_raw(e, "s_sp", 16 * 8)
    if B2_LEVEL < 4:
        return
    c.wt(e, f"pe_s3_{tau}_15")
    for ch in range(NCH):
        if ch >= 2 and B2_LEVEL >= 6:
            c.wt(e, f"v_yc_{tau}_{ch - 2}")
            c.inc(e.dma_start(
                t["y_h"].ap()[(ch - 2) * 128:(ch - 1) * 128,
                              tau * 1024:(tau + 1) * 1024],
                t["y_t"][:, ch % 2, :]), "s_y")
        if tau == 1 and ch < 2:
            c.wt(e, f"v_yc_0_{6 + ch}")
        src = t["x1sp"].ap()[ch * 128:(ch + 1) * 128,
                             tau * 1024:(tau + 1) * 1024]
        c.inc(e.dma_start(t["x1r"][:, ch % 2, :], src), "s_x1r")
    if B2_LEVEL >= 6:
        for ch in (6, 7):
            c.wt(e, f"v_yc_{tau}_{ch}")
            c.inc(e.dma_start(
                t["y_h"].ap()[ch * 128:(ch + 1) * 128,
                              tau * 1024:(tau + 1) * 1024],
                t["y_t"][:, ch % 2, :]), "s_y")
        if tau == 1:
            c.wt_raw(e, "s_y", 16 * 16)


def em_B2_gpsimd(c, e, tau):
    lo = _L_B1_0_END if tau == 0 else _L_B1_1_END
    hi = _L_B2_0_END if tau == 0 else _L_B2_1_END
    for g in range(lo, hi):
        if QID[g] == 0:
            _em_wload(c, e, g)


def em_B2_sync_wloads(c, e, tau):
    lo = _L_B1_0_END if tau == 0 else _L_B1_1_END
    hi = _L_B2_0_END if tau == 0 else _L_B2_1_END
    for g in range(lo, hi):
        if QID[g] == 1:
            _em_wload(c, e, g)


def em_B2_tensor(c, e, tau):
    t = c.t
    ps = t["ps"]
    c.wt(e, f"ev_s3_{tau}_15")

    def stats(m):
        c.wt(e, f"ev_s4sq_{tau}_{m}")
        ins = None
        for n2 in range(2):
            ins = e.matmul(ps[0:1, 4 + n2, :], t["cst_col"][:],
                           t["ffn_t"][:, m, n2 * 512:(n2 + 1) * 512],
                           start=(m == 0), stop=(m == 7),
                           skip_group_check=True)
            ins = e.matmul(ps[0:1, 6 + n2, :], t["cst_col"][:],
                           t["sqf"][:, m % 2, n2 * 512:(n2 + 1) * 512],
                           start=(m == 0), stop=(m == 7),
                           skip_group_check=True)
        c.inc(ins, "c_t")
        c.mark(f"pe_s4stats_{tau}_{m}", "c_t")

    for m in range(8):
        if m >= 2:
            c.wt(e, f"ev_s4_{tau}_{m - 2}")
        for kg in range(4):
            g = GIDX[("s4", tau, m // 2, kg)]
            _wt_w(c, e, g)
            ins = None
            for kk in range(8):
                k = kg * 8 + kk
                lhsT = t["wbuf"][:, g % NBUF, kk,
                                 (m % 2) * 128:(m % 2 + 1) * 128]
                for n2 in range(2):
                    ins = e.matmul(ps[:, 2 * (m % 2) + n2, :], lhsT,
                                   t["h1_t"][:, k, n2 * 512:(n2 + 1) * 512],
                                   start=(k == 0), stop=(k == 31))
            if m % 2 == 1:
                c.inc(ins, "c_w")
            elif kg == 3:
                c.inc(ins, "c_t")
        c.mark(f"pe_s4_{tau}_{m}", "c_w" if m % 2 == 1 else "c_t")
        if m >= 1 and B2_LEVEL >= 2:
            stats(m - 1)
    if B2_LEVEL >= 2:
        stats(7)
    if B2_LEVEL < 3:
        return
    c.wt(e, f"v_C2_{tau}")
    r4 = t["y_t"][0:1, 0, :]
    r5 = t["y_t"][0:1, 1, :]
    ins = None
    for n2 in range(2):
        ins = e.matmul(ps[:, 4 + n2, :], t["ones_row"][:],
                       r4[0:1, n2 * 512:(n2 + 1) * 512], start=True, stop=True)
    for n2 in range(2):
        ins = e.matmul(ps[:, 6 + n2, :], t["ones_row"][:],
                       r5[0:1, n2 * 512:(n2 + 1) * 512], start=True, stop=True)
    c.inc(ins, "c_t")
    c.mark(f"t_bc4_{tau}", "c_t")


def em_B2_scalar(c, e, tau):
    t = c.t
    ps = t["ps"]
    if tau == 1:
        c.wt(e, "v_yc_0_7")
    for m in range(8):
        c.wt(e, f"pe_s4_{tau}_{m}")
        if m >= 2 and B2_LEVEL >= 2:
            c.wt(e, f"pe_s4stats_{tau}_{m - 2}")
        ins = None
        for n2 in range(2):
            ins = e.activation(t["ffn_t"][:, m, n2 * 512:(n2 + 1) * 512],
                               ps[:, 2 * (m % 2) + n2, :], AF.Copy,
                               scale=t["s3c"][:, m:m + 1])
        c.inc(ins, "c_a")
        c.mark(f"ev_s4_{tau}_{m}", "c_a")
        for n2 in range(2):
            ins = e.activation(t["sqf"][:, m % 2, n2 * 512:(n2 + 1) * 512],
                               t["ffn_t"][:, m, n2 * 512:(n2 + 1) * 512],
                               AF.Square)
        c.inc(ins, "c_a")
        c.mark(f"ev_s4sq_{tau}_{m}", "c_a")
    if B2_LEVEL < 3:
        return
    c.wt(e, f"v_C1_{tau}")
    tmp2 = t["sqf"][0:1, 1, :]
    r4 = t["y_t"][0:1, 0, :]
    s4 = t["t1_buf"][0:1, 1, :]
    c.ss(e.activation(tmp2, s4, AF.Ln), e, "c_a")
    ins = e.activation(r4, tmp2, AF.Exp, scale=-0.5)
    c.inc(ins, "c_a")
    c.mark(f"a_C_{tau}", "c_a")


def em_B2_vector(c, e, tau):
    t = c.t
    ps = t["ps"]
    if B2_LEVEL < 3:
        return
    m4 = t["t1_buf"][0:1, 0, :]
    s4 = t["t1_buf"][0:1, 1, :]
    tmp = t["sqf"][0:1, 0, :]
    r5 = t["y_t"][0:1, 1, :]
    c.wt(e, f"pe_s4stats_{tau}_7")
    c.ss(e.tensor_copy(m4.rearrange("p (a n) -> p a n", a=2),
                       ps[0:1, 4:6, :]), e, "c_v")
    c.ss(e.tensor_copy(s4.rearrange("p (a n) -> p a n", a=2),
                       ps[0:1, 6:8, :]), e, "c_v")
    c.ss(e.tensor_mul(tmp, m4, m4), e, "c_v")
    c.ss(e.tensor_sub(s4, s4, tmp), e, "c_v")
    c.ss(e.tensor_single_scalar(tmp, s4, 0.0, OP.max), e, "c_v")
    ins = e.tensor_scalar_add(s4, tmp, EPS_LN)
    c.inc(ins, "c_v")
    c.mark(f"v_C1_{tau}", "c_v")
    c.wt(e, f"a_C_{tau}")
    r4 = t["y_t"][0:1, 0, :]
    ins = e.tensor_mul(r5, m4, r4)
    c.inc(ins, "c_v")
    c.mark(f"v_C2_{tau}", "c_v")
    c.wt(e, f"t_bc4_{tau}")
    e.tensor_copy(t["r3_b"][:].rearrange("p (a n) -> p a n", a=2),
                  ps[:, 4:6, :])
    ins = e.tensor_copy(t["mr3_b"][:].rearrange("p (a n) -> p a n", a=2),
                        ps[:, 6:8, :])
    c.inc(ins, "c_v")
    c.mark(f"v_bc4_{tau}", "c_v")
    if B2_LEVEL < 4:
        return
    for ch in range(NCH):
        c.wt_raw(e, "s_x1r", 16 * (8 * tau + ch + 1))
        if B2_LEVEL >= 6:
            c.wt_raw(e, "s_y", 16 * (8 * tau + ch - 1))
        e.tensor_mul(t["sqf"][:, 0, :], t["ffn_t"][:, ch, :], t["r3_b"][:])
        e.tensor_sub(t["sqf"][:, 1, :], t["sqf"][:, 0, :], t["mr3_b"][:])
        ins = e.tensor_add(t["y_t"][:, ch % 2, :], t["x1r"][:, ch % 2, :],
                           t["sqf"][:, 1, :])
        c.inc(ins, "c_v")
        c.mark(f"v_yc_{tau}_{ch}", "c_v")



def build_nc(dbg=False, n_blocks=6):
    nc = bass.Bass("TRN2", target_bir_lowering=False)
    tn = {}
    tn["x_h"] = nc.dram_tensor("x", [D, T], F32, kind="ExternalInput")
    tn["wgin_h"] = nc.dram_tensor("wgin", [8, 128, 128], F32, kind="ExternalInput")
    tn["wga_h"] = nc.dram_tensor("wga", [8, 128, 128], F32, kind="ExternalInput")
    tn["bcin_h"] = nc.dram_tensor("bcin", [128, 8], F32, kind="ExternalInput")
    tn["bca_h"] = nc.dram_tensor("bca", [128, 8], F32, kind="ExternalInput")
    tn["acol_h"] = nc.dram_tensor("acol", [128, 8], F32, kind="ExternalInput")
    tn["w1t_h"] = nc.dram_tensor("w1t", [1024, 4096], F32, kind="ExternalInput")
    tn["w2t_h"] = nc.dram_tensor("w2t", [1024, 4096], F32, kind="ExternalInput")
    tn["h1t_h"] = nc.dram_tensor("h1t", [4096, 4096], F32, kind="ExternalInput")
    tn["h2t_h"] = nc.dram_tensor("h2t", [4096, 4096], F32, kind="ExternalInput")
    tn["w3t_h"] = nc.dram_tensor("w3t", [4096, 1024], F32, kind="ExternalInput")
    tn["s1c_h"] = nc.dram_tensor("s1c", [128, 32], F32, kind="ExternalInput")
    tn["s2c_h"] = nc.dram_tensor("s2c", [128, 32], F32, kind="ExternalInput")
    tn["hs1c_h"] = nc.dram_tensor("hs1c", [128, 32], F32, kind="ExternalInput")
    tn["hs2c_h"] = nc.dram_tensor("hs2c", [128, 32], F32, kind="ExternalInput")
    tn["s3c_h"] = nc.dram_tensor("s3c", [128, 8], F32, kind="ExternalInput")
    tn["x1sp"] = nc.dram_tensor("x1sp", [D, T], F32,
                                kind="ExternalOutput" if dbg else "Internal")
    tn["ln2sp"] = nc.dram_tensor("ln2sp", [D, 1024], BF16,
                                 kind="ExternalOutput" if dbg else "Internal")
    tn["y_h"] = nc.dram_tensor("y", [D, T], F32, kind="ExternalOutput")
    if dbg:
        tn["dbg_h"] = nc.dram_tensor("dbg_h", [D, T], F32, kind="ExternalOutput")
    if dbg == 2:
        for nm in ("dbg_ln1", "dbg_a", "dbg_nrm", "dbg_cum", "dbg_gx", "dbg_mu"):
            tn[nm] = nc.dram_tensor(nm, [D, T], F32, kind="ExternalOutput")

    tn["wg_in"] = nc.alloc_sbuf_tensor("wg_in", [128, 8, 128], F32)
    tn["wg_a"] = nc.alloc_sbuf_tensor("wg_a", [128, 8, 128], F32)
    tn["bcol_in"] = nc.alloc_sbuf_tensor("bcol_in", [128, 8], F32)
    tn["bcol_a"] = nc.alloc_sbuf_tensor("bcol_a", [128, 8], F32)
    tn["acol"] = nc.alloc_sbuf_tensor("acol_sb", [128, 8], F32)
    tn["nspcol"] = nc.alloc_sbuf_tensor("nspcol", [128, 8], F32)
    tn["tiny"] = nc.alloc_sbuf_tensor("tiny", [128, 32], F32)
    tn["s1c"] = nc.alloc_sbuf_tensor("s1c_sb", [128, 32], F32)
    tn["s2c"] = nc.alloc_sbuf_tensor("s2c_sb", [128, 32], F32)
    tn["hs1c"] = nc.alloc_sbuf_tensor("hs1c_sb", [128, 32], F32)
    tn["hs2c"] = nc.alloc_sbuf_tensor("hs2c_sb", [128, 32], F32)
    tn["s3c"] = nc.alloc_sbuf_tensor("s3c_sb", [128, 8], F32)
    tn["ones_row"] = nc.alloc_sbuf_tensor("ones_row", [1, 128], F32)
    tn["cst_col"] = nc.alloc_sbuf_tensor("cst_col", [128, 1], F32)
    tn["wbuf"] = nc.alloc_sbuf_tensor("wbuf", [128, NBUF, 8, 256], BF16)
    tn["ln2_t"] = nc.alloc_sbuf_tensor("ln2_t", [128, 8, 1024], BF16)
    tn["ps"] = nc.alloc_psum_tensor("ps", [128, 8, 512], F32)

    semnames = ["s_c", "s_x", "s_x2", "s_x3", "s_w", "s_w2", "s_sp", "s_l2", "s_x1r", "s_y", "s_dbg",
                "s_d3v", "s_d3a", "c_v", "c_a", "c_p", "c_t", "c_w", "bar"]
    sems = {n: nc.alloc_semaphore(n) for n in semnames}

    ctx = Ctx()
    ctx.sems = sems
    ctx.dbg = dbg

    def emit_all(c):
        blocks = []
        blocks.append({
            "sync": lambda e: em_S0_sync(c, e),
            "vector": lambda e: em_S0_vector(c, e),
            "scalar": lambda e: em_S0_scalar(c, e),
            "gpsimd": lambda e: None,
            "tensor": lambda e: None,
        })
        blocks.append({
            "sync": lambda e: em_A_sync(c, e),
            "gpsimd": lambda e: em_A_pool(c, e, dbg),
            "scalar": lambda e: em_A_scalar(c, e),
            "vector": lambda e: em_A_vector(c, e),
            "tensor": lambda e: em_A_tensor(c, e),
        })
        for tau in (0, 1):
            blocks.append({
                "sync": lambda e, tau=tau: em_B1_sync(c, e, tau),
                "gpsimd": lambda e, tau=tau: em_B1_gpsimd(c, e, tau),
                "tensor": lambda e, tau=tau: em_B1_tensor(c, e, tau),
                "scalar": lambda e, tau=tau: em_B1_scalar(c, e, tau),
                "vector": lambda e, tau=tau: em_B1_vector(c, e, tau),
            })
            blocks.append({
                "sync": lambda e, tau=tau: em_B2_sync(c, e, tau),
                "gpsimd": lambda e, tau=tau: em_B2_gpsimd(c, e, tau),
                "tensor": lambda e, tau=tau: em_B2_tensor(c, e, tau),
                "scalar": lambda e, tau=tau: em_B2_scalar(c, e, tau),
                "vector": lambda e, tau=tau: em_B2_vector(c, e, tau),
            })
        return blocks

    mock = _MockEng()
    mockt = _MockTensor()
    ctx.t = {k: mockt for k in list(tn.keys()) + [
        "xc2", "pA0", "pB0", "pC0", "pD0", "pA1", "pB1", "pC1", "pD1",
        "T0", "T1", "T2", "T3", "T4", "T5",
        "r_b", "mr_b", "h_all", "l2c2", "ln2_t", "h1_t", "h2_t", "t1_buf",
        "ffn_t", "sqf", "x1r", "y_t", "r3_b", "mr3_b"]}
    for blk in emit_all(ctx)[:n_blocks]:
        for en in ("sync", "gpsimd", "scalar", "vector", "tensor"):
            blk[en](mock)
    missing = ctx.refs - set(ctx.M.keys())
    assert not missing, f"unresolved marks: {sorted(missing)}"

    ctx.counting = False
    ctx.t = tn
    ctx._ssn = 0
    blocks = emit_all(ctx)
    engs = ("sync", "gpsimd", "scalar", "vector", "tensor")

    with nc.Block() as block:
        def chain(blk):
            for en in engs:
                getattr(block, en)(lambda eng, en=en, blk=blk: blk[en](eng))

        chain(blocks[0])
        if n_blocks >= 2:
            sa = _ScopeA(nc, tn)
            sa.__enter__()
            chain(blocks[1])
            sa.__exit__(None, None, None)
        if n_blocks >= 3:
            _ScopeBOuter(nc, tn).__enter__()
            for tau in (0, 1):
                if n_blocks >= 2 + 2 * tau + 1:
                    b1 = _ScopeB1(nc, tn)
                    b1.__enter__()
                    chain(blocks[2 + 2 * tau])
                    b1.__exit__(None, None, None)
                if n_blocks >= 3 + 2 * tau + 1:
                    b2 = _ScopeB2(nc, tn)
                    b2.__enter__()
                    chain(blocks[3 + 2 * tau])
                    b2.__exit__(None, None, None)
    return nc


class _ScopeA:
    def __init__(self, nc, tn):
        self.nc, self.tn = nc, tn
        self.keys = []

    def __enter__(self):
        nc, tn = self.nc, self.tn
        self.cms = []
        def add(key, shape, dt):
            cm = nc.sbuf_tensor(key, shape, dt)
            tn[key] = cm.__enter__()
            self.cms.append(cm)
            self.keys.append(key)
        add("xc2", [128, 2, T], F32)
        for p in (0, 1):
            for nm in "ABCD":
                add(f"p{nm}{p}", [128, T], F32)
        add("r_b", [128, T], F32)
        add("mr_b", [128, T], F32)
        add("h_all", [128, 8, T], F32)
        add("l2c2", [128, 2, 1024], BF16)
        for alias, real in (("T0", "pB0"), ("T1", "pC0"), ("T2", "pD0"),
                            ("T3", "pB1"), ("T4", "pC1"), ("T5", "pD1")):
            tn[alias] = tn[real]
        return self

    def __exit__(self, *a):
        for cm in reversed(self.cms):
            cm.__exit__(None, None, None)
        return False


class _ScopeBOuter:

    def __init__(self, nc, tn):
        self.nc, self.tn = nc, tn

    def __enter__(self):
        nc, tn = self.nc, self.tn
        tn["h1_t"] = nc.alloc_sbuf_tensor("h1_t", [128, 32, 1024], BF16)
        tn["t1_buf"] = nc.alloc_sbuf_tensor("t1_buf", [128, 2, 1024], F32)
        return self

    def __exit__(self, *a):
        return False


class _ScopeB1:
    _n = [0]

    def __init__(self, nc, tn):
        self.nc, self.tn = nc, tn

    def __enter__(self):
        self._n[0] += 1
        self.cm = self.nc.sbuf_tensor(f"h2_t_{self._n[0]}", [128, 32, 1024], BF16)
        self.tn["h2_t"] = self.cm.__enter__()
        return self

    def __exit__(self, *a):
        self.cm.__exit__(None, None, None)
        return False


class _ScopeB2:
    _n = [0]

    def __init__(self, nc, tn):
        self.nc, self.tn = nc, tn

    def __enter__(self):
        nc, tn = self.nc, self.tn
        self._n[0] += 1
        self.cms = []
        for key, shape, dt in [("ffn_t", [128, 8, 1024], F32),
                               ("sqf", [128, 2, 1024], F32),
                               ("x1r", [128, 2, 1024], F32),
                               ("y_t", [128, 2, 1024], F32),
                               ("r3_b", [128, 1024], F32),
                               ("mr3_b", [128, 1024], F32)]:
            cm = nc.sbuf_tensor(f"{key}_{self._n[0]}", shape, dt)
            tn[key] = cm.__enter__()
            self.cms.append(cm)
        return self

    def __exit__(self, *a):
        for cm in reversed(self.cms):
            cm.__exit__(None, None, None)
        return False



_NC_CACHE = {}


def _get_nc(dbg=False):
    if dbg not in _NC_CACHE:
        _NC_CACHE[dbg] = build_nc(dbg)
    return _NC_CACHE[dbg]


def _layout_inputs(inputs):
    f = np.ascontiguousarray
    x = np.asarray(inputs["L0_attn_x"], dtype=np.float32)
    common = {
        "wgin": f(np.asarray(inputs["L0_attn_w_in"], np.float32)),
        "wga": f(np.asarray(inputs["L0_attn_w_a"], np.float32)),
        "bcin": f(np.asarray(inputs["L0_attn_b_in"], np.float32).T),
        "bca": f(np.asarray(inputs["L0_attn_b_a"], np.float32).T),
        "acol": f(np.asarray(inputs["L0_attn_a_param"], np.float32)
                  .reshape(8, 128).T),
        "w1t": f(np.asarray(inputs["L0_ffn_w1_w"], np.float32).T),
        "w2t": f(np.asarray(inputs["L0_ffn_w2_w"], np.float32).T),
        "h1t": f(np.asarray(inputs["L0_ffn_h1_w"], np.float32).T),
        "h2t": f(np.asarray(inputs["L0_ffn_h2_w"], np.float32).T),
        "w3t": f(np.asarray(inputs["L0_ffn_w3_w"], np.float32).T),
        "s1c": f(np.asarray(inputs["L0_ffn_w1_s"], np.float32)
                 .reshape(32, 128).T),
        "s2c": f(np.asarray(inputs["L0_ffn_w2_s"], np.float32)
                 .reshape(32, 128).T),
        "hs1c": f(np.asarray(inputs["L0_ffn_h1_s"], np.float32)
                  .reshape(32, 128).T),
        "hs2c": f(np.asarray(inputs["L0_ffn_h2_s"], np.float32)
                  .reshape(32, 128).T),
        "s3c": f(np.asarray(inputs["L0_ffn_w3_s"], np.float32)
                 .reshape(8, 128).T),
    }
    in_maps = []
    for b in range(8):
        m = dict(common)
        m["x"] = f(x[b].T)
        in_maps.append(m)
    return in_maps


def kernel(**inputs):
    nc = _get_nc(dbg=False)
    in_maps = _layout_inputs(inputs)
    res = run_bass_kernel_spmd(nc, in_maps, core_ids=list(range(8)))
    out = np.stack([np.asarray(r["y"]).T for r in res.results], axis=0)
    return np.ascontiguousarray(out, dtype=np.float32)


# version 22
# speedup vs baseline: 1.0288x; 1.0176x over previous
import numpy as np
from contextlib import ExitStack

import concourse.bass as bass
import concourse.mybir as mybir
from concourse.bass_utils import run_bass_kernel_spmd

F32 = mybir.dt.float32
BF16 = mybir.dt.bfloat16
AF = mybir.ActivationFunctionType
OP = mybir.AluOpType

D = 1024
T = 2048
NCH = 8
NBUF = 4
EPS_LN = 1e-5
MIN_DECAY = 1e-3
SCAN_EPS = 1e-7
CLAMP_A = 1.0 - 1e-5


def _weight_loads():
    WL = []
    for tau in (0, 1):
        for mg in range(16):
            for part in (0, 1):
                WL.append(("s1", tau, mg, part))
        for st in ("s2", "s3"):
            for mg in range(16):
                for kg in range(4):
                    WL.append((st, tau, mg, kg))
        for mg in range(4):
            for kg in range(4):
                WL.append(("s4", tau, mg, kg))
    G = {k: g for g, k in enumerate(WL)}
    return WL, G


WL, GIDX = _weight_loads()
NLOADS = len(WL)
QID = [0 for g in range(NLOADS)]
QORD = []
_cnt = [0, 0]
for _g in range(NLOADS):
    QORD.append(_cnt[QID[_g]])
    _cnt[QID[_g]] += 1
B2_LEVEL = 6
_L_A_END = NBUF
_L_B1_0_END = GIDX[("s4", 0, 0, 0)]
_L_B2_0_END = GIDX[("s1", 1, 0, 0)]
_L_B1_1_END = GIDX[("s4", 1, 0, 0)]
_L_B2_1_END = NLOADS


class _MockEng:
    def __getattr__(self, name):
        return lambda *a, **k: None


class _MockTensor:
    def __getitem__(self, *a):
        return self

    def rearrange(self, *a, **k):
        return self

    def ap(self):
        return self


class Ctx:

    def __init__(self):
        self.counting = True
        self.cnt = {}
        self.M = {}
        self.refs = set()
        self.sems = None
        self.t = None
        self._ssn = 0

    def inc(self, ins, sem, n=1):
        amt = 16 * n if sem.startswith("s_") else n
        if self.counting:
            self.cnt[sem] = self.cnt.get(sem, 0) + amt
        else:
            ins.then_inc(self.sems[sem], amt)

    def mark(self, name, sem):
        if self.counting:
            assert name not in self.M, f"duplicate mark {name}"
            self.M[name] = (sem, self.cnt.get(sem, 0))

    def wt(self, eng, name):
        if self.counting:
            self.refs.add(name)
            return
        semname, v = self.M[name]
        if v > 0:
            eng.wait_ge(self.sems[semname], v)

    def wt_raw(self, eng, sem, v):
        if not self.counting and v > 0:
            eng.wait_ge(self.sems[sem], v)

    def ss(self, ins, eng, sem):
        name = f"_ss_{sem}_{self._ssn}"
        self._ssn += 1
        self.inc(ins, sem)
        self.mark(name, sem)
        self.wt(eng, name)



N_CONST = 10
C1 = float(np.log1p(-1e-5))
LCL = float(np.log(1e-7))


def em_S0_sync(c, e):
    t = c.t
    for name, dst in [("wgin_h", "wg_in"), ("wga_h", "wg_a"),
                      ("bcin_h", "bcol_in"), ("bca_h", "bcol_a"),
                      ("acol_h", "acol"), ("s1c_h", "s1c"), ("s2c_h", "s2c"),
                      ("hs1c_h", "hs1c"), ("hs2c_h", "hs2c"), ("s3c_h", "s3c")]:
        if name in ("wgin_h", "wga_h"):
            src = t[name].ap().rearrange("h i j -> i h j")
        else:
            src = t[name].ap()
        c.inc(e.dma_start(t[dst][:], src), "s_c")


def em_S0_vector(c, e):
    t = c.t
    e.memset(t["ones_row"][:], 1.0)
    e.memset(t["ones_rowb"][:], 1.0)
    ins = e.memset(t["cst_col"][:], 1.0 / D)
    c.inc(ins, "c_v")
    c.mark("v_ms", "c_v")
    c.wt_raw(e, "s_c", 16 * N_CONST)
    tr = t["tiny"][:, 0:8]
    tb = t["tiny"][:, 8:16]
    tl = t["tiny"][:, 24:32]
    ins = e.tensor_single_scalar(tr, t["acol"][:], 0.0, OP.max)
    c.inc(ins, "c_v")
    c.mark("v_pro1", "c_v")
    c.wt(e, "a_pro1")
    c.ss(e.scalar_tensor_tensor(tb, tl, MIN_DECAY, tr, OP.add, OP.add), e, "c_v")
    ins = e.tensor_single_scalar(t["nspcol"][:], tb, -1.0, OP.mult)
    c.inc(ins, "c_v")
    c.mark("v_pro2", "c_v")


def em_S0_scalar(c, e):
    t = c.t
    tb = t["tiny"][:, 8:16]
    te = t["tiny"][:, 16:24]
    tl = t["tiny"][:, 24:32]
    c.wt_raw(e, "s_c", 16 * N_CONST)
    c.ss(e.activation(tb, t["acol"][:], AF.Abs), e, "c_a")
    c.ss(e.activation(te, tb, AF.Exp, scale=-1.0), e, "c_a")
    ins = e.activation(tl, te, AF.Ln, bias=1.0)
    c.inc(ins, "c_a")
    c.mark("a_pro1", "c_a")


def em_S0_gpsimd(c, e):
    pass


def em_S0_tensor(c, e):
    pass




def _pt(c, name, ch):
    return c.t[f"p{name}{ch % 2}"]


def _xload(c, e, ch, sem):
    src = c.t["x_h"].ap()[ch * 128:(ch + 1) * 128, :]
    c.inc(e.dma_start(c.t["xc2"][:, ch % 2, :], src), sem)


XQ = {0: ("s_x", 1), 3: ("s_x", 2), 6: ("s_x", 3),
      1: ("s_x2", 1), 4: ("s_x2", 2), 7: ("s_x2", 3),
      2: ("s_x3", 1), 5: ("s_x3", 2)}


def _wt_x1r(c, e, ch):
    sem, o = XQ[ch]
    c.wt_raw(e, sem, 16 * o)


def em_A_sync(c, e):
    for ch in (0, 3, 6):
        if ch >= 2:
            c.wt(e, f"t_A0st_{ch - 2}")
        _xload(c, e, ch, "s_x")
    for ch in range(8):
        c.wt(e, [f"t_A0st_{6 + ch}", f"p_xr_{ch - 2}"][ch >= 2])
        _xload(c, e, ch, "s_x")
    for ch in range(8):
        c.wt(e, [f"p_xr_{6 + ch}", f"v_x1_{ch - 2}"][ch >= 2])
        _xload(c, e, ch, "s_x")


def em_A_pool(c, e, dbg=False):
    t = c.t
    dbg2 = dbg == 2

    def dump(ch, name, tile, markname):
        if not dbg2:
            return
        c.wt(e, markname)
        c.inc(e.dma_start(t[name].ap()[ch * 128:(ch + 1) * 128, :],
                          _pt(c, tile, ch)[:]), "s_d3v")
        c.mark(f"d3_{name}_{ch}", "s_d3v")

    def gated(ch, hh):
        sl = slice(hh * 1024, (hh + 1) * 1024)
        if hh == 0:
            dump(ch, "dbg_ln1", "B", f"v_ln1s_{ch}")
            c.wt(e, f"a_gxh0_{ch}")
            c.wt(e, f"v_ln1s_{ch}")
        else:
            c.wt(e, f"a_gx_{ch}")
        A = _pt(c, "A", ch)
        ins = e.tensor_mul(A[:, sl], A[:, sl], _pt(c, "B", ch)[:, sl])
        c.inc(ins, "c_p")
        c.mark([f"p_g0_{ch}", f"p_gated_{ch}"][hh], "c_p")
        if hh == 1:
            dump(ch, "dbg_gx", "A", f"p_gated_{ch}")

    def xt(ch):
        dump(ch, "dbg_mu", "C", f"a_mult_{ch}")
        dump(ch, "dbg_nrm", "D", f"v_normed_{ch}")
        c.wt(e, f"v_normed_{ch}")
        ins = e.tensor_scalar(_pt(c, "C", ch)[:], _pt(c, "D", ch)[:],
                              SCAN_EPS, 1.0, OP.add, OP.mult)
        c.inc(ins, "c_p")
        c.mark(f"p_xt_{ch}", "c_p")

    def xr(ch):
        c.wt_raw(e, "s_x", 16 * (4 + ch))
        if ch < 2:
            c.wt(e, "v_A0bc")
        else:
            c.wt(e, f"v_normed_{ch - 2}")
        ins = e.tensor_mul(_pt(c, "A", ch)[:], t["xc2"][:, ch % 2, :],
                           t["r_b"][:])
        c.inc(ins, "c_p")
        c.mark(f"p_xr_{ch}", "c_p")

    c.wt(e, "t_A0st_0")
    _xload(c, e, 2, "s_x3")
    c.wt(e, "t_A0st_3")
    _xload(c, e, 5, "s_x3")
    xr(0)
    xr(1)
    for q in range(NCH // 2):
        c0, c1 = 2 * q, 2 * q + 1
        gated(c0, 0)
        gated(c1, 0)
        gated(c0, 1)
        gated(c1, 1)
        if dbg2:
            dump(c0, "dbg_a", "C", f"v_lac_{c0}")
            dump(c0, "dbg_cum", "B", f"v_lcum_{c0}")
            dump(c1, "dbg_a", "C", f"v_lac_{c1}")
            dump(c1, "dbg_cum", "B", f"v_lcum_{c1}")
        xt(c0)
        if c1 + 1 < NCH:
            xr(c1 + 1)
        xt(c1)
        if c1 + 2 < NCH:
            xr(c1 + 2)
    if dbg:
        for ch in range(NCH):
            c.wt(e, f"v_h_{ch}")
            c.inc(e.dma_start(t["dbg_h"].ap()[ch * 128:(ch + 1) * 128, :],
                              t["h_all"][:, ch, :]), "s_dbg")
    for ch in range(NCH):
        if ch == 0:
            c.wt(e, "v_A2bc")
        elif ch >= 2:
            c.wt(e, f"v_x1_{ch - 2}")
        ins = e.tensor_mul(_pt(c, "B", ch)[:], t["h_all"][:, ch, :],
                           t["r_b"][:])
        c.inc(ins, "c_p")
        c.mark(f"p_x1m_{ch}", "c_p")
    for ch in range(NCH):
        c.wt(e, f"v_x1_{ch}")
        c.inc(e.dma_start(t["x1sp"].ap()[ch * 128:(ch + 1) * 128, :],
                          t["h_all"][:, ch, :]), "s_sp")
    def spill_ln2(ch):
        c.wt(e, f"v_ln2_{ch}")
        c.inc(e.dma_start(t["ln2sp"].ap()[ch * 128:(ch + 1) * 128, :],
                          t["l2c2"][:, ch % 2, :]), "s_sp")

    for ch in range(NCH):
        if ch == 0:
            c.wt(e, "v_A3bc")
        elif ch >= 2:
            c.wt(e, f"v_ln2_{ch - 2}")
        ins = e.tensor_mul(_pt(c, "B", ch)[:], t["h_all"][:, ch, :],
                           t["r_b"][:])
        c.inc(ins, "c_p")
        c.mark(f"p_l2m_{ch}", "c_p")
        if ch == 0:
            for g in range(_L_A_END):
                _em_wload(c, e, g)
        if ch >= 1:
            spill_ln2(ch - 1)
    spill_ln2(7)


def _wsrc(c, g):
    t = c.t
    st, tau, mg, x = WL[g]
    if st == "s1":
        wname = "w1t_h" if x == 0 else "w2t_h"
        src = c.t[wname].ap()[0:1024, mg * 256:(mg + 1) * 256]
    elif st == "s2":
        src = t["h1t_h"].ap()[x * 1024:(x + 1) * 1024, mg * 256:(mg + 1) * 256]
    elif st == "s3":
        src = t["h2t_h"].ap()[x * 1024:(x + 1) * 1024, mg * 256:(mg + 1) * 256]
    else:
        src = t["w3t_h"].ap()[x * 1024:(x + 1) * 1024, mg * 256:(mg + 1) * 256]
    return src.rearrange("(a p) n -> p a n", p=128)


def _em_wload(c, e, g):
    if g >= NBUF:
        c.wt_raw(e, "c_w", g - (NBUF - 1))
    ins = e.dma_start(c.t["wbuf"][:, g % NBUF, :, :], _wsrc(c, g))
    c.inc(ins, "s_w2" if QID[g] else "s_w")


def _wt_w(c, e, g):
    c.wt_raw(e, "s_w2" if QID[g] else "s_w", 16 * (QORD[g] + 1))


def _rows(c, name, n2048=True):
    t = c.t
    return t[name][0:1, :]


def _em_rows_dve_1(c, e, pref):
    t = c.t
    r0, r1, r2, r3 = (t["T0"][0:1, :], t["T1"][0:1, :],
                      t["T2"][0:1, :], t["T3"][0:1, :])
    c.ss(e.tensor_copy(r0.rearrange("p (a n) -> p a n", a=4),
                       t["ps"][0:1, 0:4, :]), e, "c_v")
    c.ss(e.tensor_copy(r1.rearrange("p (a n) -> p a n", a=4),
                       t["ps"][0:1, 4:8, :]), e, "c_v")
    c.ss(e.tensor_mul(r2, r0, r0), e, "c_v")
    c.ss(e.tensor_sub(r3, r1, r2), e, "c_v")
    c.ss(e.tensor_single_scalar(r2, r3, 0.0, OP.max), e, "c_v")
    ins = e.tensor_scalar_add(r3, r2, EPS_LN)
    c.inc(ins, "c_v")
    c.mark(f"v_{pref}r1", "c_v")


def _em_rows_act(c, e, pref):
    t = c.t
    r2, r3, r4 = t["T2"][0:1, :], t["T3"][0:1, :], t["T4"][0:1, :]
    c.wt(e, f"v_{pref}r1")
    c.ss(e.activation(r2, r3, AF.Ln), e, "c_a")
    ins = e.activation(r4, r2, AF.Exp, scale=-0.5)
    c.inc(ins, "c_a")
    c.mark(f"a_{pref}r", "c_a")


def _em_rows_dve_2(c, e, pref):
    t = c.t
    r0, r4, r5 = t["T0"][0:1, :], t["T4"][0:1, :], t["T5"][0:1, :]
    c.wt(e, f"a_{pref}r")
    ins = e.tensor_mul(r5, r0, r4)
    c.inc(ins, "c_v")
    c.mark(f"v_{pref}r2", "c_v")


def _em_bcast_pe(c, e, pref):
    t = c.t
    c.wt(e, f"v_{pref}r2")
    r4, r5 = t["T4"][0:1, :], t["T5"][0:1, :]
    ins = None
    for n in range(4):
        ins = e.matmul(t["ps"][:, n, :], t["ones_row"][:],
                       r4[0:1, n * 512:(n + 1) * 512], start=True, stop=True)
    for n in range(4):
        ins = e.matmul(t["ps"][:, 4 + n, :], t["ones_row"][:],
                       r5[0:1, n * 512:(n + 1) * 512], start=True, stop=True)
    c.inc(ins, "c_t")
    c.mark(f"t_{pref}bc", "c_t")


def _em_bcast_evac_dve(c, e, pref):
    t = c.t
    c.wt(e, f"t_{pref}bc")
    e.tensor_copy(t["r_b"][:].rearrange("p (a n) -> p a n", a=4),
                  t["ps"][:, 0:4, :])
    ins = e.tensor_copy(t["mr_b"][:].rearrange("p (a n) -> p a n", a=4),
                        t["ps"][:, 4:8, :])
    c.inc(ins, "c_v")
    c.mark(f"v_{pref}bc", "c_v")


def em_A_scalar(c, e):
    t = c.t
    ps = t["ps"]
    for ch in range(NCH):
        if ch in (0, 3, 6):
            och = ch + 1
            if och >= 4:
                c.wt(e, f"t_A0st_{och - 2}")
            _xload(c, e, och, "s_x2")
        _wt_x1r(c, e, ch)
        if ch >= 2:
            c.wt(e, f"t_A0st_{ch - 2}")
        ins = e.activation(_pt(c, "A", ch)[:], t["xc2"][:, ch % 2, :],
                           AF.Square)
        c.inc(ins, "c_a")
        c.mark(f"a_A0sq_{ch}", "c_a")
    _em_rows_act(c, e, "A0")
    def sig(ch, hh, which):
        bk = 4 * (ch % 2) + (0 if which == "gx" else 2)
        dst = _pt(c, "A" if which == "gx" else "C", ch)
        sl = slice(hh * 1024, (hh + 1) * 1024)
        if which == "gx":
            c.wt(e, f"t_g{hh}_{ch}")
        elif hh == 0 and ch >= 2:
            c.wt(e, f"v_h_{ch - 2}")
        col = t["bcol_in" if which == "gx" else "bcol_a"][:, ch:ch + 1]
        ins = e.activation(dst[:, sl].rearrange("p (a n) -> p a n", a=2),
                           ps[:, bk:bk + 2, :], AF.Sigmoid, bias=col)
        c.inc(ins, "c_a")
        mk = {("gx", 0): f"a_gxh0_{ch}", ("ga", 0): f"a_gah0_{ch}",
              ("gx", 1): f"a_gx_{ch}", ("ga", 1): f"a_ga_{ch}"}[(which, hh)]
        c.mark(mk, "c_a")

    for q in range(NCH // 2):
        pair = (2 * q, 2 * q + 1)
        for ch in pair:
            sig(ch, 0, "gx")
        for ch in pair:
            sig(ch, 0, "ga")
        for ch in pair:
            sig(ch, 1, "gx")
        for ch in pair:
            sig(ch, 1, "ga")
        for ch in pair:
            c.wt(e, f"v_lac_{ch}")
            if ch >= 2:
                c.wt(e, f"p_xt_{ch - 2}")
            ins = e.activation(_pt(c, "D", ch)[:], _pt(c, "C", ch)[:],
                               AF.Exp, scale=2.0)
            c.inc(ins, "c_a")
            c.mark(f"a_aa_{ch}", "c_a")
        for ch in pair:
            c.wt(e, f"v_lcum_{ch}")
            if c.dbg == 2:
                c.wt(e, f"d3_dbg_a_{ch}")
            ins = e.activation(_pt(c, "C", ch)[:], _pt(c, "D", ch)[:],
                               AF.Sqrt, scale=-1.0, bias=1.0)
            c.inc(ins, "c_a")
            c.mark(f"a_mult_{ch}", "c_a")
        for ch in pair:
            B, Dd = _pt(c, "B", ch), _pt(c, "D", ch)
            c.wt(e, f"v_m1_{ch}")
            e.activation(B[:, 0:1], Dd[:, 0:1], AF.Copy, scale=0.0)
            ins = e.activation(B[:, 1:2], Dd[:, 0:1], AF.Copy)
            c.inc(ins, "c_a")
            c.mark(f"a_ed_{ch}", "c_a")
            c.wt(e, f"v_d_{ch}")
            ins = e.activation(t["h_all"][:, ch, :], B[:], AF.Exp)
            c.inc(ins, "c_a")
            c.mark(f"a_at_{ch}", "c_a")
    for ch in range(NCH):
        c.wt(e, f"v_h_{ch}")
        if ch >= 2:
            c.wt(e, f"t_A2st_{ch - 2}")
        else:
            c.wt(e, f"v_h_{6 + ch}")
        ins = e.activation(_pt(c, "A", ch)[:], t["h_all"][:, ch, :],
                           AF.Square)
        c.inc(ins, "c_a")
        c.mark(f"a_hsq_{ch}", "c_a")
    _em_rows_act(c, e, "A2")
    for ch in range(NCH):
        c.wt(e, f"v_x1_{ch}")
        if ch >= 2:
            c.wt(e, f"t_lnst_{ch - 2}")
        else:
            c.wt(e, f"t_A2st_{6 + ch}")
        ins = e.activation(_pt(c, "A", ch)[:], t["h_all"][:, ch, :],
                           AF.Square)
        c.inc(ins, "c_a")
        c.mark(f"a_x1sq_{ch}", "c_a")
    _em_rows_act(c, e, "A3")


def em_A_vector(c, e):
    t = c.t
    c.wt(e, "t_A0st_7")
    _em_rows_dve_1(c, e, "A0")
    _em_rows_dve_2(c, e, "A0")
    _em_bcast_evac_dve(c, e, "A0")
    def ln1s(ch):
        A, B = _pt(c, "A", ch), _pt(c, "B", ch)
        c.wt(e, f"p_xr_{ch}")
        if ch >= 2:
            c.wt(e, f"a_at_{ch - 2}")
        ins = e.tensor_sub(B[:], A[:], t["mr_b"][:])
        c.inc(ins, "c_v")
        c.mark(f"v_ln1s_{ch}", "c_v")

    def lac(ch):
        Cc = _pt(c, "C", ch)
        c.wt(e, f"a_ga_{ch}")
        ins = e.tensor_scalar(Cc[:], Cc[:],
                              t["nspcol"][:, ch:ch + 1], C1, OP.mult, OP.min)
        c.inc(ins, "c_v")
        c.mark(f"v_lac_{ch}", "c_v")

    def lcum(ch):
        B, C = _pt(c, "B", ch), _pt(c, "C", ch)
        c.wt(e, f"t_g1_{ch}")
        c.wt(e, f"p_gated_{ch}")
        if c.dbg == 2:
            c.wt(e, f"d3_dbg_ln1_{ch}")
        ins = e.tensor_tensor_scan(B[:], C[:], C[:], 0.0, OP.add, OP.bypass)
        c.inc(ins, "c_v")
        c.mark(f"v_lcum_{ch}", "c_v")

    def m1(ch):
        c.wt(e, f"a_mult_{ch}")
        ins = e.tensor_scalar(_pt(c, "D", ch)[:], _pt(c, "B", ch)[:],
                              0.0, LCL, OP.min, OP.max)
        c.inc(ins, "c_v")
        c.mark(f"v_m1_{ch}", "c_v")

    def d_normed(ch):
        A, B, C, D = (_pt(c, "A", ch), _pt(c, "B", ch),
                      _pt(c, "C", ch), _pt(c, "D", ch))
        if c.dbg == 2:
            c.wt(e, f"d3_dbg_cum_{ch}")
        ins = e.tensor_sub(B[:, 2:T], D[:, 1:T - 1], D[:, 0:T - 2])
        c.inc(ins, "c_v")
        c.mark(f"v_d_{ch}", "c_v")
        c.wt(e, f"a_ed_{ch}")
        ins = e.tensor_mul(D[:], A[:], C[:])
        c.inc(ins, "c_v")
        c.mark(f"v_normed_{ch}", "c_v")

    ln1s(0)
    ln1s(1)
    for q in range(NCH // 2):
        c0, c1 = 2 * q, 2 * q + 1
        lac(c0)
        lac(c1)
        lcum(c0)
        lcum(c1)
        m1(c0)
        m1(c1)
        d_normed(c0)
        d_normed(c1)
        if c1 + 2 < NCH:
            ln1s(c1 + 1)
            ln1s(c1 + 2)
        _em_hscan(c, e, c0)
        _em_hscan(c, e, c1)
    c.wt(e, "t_A2st_7")
    _em_rows_dve_1(c, e, "A2")
    _em_rows_dve_2(c, e, "A2")
    _em_bcast_evac_dve(c, e, "A2")
    for ch in range(NCH):
        p = ch % 2
        c.wt_raw(e, "s_x", 16 * (12 + ch))
        if getattr(c, "dbg", False):
            c.wt_raw(e, "s_dbg", 16 * (ch + 1))
        B, Cc = _pt(c, "B", ch), _pt(c, "C", ch)
        c.wt(e, f"p_x1m_{ch}")
        e.tensor_sub(Cc[:], B[:], t["mr_b"][:])
        ins = e.tensor_add(t["h_all"][:, ch, :], t["xc2"][:, p, :], Cc[:])
        c.inc(ins, "c_v")
        c.mark(f"v_x1_{ch}", "c_v")
    c.wt(e, "t_lnst_7")
    _em_rows_dve_1(c, e, "A3")
    _em_rows_dve_2(c, e, "A3")
    _em_bcast_evac_dve(c, e, "A3")
    for ch in range(NCH):
        if ch >= 2:
            c.wt_raw(e, "s_sp", 16 * (7 + ch))
        B = _pt(c, "B", ch)
        c.wt(e, f"p_l2m_{ch}")
        e.tensor_sub(t["ln2_t"][:, ch, :], B[:, 0:1024], t["mr_b"][:, 0:1024])
        ins = e.tensor_sub(t["l2c2"][:, ch % 2, :], B[:, 1024:T],
                           t["mr_b"][:, 1024:T])
        c.inc(ins, "c_v")
        c.mark(f"v_ln2_{ch}", "c_v")


def _em_hscan(c, e, ch):
    c.wt(e, f"a_at_{ch}")
    c.wt(e, f"p_xt_{ch}")
    h = c.t["h_all"][:, ch, :]
    ins = e.tensor_tensor_scan(h, h, _pt(c, "C", ch)[:], 0.0, OP.mult, OP.add)
    c.inc(ins, "c_v")
    c.mark(f"v_h_{ch}", "c_v")


def em_A_tensor(c, e):
    t = c.t
    ps = t["ps"]

    def stats8(ch, markpref, rhs_x):
        ins = None
        for n in range(4):
            ins = e.matmul(ps[0:1, n, :], t["cst_col"][:],
                           rhs_x[:, n * 512:(n + 1) * 512],
                           start=(ch == 0), stop=(ch == 7),
                           skip_group_check=True)
        sq = _pt(c, "A", ch)
        for n in range(4):
            ins = e.matmul(ps[0:1, 4 + n, :], t["cst_col"][:],
                           sq[:, n * 512:(n + 1) * 512],
                           start=(ch == 0), stop=(ch == 7),
                           skip_group_check=True)
        c.inc(ins, "c_t")
        c.mark(f"{markpref}_{ch}", "c_t")

    c.wt(e, "v_ms")
    for ch in range(NCH):
        _wt_x1r(c, e, ch)
        c.wt(e, f"a_A0sq_{ch}")
        stats8(ch, "t_A0st", t["xc2"][:, ch % 2, :])
    _em_bcast_pe(c, e, "A0")
    def gates_half(ch, hh):
        bk = 4 * (ch % 2)
        B = _pt(c, "B", ch)
        if hh == 0:
            c.wt(e, f"v_ln1s_{ch}")
            if ch >= 2:
                c.wt(e, f"a_ga_{ch - 2}")
            else:
                c.wt(e, "v_A0bc")
        else:
            c.wt(e, f"a_gah0_{ch}")
        ins = None
        for n in range(2):
            nn = 2 * hh + n
            ins = e.matmul(ps[:, bk + n, :], t["wg_in"][:, ch, :],
                           B[:, nn * 512:(nn + 1) * 512],
                           start=True, stop=True)
        for n in range(2):
            nn = 2 * hh + n
            ins = e.matmul(ps[:, bk + 2 + n, :], t["wg_a"][:, ch, :],
                           B[:, nn * 512:(nn + 1) * 512],
                           start=True, stop=True)
        c.inc(ins, "c_t")
        c.mark(f"t_g{hh}_{ch}", "c_t")

    for q in range(NCH // 2):
        for hh in (0, 1):
            gates_half(2 * q, hh)
            gates_half(2 * q + 1, hh)
    for ch in range(NCH):
        c.wt(e, f"a_hsq_{ch}")
        if ch == 0:
            c.wt(e, "a_ga_7")
        stats8(ch, "t_A2st", t["h_all"][:, ch, :])
    _em_bcast_pe(c, e, "A2")
    for ch in range(NCH):
        c.wt(e, f"a_x1sq_{ch}")
        if ch == 0:
            c.wt(e, "v_A2bc")
        stats8(ch, "t_lnst", t["h_all"][:, ch, :])
    _em_bcast_pe(c, e, "A3")



def em_B1_sync(c, e, tau):
    em_B1_sync_wloads(c, e, tau)


def em_B1_gpsimd(c, e, tau):
    lo = _L_A_END if tau == 0 else _L_B2_0_END
    hi = _L_B1_0_END if tau == 0 else _L_B1_1_END
    for g in range(lo, hi):
        if QID[g] == 0:
            _em_wload(c, e, g)


def em_B1_sync_wloads(c, e, tau):
    lo = _L_A_END if tau == 0 else _L_B2_0_END
    hi = _L_B1_0_END if tau == 0 else _L_B1_1_END
    for g in range(lo, hi):
        if QID[g] == 1:
            _em_wload(c, e, g)


def em_B1_tensor(c, e, tau):
    t = c.t
    ps = t["ps"]
    if tau == 0:
        c.wt(e, "v_A3bc")
    else:
        c.wt_raw(e, "s_l2", 16)
        c.wt(e, "v_bc4_0")
        c.wt(e, "ev_s4_0_7")
    for mg in range(16):
        for part in (0, 1):
            g = GIDX[("s1", tau, mg, part)]
            if mg >= 1:
                c.wt(e, [f"ev_s1a_{tau}_{mg - 1}", f"ev_s1b_{tau}_{mg - 1}"][part])
            _wt_w(c, e, g)
            ins = None
            for kk in range(8):
                if tau == 0 and mg == 0 and part == 0:
                    c.wt(e, f"v_ln2_{kk}")
                for mi in range(2):
                    lhsT = t["wbuf"][:, g % NBUF, kk, mi * 128:(mi + 1) * 128]
                    for n2 in range(2):
                        ins = e.matmul(ps[:, 4 * part + 2 * mi + n2, :], lhsT,
                                       t["ln2_t"][:, kk, n2 * 512:(n2 + 1) * 512],
                                       start=(kk == 0), stop=(kk == 7))
            c.inc(ins, "c_w")
            c.mark(f"pe_s1{'ab'[part]}_{tau}_{mg}", "c_w")
    for st, rhs in (("s2", "h1_t"), ("s3", "h2_t")):
        prev_ev = f"ev_s1b_{tau}_15" if st == "s2" else f"ev_s2_{tau}_15"
        c.wt(e, prev_ev)
        for mg in range(16):
            se = mg % 2
            if mg >= 2:
                c.wt(e, f"ev_{st}_{tau}_{mg - 2}")
            for kg in range(4):
                g = GIDX[(st, tau, mg, kg)]
                _wt_w(c, e, g)
                ins = None
                for kk in range(8):
                    k = kg * 8 + kk
                    for mi in range(2):
                        lhsT = t["wbuf"][:, g % NBUF, kk, mi * 128:(mi + 1) * 128]
                        for n2 in range(2):
                            ins = e.matmul(
                                ps[:, 4 * se + 2 * mi + n2, :], lhsT,
                                t[rhs][:, k, n2 * 512:(n2 + 1) * 512],
                                start=(k == 0), stop=(k == 31))
                c.inc(ins, "c_w")
            c.mark(f"pe_{st}_{tau}_{mg}", "c_w")


def em_B1_scalar(c, e, tau):
    t = c.t
    ps = t["ps"]
    if tau == 0:
        c.wt_raw(e, "s_sp", 16 * 16)
    else:
        c.wt(e, "v_C2_0")
    for mg in range(16):
        c.wt(e, f"pe_s1a_{tau}_{mg}")
        if mg >= 1:
            c.wt(e, f"ev_s1b_{tau}_{mg - 1}")
        ins = None
        for mi in range(2):
            m = 2 * mg + mi
            for n2 in range(2):
                ins = e.activation(t["t1_buf"][:, mi, n2 * 512:(n2 + 1) * 512],
                                   ps[:, 2 * mi + n2, :], AF.Silu,
                                   scale=t["s1c"][:, m:m + 1])
        c.inc(ins, "c_a")
        c.mark(f"ev_s1a_{tau}_{mg}", "c_a")
    for st, dst, scol in (("s2", "h2_t", "hs1c"), ("s3", "h1_t", "hs2c")):
        for mg in range(16):
            se = mg % 2
            c.wt(e, f"pe_{st}_{tau}_{mg}")
            if tau == 0 and st == "s2" and mg == 0:
                c.wt_raw(e, "s_sp", 16 * 16)
                src2 = t["ln2sp"].ap().rearrange("(a p) n -> p a n", p=128)
                c.inc(e.dma_start(t["ln2_t"][:], src2), "s_l2")
            if tau == 1 and st == "s2" and mg == 0:
                c.wt(e, "v_yc_0_7")
                c.wt_raw(e, "s_y", 16 * 8)
            ins = None
            for mi in range(2):
                m = 2 * mg + mi
                for n2 in range(2):
                    ins = e.activation(t[dst][:, m, n2 * 512:(n2 + 1) * 512],
                                       ps[:, 4 * se + 2 * mi + n2, :], AF.Gelu,
                                       scale=t[scol][:, m:m + 1])
            c.inc(ins, "c_a")
            c.mark(f"ev_{st}_{tau}_{mg}", "c_a")


def em_B1_vector(c, e, tau):
    t = c.t
    ps = t["ps"]
    if tau == 0:
        c.wt_raw(e, "s_sp", 16 * 16)
    else:
        c.wt(e, "pe_s4_0_7")
    for mg in range(16):
        c.wt(e, f"pe_s1b_{tau}_{mg}")
        c.wt(e, f"ev_s1a_{tau}_{mg}")
        ins = None
        for mi in range(2):
            m = 2 * mg + mi
            for n2 in range(2):
                ins = e.scalar_tensor_tensor(
                    t["h1_t"][:, m, n2 * 512:(n2 + 1) * 512],
                    ps[:, 4 + 2 * mi + n2, :], t["s2c"][:, m:m + 1],
                    t["t1_buf"][:, mi, n2 * 512:(n2 + 1) * 512],
                    OP.mult, OP.mult)
        c.inc(ins, "c_v")
        c.mark(f"ev_s1b_{tau}_{mg}", "c_v")



def em_B2_sync(c, e, tau):
    t = c.t
    em_B2_sync_wloads(c, e, tau)
    c.wt_raw(e, "s_sp", 16 * 8)
    if B2_LEVEL < 4:
        return
    c.wt(e, f"pe_s3_{tau}_15")
    for ch in range(NCH):
        if ch >= 2 and B2_LEVEL >= 6:
            c.wt(e, f"v_yc_{tau}_{ch - 2}")
            c.inc(e.dma_start(
                t["y_h"].ap()[(ch - 2) * 128:(ch - 1) * 128,
                              tau * 1024:(tau + 1) * 1024],
                t["y_t"][:, ch % 2, :]), "s_y")
        if tau == 1 and ch < 2:
            c.wt(e, f"v_yc_0_{6 + ch}")
        src = t["x1sp"].ap()[ch * 128:(ch + 1) * 128,
                             tau * 1024:(tau + 1) * 1024]
        c.inc(e.dma_start(t["x1r"][:, ch % 2, :], src), "s_x1r")
    if B2_LEVEL >= 6:
        for ch in (6, 7):
            c.wt(e, f"v_yc_{tau}_{ch}")
            c.inc(e.dma_start(
                t["y_h"].ap()[ch * 128:(ch + 1) * 128,
                              tau * 1024:(tau + 1) * 1024],
                t["y_t"][:, ch % 2, :]), "s_y")
        if tau == 1:
            c.wt_raw(e, "s_y", 16 * 16)


def em_B2_gpsimd(c, e, tau):
    lo = _L_B1_0_END if tau == 0 else _L_B1_1_END
    hi = _L_B2_0_END if tau == 0 else _L_B2_1_END
    for g in range(lo, hi):
        if QID[g] == 0:
            _em_wload(c, e, g)


def em_B2_sync_wloads(c, e, tau):
    lo = _L_B1_0_END if tau == 0 else _L_B1_1_END
    hi = _L_B2_0_END if tau == 0 else _L_B2_1_END
    for g in range(lo, hi):
        if QID[g] == 1:
            _em_wload(c, e, g)


def em_B2_tensor(c, e, tau):
    t = c.t
    ps = t["ps"]
    c.wt(e, f"ev_s3_{tau}_15")

    def stats(m):
        c.wt(e, f"ev_s4sq_{tau}_{m}")
        ins = None
        for n2 in range(2):
            ins = e.matmul(ps[0:1, 4 + n2, :], t["cst_col"][:],
                           t["ffn_t"][:, m, n2 * 512:(n2 + 1) * 512],
                           start=(m == 0), stop=(m == 7),
                           skip_group_check=True)
            ins = e.matmul(ps[0:1, 6 + n2, :], t["cst_col"][:],
                           t["sqf"][:, m % 2, n2 * 512:(n2 + 1) * 512],
                           start=(m == 0), stop=(m == 7),
                           skip_group_check=True)
        c.inc(ins, "c_t")
        c.mark(f"pe_s4stats_{tau}_{m}", "c_t")

    for m in range(8):
        if m >= 2:
            c.wt(e, f"ev_s4_{tau}_{m - 2}")
        for kg in range(4):
            g = GIDX[("s4", tau, m // 2, kg)]
            _wt_w(c, e, g)
            ins = None
            for kk in range(8):
                k = kg * 8 + kk
                lhsT = t["wbuf"][:, g % NBUF, kk,
                                 (m % 2) * 128:(m % 2 + 1) * 128]
                for n2 in range(2):
                    ins = e.matmul(ps[:, 2 * (m % 2) + n2, :], lhsT,
                                   t["h1_t"][:, k, n2 * 512:(n2 + 1) * 512],
                                   start=(k == 0), stop=(k == 31))
            if m % 2 == 1:
                c.inc(ins, "c_w")
            elif kg == 3:
                c.inc(ins, "c_t")
        c.mark(f"pe_s4_{tau}_{m}", "c_w" if m % 2 == 1 else "c_t")
        if m >= 1 and B2_LEVEL >= 2:
            stats(m - 1)
    if B2_LEVEL >= 2:
        stats(7)
    if B2_LEVEL < 3:
        return
    c.wt(e, f"v_C2_{tau}")
    r4 = t["rowb"][0:1, 0, :]
    r5 = t["rowb"][0:1, 1, :]
    ins = None
    for n2 in range(2):
        ins = e.matmul(ps[:, 4 + n2, :], t["ones_rowb"][:],
                       r4[0:1, n2 * 512:(n2 + 1) * 512], start=True, stop=True)
    for n2 in range(2):
        ins = e.matmul(ps[:, 6 + n2, :], t["ones_rowb"][:],
                       r5[0:1, n2 * 512:(n2 + 1) * 512], start=True, stop=True)
    c.inc(ins, "c_t")
    c.mark(f"t_bc4_{tau}", "c_t")


def em_B2_scalar(c, e, tau):
    t = c.t
    ps = t["ps"]
    if tau == 1:
        c.wt(e, "v_yc_0_7")
    for m in range(8):
        c.wt(e, f"pe_s4_{tau}_{m}")
        if m >= 2 and B2_LEVEL >= 2:
            c.wt(e, f"pe_s4stats_{tau}_{m - 2}")
        ins = None
        for n2 in range(2):
            ins = e.activation(t["ffn_t"][:, m, n2 * 512:(n2 + 1) * 512],
                               ps[:, 2 * (m % 2) + n2, :], AF.Copy,
                               scale=t["s3c"][:, m:m + 1])
        c.inc(ins, "c_a")
        c.mark(f"ev_s4_{tau}_{m}", "c_a")
        for n2 in range(2):
            ins = e.activation(t["sqf"][:, m % 2, n2 * 512:(n2 + 1) * 512],
                               t["ffn_t"][:, m, n2 * 512:(n2 + 1) * 512],
                               AF.Square)
        c.inc(ins, "c_a")
        c.mark(f"ev_s4sq_{tau}_{m}", "c_a")
    if B2_LEVEL < 3:
        return
    c.wt(e, f"v_C1_{tau}")
    tmp2 = t["sqf"][0:1, 1, :]
    r4 = t["y_t"][0:1, 0, :]
    s4 = t["t1_buf"][0:1, 1, :]
    c.ss(e.activation(tmp2, s4, AF.Ln), e, "c_a")
    c.ss(e.activation(r4, tmp2, AF.Exp, scale=-0.5), e, "c_a")
    ins = e.activation(t["rowb"][0:1, 0, :], r4, AF.Copy)
    c.inc(ins, "c_a")
    c.mark(f"a_C_{tau}", "c_a")


def em_B2_vector(c, e, tau):
    t = c.t
    ps = t["ps"]
    if B2_LEVEL < 3:
        return
    m4 = t["t1_buf"][0:1, 0, :]
    s4 = t["t1_buf"][0:1, 1, :]
    tmp = t["sqf"][0:1, 0, :]
    r5 = t["y_t"][0:1, 1, :]
    c.wt(e, f"pe_s4stats_{tau}_7")
    c.ss(e.tensor_copy(m4.rearrange("p (a n) -> p a n", a=2),
                       ps[0:1, 4:6, :]), e, "c_v")
    c.ss(e.tensor_copy(s4.rearrange("p (a n) -> p a n", a=2),
                       ps[0:1, 6:8, :]), e, "c_v")
    c.ss(e.tensor_mul(tmp, m4, m4), e, "c_v")
    c.ss(e.tensor_sub(s4, s4, tmp), e, "c_v")
    c.ss(e.tensor_single_scalar(tmp, s4, 0.0, OP.max), e, "c_v")
    ins = e.tensor_scalar_add(s4, tmp, EPS_LN)
    c.inc(ins, "c_v")
    c.mark(f"v_C1_{tau}", "c_v")
    c.wt(e, f"a_C_{tau}")
    r4 = t["y_t"][0:1, 0, :]
    ins = e.tensor_mul(t["rowb"][0:1, 1, :], m4, r4)
    c.inc(ins, "c_v")
    c.mark(f"v_C2_{tau}", "c_v")
    c.wt(e, f"t_bc4_{tau}")
    e.tensor_copy(t["r3_b"][:].rearrange("p (a n) -> p a n", a=2),
                  ps[:, 4:6, :])
    ins = e.tensor_copy(t["mr3_b"][:].rearrange("p (a n) -> p a n", a=2),
                        ps[:, 6:8, :])
    c.inc(ins, "c_v")
    c.mark(f"v_bc4_{tau}", "c_v")
    if B2_LEVEL < 4:
        return
    for ch in range(NCH):
        c.wt_raw(e, "s_x1r", 16 * (8 * tau + ch + 1))
        if B2_LEVEL >= 6:
            c.wt_raw(e, "s_y", 16 * (8 * tau + ch - 1))
        e.tensor_mul(t["sqf"][:, 0, :], t["ffn_t"][:, ch, :], t["r3_b"][:])
        e.tensor_sub(t["sqf"][:, 1, :], t["sqf"][:, 0, :], t["mr3_b"][:])
        ins = e.tensor_add(t["y_t"][:, ch % 2, :], t["x1r"][:, ch % 2, :],
                           t["sqf"][:, 1, :])
        c.inc(ins, "c_v")
        c.mark(f"v_yc_{tau}_{ch}", "c_v")



def build_nc(dbg=False, n_blocks=6):
    nc = bass.Bass("TRN2", target_bir_lowering=False)
    tn = {}
    tn["x_h"] = nc.dram_tensor("x", [D, T], F32, kind="ExternalInput")
    tn["wgin_h"] = nc.dram_tensor("wgin", [8, 128, 128], F32, kind="ExternalInput")
    tn["wga_h"] = nc.dram_tensor("wga", [8, 128, 128], F32, kind="ExternalInput")
    tn["bcin_h"] = nc.dram_tensor("bcin", [128, 8], F32, kind="ExternalInput")
    tn["bca_h"] = nc.dram_tensor("bca", [128, 8], F32, kind="ExternalInput")
    tn["acol_h"] = nc.dram_tensor("acol", [128, 8], F32, kind="ExternalInput")
    tn["w1t_h"] = nc.dram_tensor("w1t", [1024, 4096], F32, kind="ExternalInput")
    tn["w2t_h"] = nc.dram_tensor("w2t", [1024, 4096], F32, kind="ExternalInput")
    tn["h1t_h"] = nc.dram_tensor("h1t", [4096, 4096], F32, kind="ExternalInput")
    tn["h2t_h"] = nc.dram_tensor("h2t", [4096, 4096], F32, kind="ExternalInput")
    tn["w3t_h"] = nc.dram_tensor("w3t", [4096, 1024], F32, kind="ExternalInput")
    tn["s1c_h"] = nc.dram_tensor("s1c", [128, 32], F32, kind="ExternalInput")
    tn["s2c_h"] = nc.dram_tensor("s2c", [128, 32], F32, kind="ExternalInput")
    tn["hs1c_h"] = nc.dram_tensor("hs1c", [128, 32], F32, kind="ExternalInput")
    tn["hs2c_h"] = nc.dram_tensor("hs2c", [128, 32], F32, kind="ExternalInput")
    tn["s3c_h"] = nc.dram_tensor("s3c", [128, 8], F32, kind="ExternalInput")
    tn["x1sp"] = nc.dram_tensor("x1sp", [D, T], F32,
                                kind="ExternalOutput" if dbg else "Internal")
    tn["ln2sp"] = nc.dram_tensor("ln2sp", [D, 1024], BF16,
                                 kind="ExternalOutput" if dbg else "Internal")
    tn["y_h"] = nc.dram_tensor("y", [D, T], F32, kind="ExternalOutput")
    if dbg:
        tn["dbg_h"] = nc.dram_tensor("dbg_h", [D, T], F32, kind="ExternalOutput")
    if dbg == 2:
        for nm in ("dbg_ln1", "dbg_a", "dbg_nrm", "dbg_cum", "dbg_gx", "dbg_mu"):
            tn[nm] = nc.dram_tensor(nm, [D, T], F32, kind="ExternalOutput")

    tn["wg_in"] = nc.alloc_sbuf_tensor("wg_in", [128, 8, 128], F32)
    tn["wg_a"] = nc.alloc_sbuf_tensor("wg_a", [128, 8, 128], F32)
    tn["bcol_in"] = nc.alloc_sbuf_tensor("bcol_in", [128, 8], F32)
    tn["bcol_a"] = nc.alloc_sbuf_tensor("bcol_a", [128, 8], F32)
    tn["acol"] = nc.alloc_sbuf_tensor("acol_sb", [128, 8], F32)
    tn["nspcol"] = nc.alloc_sbuf_tensor("nspcol", [128, 8], F32)
    tn["tiny"] = nc.alloc_sbuf_tensor("tiny", [128, 32], F32)
    tn["s1c"] = nc.alloc_sbuf_tensor("s1c_sb", [128, 32], F32)
    tn["s2c"] = nc.alloc_sbuf_tensor("s2c_sb", [128, 32], F32)
    tn["hs1c"] = nc.alloc_sbuf_tensor("hs1c_sb", [128, 32], F32)
    tn["hs2c"] = nc.alloc_sbuf_tensor("hs2c_sb", [128, 32], F32)
    tn["s3c"] = nc.alloc_sbuf_tensor("s3c_sb", [128, 8], F32)
    tn["ones_row"] = nc.alloc_sbuf_tensor("ones_row", [1, 128], F32)
    tn["ones_rowb"] = nc.alloc_sbuf_tensor("ones_rowb", [1, 128], BF16)
    tn["cst_col"] = nc.alloc_sbuf_tensor("cst_col", [128, 1], F32)
    tn["wbuf"] = nc.alloc_sbuf_tensor("wbuf", [128, NBUF, 8, 256], BF16)
    tn["ln2_t"] = nc.alloc_sbuf_tensor("ln2_t", [128, 8, 1024], BF16)
    tn["ps"] = nc.alloc_psum_tensor("ps", [128, 8, 512], F32)

    semnames = ["s_c", "s_x", "s_x2", "s_x3", "s_w", "s_w2", "s_sp", "s_l2", "s_x1r", "s_y", "s_dbg",
                "s_d3v", "s_d3a", "c_v", "c_a", "c_p", "c_t", "c_w", "bar"]
    sems = {n: nc.alloc_semaphore(n) for n in semnames}

    ctx = Ctx()
    ctx.sems = sems
    ctx.dbg = dbg

    def emit_all(c):
        blocks = []
        blocks.append({
            "sync": lambda e: em_S0_sync(c, e),
            "vector": lambda e: em_S0_vector(c, e),
            "scalar": lambda e: em_S0_scalar(c, e),
            "gpsimd": lambda e: None,
            "tensor": lambda e: None,
        })
        blocks.append({
            "sync": lambda e: em_A_sync(c, e),
            "gpsimd": lambda e: em_A_pool(c, e, dbg),
            "scalar": lambda e: em_A_scalar(c, e),
            "vector": lambda e: em_A_vector(c, e),
            "tensor": lambda e: em_A_tensor(c, e),
        })
        for tau in (0, 1):
            blocks.append({
                "sync": lambda e, tau=tau: em_B1_sync(c, e, tau),
                "gpsimd": lambda e, tau=tau: em_B1_gpsimd(c, e, tau),
                "tensor": lambda e, tau=tau: em_B1_tensor(c, e, tau),
                "scalar": lambda e, tau=tau: em_B1_scalar(c, e, tau),
                "vector": lambda e, tau=tau: em_B1_vector(c, e, tau),
            })
            blocks.append({
                "sync": lambda e, tau=tau: em_B2_sync(c, e, tau),
                "gpsimd": lambda e, tau=tau: em_B2_gpsimd(c, e, tau),
                "tensor": lambda e, tau=tau: em_B2_tensor(c, e, tau),
                "scalar": lambda e, tau=tau: em_B2_scalar(c, e, tau),
                "vector": lambda e, tau=tau: em_B2_vector(c, e, tau),
            })
        return blocks

    mock = _MockEng()
    mockt = _MockTensor()
    ctx.t = {k: mockt for k in list(tn.keys()) + [
        "xc2", "pA0", "pB0", "pC0", "pD0", "pA1", "pB1", "pC1", "pD1",
        "T0", "T1", "T2", "T3", "T4", "T5",
        "r_b", "mr_b", "h_all", "l2c2", "ln2_t", "h1_t", "h2_t", "t1_buf",
        "ffn_t", "sqf", "x1r", "y_t", "r3_b", "mr3_b", "rowb"]}
    for blk in emit_all(ctx)[:n_blocks]:
        for en in ("sync", "gpsimd", "scalar", "vector", "tensor"):
            blk[en](mock)
    missing = ctx.refs - set(ctx.M.keys())
    assert not missing, f"unresolved marks: {sorted(missing)}"

    ctx.counting = False
    ctx.t = tn
    ctx._ssn = 0
    blocks = emit_all(ctx)
    engs = ("sync", "gpsimd", "scalar", "vector", "tensor")

    with nc.Block() as block:
        def chain(blk):
            for en in engs:
                getattr(block, en)(lambda eng, en=en, blk=blk: blk[en](eng))

        chain(blocks[0])
        if n_blocks >= 2:
            sa = _ScopeA(nc, tn)
            sa.__enter__()
            chain(blocks[1])
            sa.__exit__(None, None, None)
        if n_blocks >= 3:
            _ScopeBOuter(nc, tn).__enter__()
            for tau in (0, 1):
                if n_blocks >= 2 + 2 * tau + 1:
                    b1 = _ScopeB1(nc, tn)
                    b1.__enter__()
                    chain(blocks[2 + 2 * tau])
                    b1.__exit__(None, None, None)
                if n_blocks >= 3 + 2 * tau + 1:
                    b2 = _ScopeB2(nc, tn)
                    b2.__enter__()
                    chain(blocks[3 + 2 * tau])
                    b2.__exit__(None, None, None)
    return nc


class _ScopeA:
    def __init__(self, nc, tn):
        self.nc, self.tn = nc, tn
        self.keys = []

    def __enter__(self):
        nc, tn = self.nc, self.tn
        self.cms = []
        def add(key, shape, dt):
            cm = nc.sbuf_tensor(key, shape, dt)
            tn[key] = cm.__enter__()
            self.cms.append(cm)
            self.keys.append(key)
        add("xc2", [128, 2, T], F32)
        for p in (0, 1):
            for nm in "ABCD":
                add(f"p{nm}{p}", [128, T], F32)
        add("r_b", [128, T], F32)
        add("mr_b", [128, T], F32)
        add("h_all", [128, 8, T], F32)
        add("l2c2", [128, 2, 1024], BF16)
        for alias, real in (("T0", "pB0"), ("T1", "pC0"), ("T2", "pD0"),
                            ("T3", "pB1"), ("T4", "pC1"), ("T5", "pD1")):
            tn[alias] = tn[real]
        return self

    def __exit__(self, *a):
        for cm in reversed(self.cms):
            cm.__exit__(None, None, None)
        return False


class _ScopeBOuter:

    def __init__(self, nc, tn):
        self.nc, self.tn = nc, tn

    def __enter__(self):
        nc, tn = self.nc, self.tn
        tn["h1_t"] = nc.alloc_sbuf_tensor("h1_t", [128, 32, 1024], BF16)
        tn["t1_buf"] = nc.alloc_sbuf_tensor("t1_buf", [128, 2, 1024], F32)
        return self

    def __exit__(self, *a):
        return False


class _ScopeB1:
    _n = [0]

    def __init__(self, nc, tn):
        self.nc, self.tn = nc, tn

    def __enter__(self):
        self._n[0] += 1
        self.cm = self.nc.sbuf_tensor(f"h2_t_{self._n[0]}", [128, 32, 1024], BF16)
        self.tn["h2_t"] = self.cm.__enter__()
        return self

    def __exit__(self, *a):
        self.cm.__exit__(None, None, None)
        return False


class _ScopeB2:
    _n = [0]

    def __init__(self, nc, tn):
        self.nc, self.tn = nc, tn

    def __enter__(self):
        nc, tn = self.nc, self.tn
        self._n[0] += 1
        self.cms = []
        for key, shape, dt in [("ffn_t", [128, 8, 1024], F32),
                               ("sqf", [128, 2, 1024], F32),
                               ("x1r", [128, 2, 1024], F32),
                               ("y_t", [128, 2, 1024], F32),
                               ("r3_b", [128, 1024], F32),
                               ("mr3_b", [128, 1024], F32),
                               ("rowb", [1, 2, 1024], BF16)]:
            cm = nc.sbuf_tensor(f"{key}_{self._n[0]}", shape, dt)
            tn[key] = cm.__enter__()
            self.cms.append(cm)
        return self

    def __exit__(self, *a):
        for cm in reversed(self.cms):
            cm.__exit__(None, None, None)
        return False



_NC_CACHE = {}


def _get_nc(dbg=False):
    if dbg not in _NC_CACHE:
        _NC_CACHE[dbg] = build_nc(dbg)
    return _NC_CACHE[dbg]


def _layout_inputs(inputs):
    f = np.ascontiguousarray
    x = np.asarray(inputs["L0_attn_x"], dtype=np.float32)
    common = {
        "wgin": f(np.asarray(inputs["L0_attn_w_in"], np.float32)),
        "wga": f(np.asarray(inputs["L0_attn_w_a"], np.float32)),
        "bcin": f(np.asarray(inputs["L0_attn_b_in"], np.float32).T),
        "bca": f(np.asarray(inputs["L0_attn_b_a"], np.float32).T),
        "acol": f(np.asarray(inputs["L0_attn_a_param"], np.float32)
                  .reshape(8, 128).T),
        "w1t": f(np.asarray(inputs["L0_ffn_w1_w"], np.float32).T),
        "w2t": f(np.asarray(inputs["L0_ffn_w2_w"], np.float32).T),
        "h1t": f(np.asarray(inputs["L0_ffn_h1_w"], np.float32).T),
        "h2t": f(np.asarray(inputs["L0_ffn_h2_w"], np.float32).T),
        "w3t": f(np.asarray(inputs["L0_ffn_w3_w"], np.float32).T),
        "s1c": f(np.asarray(inputs["L0_ffn_w1_s"], np.float32)
                 .reshape(32, 128).T),
        "s2c": f(np.asarray(inputs["L0_ffn_w2_s"], np.float32)
                 .reshape(32, 128).T),
        "hs1c": f(np.asarray(inputs["L0_ffn_h1_s"], np.float32)
                  .reshape(32, 128).T),
        "hs2c": f(np.asarray(inputs["L0_ffn_h2_s"], np.float32)
                  .reshape(32, 128).T),
        "s3c": f(np.asarray(inputs["L0_ffn_w3_s"], np.float32)
                 .reshape(8, 128).T),
    }
    in_maps = []
    for b in range(8):
        m = dict(common)
        m["x"] = f(x[b].T)
        in_maps.append(m)
    return in_maps


def kernel(**inputs):
    nc = _get_nc(dbg=False)
    in_maps = _layout_inputs(inputs)
    res = run_bass_kernel_spmd(nc, in_maps, core_ids=list(range(8)))
    out = np.stack([np.asarray(r["y"]).T for r in res.results], axis=0)
    return np.ascontiguousarray(out, dtype=np.float32)


# version 23
# speedup vs baseline: 1.0326x; 1.0037x over previous
import numpy as np
from contextlib import ExitStack

import concourse.bass as bass
import concourse.mybir as mybir
from concourse.bass_utils import run_bass_kernel_spmd

F32 = mybir.dt.float32
BF16 = mybir.dt.bfloat16
AF = mybir.ActivationFunctionType
OP = mybir.AluOpType

D = 1024
T = 2048
NCH = 8
NBUF = 4
EPS_LN = 1e-5
MIN_DECAY = 1e-3
SCAN_EPS = 1e-7
CLAMP_A = 1.0 - 1e-5


def _weight_loads():
    WL = []
    for tau in (0, 1):
        for mg in range(16):
            for part in (0, 1):
                WL.append(("s1", tau, mg, part))
        for st in ("s2", "s3"):
            for mg in range(16):
                for kg in range(4):
                    WL.append((st, tau, mg, kg))
        for mg in range(4):
            for kg in range(4):
                WL.append(("s4", tau, mg, kg))
    G = {k: g for g, k in enumerate(WL)}
    return WL, G


WL, GIDX = _weight_loads()
NLOADS = len(WL)
QID = [0 for g in range(NLOADS)]
QORD = []
_cnt = [0, 0]
for _g in range(NLOADS):
    QORD.append(_cnt[QID[_g]])
    _cnt[QID[_g]] += 1
B2_LEVEL = 6
_L_A_END = NBUF
_L_B1_0_END = GIDX[("s4", 0, 0, 0)]
_L_B2_0_END = GIDX[("s1", 1, 0, 0)]
_L_B1_1_END = GIDX[("s4", 1, 0, 0)]
_L_B2_1_END = NLOADS


class _MockEng:
    def __getattr__(self, name):
        return lambda *a, **k: None


class _MockTensor:
    def __getitem__(self, *a):
        return self

    def rearrange(self, *a, **k):
        return self

    def ap(self):
        return self


class Ctx:

    def __init__(self):
        self.counting = True
        self.cnt = {}
        self.M = {}
        self.refs = set()
        self.sems = None
        self.t = None
        self._ssn = 0

    def inc(self, ins, sem, n=1):
        amt = 16 * n if sem.startswith("s_") else n
        if self.counting:
            self.cnt[sem] = self.cnt.get(sem, 0) + amt
        else:
            ins.then_inc(self.sems[sem], amt)

    def mark(self, name, sem):
        if self.counting:
            assert name not in self.M, f"duplicate mark {name}"
            self.M[name] = (sem, self.cnt.get(sem, 0))

    def wt(self, eng, name):
        if self.counting:
            self.refs.add(name)
            return
        semname, v = self.M[name]
        if v > 0:
            eng.wait_ge(self.sems[semname], v)

    def wt_raw(self, eng, sem, v):
        if not self.counting and v > 0:
            eng.wait_ge(self.sems[sem], v)

    def ss(self, ins, eng, sem):
        name = f"_ss_{sem}_{self._ssn}"
        self._ssn += 1
        self.inc(ins, sem)
        self.mark(name, sem)
        self.wt(eng, name)



N_CONST = 10
C1 = float(np.log1p(-1e-5))
LCL = float(np.log(1e-7))


def em_S0_sync(c, e):
    t = c.t
    for name, dst in [("wgin_h", "wg_in"), ("wga_h", "wg_a"),
                      ("bcin_h", "bcol_in"), ("bca_h", "bcol_a"),
                      ("acol_h", "acol"), ("s1c_h", "s1c"), ("s2c_h", "s2c"),
                      ("hs1c_h", "hs1c"), ("hs2c_h", "hs2c"), ("s3c_h", "s3c")]:
        if name in ("wgin_h", "wga_h"):
            src = t[name].ap().rearrange("h i j -> i h j")
        else:
            src = t[name].ap()
        c.inc(e.dma_start(t[dst][:], src), "s_c")


def em_S0_vector(c, e):
    t = c.t
    e.memset(t["ones_row"][:], 1.0)
    e.memset(t["ones_rowb"][:], 1.0)
    ins = e.memset(t["cst_col"][:], 1.0 / D)
    c.inc(ins, "c_v")
    c.mark("v_ms", "c_v")
    c.wt_raw(e, "s_c", 16 * N_CONST)
    tr = t["tiny"][:, 0:8]
    tb = t["tiny"][:, 8:16]
    tl = t["tiny"][:, 24:32]
    ins = e.tensor_single_scalar(tr, t["acol"][:], 0.0, OP.max)
    c.inc(ins, "c_v")
    c.mark("v_pro1", "c_v")
    c.wt(e, "a_pro1")
    c.ss(e.scalar_tensor_tensor(tb, tl, MIN_DECAY, tr, OP.add, OP.add), e, "c_v")
    ins = e.tensor_single_scalar(t["nspcol"][:], tb, -1.0, OP.mult)
    c.inc(ins, "c_v")
    c.mark("v_pro2", "c_v")


def em_S0_scalar(c, e):
    t = c.t
    tb = t["tiny"][:, 8:16]
    te = t["tiny"][:, 16:24]
    tl = t["tiny"][:, 24:32]
    c.wt_raw(e, "s_c", 16 * N_CONST)
    c.ss(e.activation(tb, t["acol"][:], AF.Abs), e, "c_a")
    c.ss(e.activation(te, tb, AF.Exp, scale=-1.0), e, "c_a")
    ins = e.activation(tl, te, AF.Ln, bias=1.0)
    c.inc(ins, "c_a")
    c.mark("a_pro1", "c_a")


def em_S0_gpsimd(c, e):
    pass


def em_S0_tensor(c, e):
    pass




def _pt(c, name, ch):
    return c.t[f"p{name}{ch % 2}"]


def _xload(c, e, ch, sem):
    src = c.t["x_h"].ap()[ch * 128:(ch + 1) * 128, :]
    c.inc(e.dma_start(c.t["xc2"][:, ch % 2, :], src), sem)


XQ = {0: ("s_x", 1), 3: ("s_x", 2), 6: ("s_x", 3),
      1: ("s_x2", 1), 4: ("s_x2", 2), 7: ("s_x2", 3),
      2: ("s_x3", 1), 5: ("s_x3", 2)}


def _wt_x1r(c, e, ch):
    sem, o = XQ[ch]
    c.wt_raw(e, sem, 16 * o)


def em_A_sync(c, e):
    for ch in (0, 3, 6):
        if ch >= 2:
            c.wt(e, f"t_A0st_{ch - 2}")
        _xload(c, e, ch, "s_x")
    for ch in range(8):
        c.wt(e, [f"t_A0st_{6 + ch}", f"p_xr_{ch - 2}"][ch >= 2])
        _xload(c, e, ch, "s_x")
    for ch in range(8):
        c.wt(e, [f"p_xr_{6 + ch}", f"v_x1_{ch - 2}"][ch >= 2])
        _xload(c, e, ch, "s_x")


def em_A_pool(c, e, dbg=False):
    t = c.t
    dbg2 = dbg == 2

    def dump(ch, name, tile, markname):
        if not dbg2:
            return
        c.wt(e, markname)
        c.inc(e.dma_start(t[name].ap()[ch * 128:(ch + 1) * 128, :],
                          _pt(c, tile, ch)[:]), "s_d3v")
        c.mark(f"d3_{name}_{ch}", "s_d3v")

    def gated(ch, hh):
        sl = slice(hh * 1024, (hh + 1) * 1024)
        if hh == 0:
            dump(ch, "dbg_ln1", "B", f"v_ln1s_{ch}")
            c.wt(e, f"a_gxh0_{ch}")
            c.wt(e, f"v_ln1s_{ch}")
        else:
            c.wt(e, f"a_gx_{ch}")
        A = _pt(c, "A", ch)
        ins = e.tensor_mul(A[:, sl], A[:, sl], _pt(c, "B", ch)[:, sl])
        c.inc(ins, "c_p")
        c.mark([f"p_g0_{ch}", f"p_gated_{ch}"][hh], "c_p")
        if hh == 1:
            dump(ch, "dbg_gx", "A", f"p_gated_{ch}")

    def xt(ch):
        dump(ch, "dbg_mu", "C", f"a_mult_{ch}")
        dump(ch, "dbg_nrm", "D", f"v_normed_{ch}")
        c.wt(e, f"v_normed_{ch}")
        ins = e.tensor_scalar(_pt(c, "C", ch)[:], _pt(c, "D", ch)[:],
                              SCAN_EPS, 1.0, OP.add, OP.mult)
        c.inc(ins, "c_p")
        c.mark(f"p_xt_{ch}", "c_p")

    def xr(ch):
        c.wt_raw(e, "s_x", 16 * (4 + ch))
        if ch < 2:
            c.wt(e, "v_A0bc")
        else:
            c.wt(e, f"v_normed_{ch - 2}")
        ins = e.tensor_mul(_pt(c, "A", ch)[:], t["xc2"][:, ch % 2, :],
                           t["r_b"][:])
        c.inc(ins, "c_p")
        c.mark(f"p_xr_{ch}", "c_p")

    c.wt(e, "t_A0st_0")
    _xload(c, e, 2, "s_x3")
    c.wt(e, "t_A0st_3")
    _xload(c, e, 5, "s_x3")
    xr(0)
    xr(1)
    for q in range(NCH // 2):
        c0, c1 = 2 * q, 2 * q + 1
        gated(c0, 0)
        gated(c1, 0)
        gated(c0, 1)
        gated(c1, 1)
        if dbg2:
            dump(c0, "dbg_a", "C", f"v_lac_{c0}")
            dump(c0, "dbg_cum", "B", f"v_lcum_{c0}")
            dump(c1, "dbg_a", "C", f"v_lac_{c1}")
            dump(c1, "dbg_cum", "B", f"v_lcum_{c1}")
        xt(c0)
        if c1 + 1 < NCH:
            xr(c1 + 1)
        xt(c1)
        if c1 + 2 < NCH:
            xr(c1 + 2)
    if dbg:
        for ch in range(NCH):
            c.wt(e, f"v_h_{ch}")
            c.inc(e.dma_start(t["dbg_h"].ap()[ch * 128:(ch + 1) * 128, :],
                              t["h_all"][:, ch, :]), "s_dbg")
    for ch in range(NCH):
        if ch == 0:
            c.wt(e, "v_A2bc")
        elif ch >= 2:
            c.wt(e, f"v_x1_{ch - 2}")
        ins = e.tensor_mul(_pt(c, "B", ch)[:], t["h_all"][:, ch, :],
                           t["r_b"][:])
        c.inc(ins, "c_p")
        c.mark(f"p_x1m_{ch}", "c_p")
    for ch in range(NCH):
        c.wt(e, f"v_x1_{ch}")
        c.inc(e.dma_start(t["x1sp"].ap()[ch * 128:(ch + 1) * 128, :],
                          t["h_all"][:, ch, :]), "s_sp")
    def spill_ln2(ch):
        c.wt(e, f"v_ln2_{ch}")
        c.inc(e.dma_start(t["ln2sp"].ap()[ch * 128:(ch + 1) * 128, :],
                          t["l2c2"][:, ch % 2, :]), "s_sp")

    for ch in range(NCH):
        if ch == 0:
            c.wt(e, "v_A3bc")
        elif ch >= 2:
            c.wt(e, f"v_ln2_{ch - 2}")
        ins = e.tensor_mul(_pt(c, "B", ch)[:], t["h_all"][:, ch, :],
                           t["r_b"][:])
        c.inc(ins, "c_p")
        c.mark(f"p_l2m_{ch}", "c_p")
        if ch == 0:
            for g in range(_L_A_END):
                _em_wload(c, e, g)
        if ch >= 1:
            spill_ln2(ch - 1)
    spill_ln2(7)


def _wsrc(c, g):
    t = c.t
    st, tau, mg, x = WL[g]
    if st == "s1":
        wname = "w1t_h" if x == 0 else "w2t_h"
        src = c.t[wname].ap()[0:1024, mg * 256:(mg + 1) * 256]
    elif st == "s2":
        src = t["h1t_h"].ap()[x * 1024:(x + 1) * 1024, mg * 256:(mg + 1) * 256]
    elif st == "s3":
        src = t["h2t_h"].ap()[x * 1024:(x + 1) * 1024, mg * 256:(mg + 1) * 256]
    else:
        src = t["w3t_h"].ap()[x * 1024:(x + 1) * 1024, mg * 256:(mg + 1) * 256]
    return src.rearrange("(a p) n -> p a n", p=128)


def _em_wload(c, e, g):
    if g >= NBUF:
        c.wt_raw(e, "c_w", g - (NBUF - 1))
    ins = e.dma_start(c.t["wbuf"][:, g % NBUF, :, :], _wsrc(c, g))
    c.inc(ins, "s_w2" if QID[g] else "s_w")


def _wt_w(c, e, g):
    c.wt_raw(e, "s_w2" if QID[g] else "s_w", 16 * (QORD[g] + 1))


def _rows(c, name, n2048=True):
    t = c.t
    return t[name][0:1, :]


def _em_rows_dve_1(c, e, pref):
    t = c.t
    r0, r1, r2, r3 = (t["T0"][0:1, :], t["T1"][0:1, :],
                      t["T2"][0:1, :], t["T3"][0:1, :])
    c.ss(e.tensor_copy(r0.rearrange("p (a n) -> p a n", a=4),
                       t["ps"][0:1, 0:4, :]), e, "c_v")
    c.ss(e.tensor_copy(r1.rearrange("p (a n) -> p a n", a=4),
                       t["ps"][0:1, 4:8, :]), e, "c_v")
    c.ss(e.tensor_mul(r2, r0, r0), e, "c_v")
    c.ss(e.tensor_sub(r3, r1, r2), e, "c_v")
    c.ss(e.tensor_single_scalar(r2, r3, 0.0, OP.max), e, "c_v")
    ins = e.tensor_scalar_add(r3, r2, EPS_LN)
    c.inc(ins, "c_v")
    c.mark(f"v_{pref}r1", "c_v")


def _em_rows_act(c, e, pref):
    t = c.t
    r2, r3, r4 = t["T2"][0:1, :], t["T3"][0:1, :], t["T4"][0:1, :]
    c.wt(e, f"v_{pref}r1")
    c.ss(e.activation(r2, r3, AF.Ln), e, "c_a")
    ins = e.activation(r4, r2, AF.Exp, scale=-0.5)
    c.inc(ins, "c_a")
    c.mark(f"a_{pref}r", "c_a")


def _em_rows_dve_2(c, e, pref):
    t = c.t
    r0, r4, r5 = t["T0"][0:1, :], t["T4"][0:1, :], t["T5"][0:1, :]
    c.wt(e, f"a_{pref}r")
    ins = e.tensor_mul(r5, r0, r4)
    c.inc(ins, "c_v")
    c.mark(f"v_{pref}r2", "c_v")


def _em_bcast_pe(c, e, pref):
    t = c.t
    c.wt(e, f"v_{pref}r2")
    r4, r5 = t["T4"][0:1, :], t["T5"][0:1, :]
    ins = None
    for n in range(4):
        ins = e.matmul(t["ps"][:, n, :], t["ones_row"][:],
                       r4[0:1, n * 512:(n + 1) * 512], start=True, stop=True)
    for n in range(4):
        ins = e.matmul(t["ps"][:, 4 + n, :], t["ones_row"][:],
                       r5[0:1, n * 512:(n + 1) * 512], start=True, stop=True)
    c.inc(ins, "c_t")
    c.mark(f"t_{pref}bc", "c_t")


def _em_bcast_evac_dve(c, e, pref):
    t = c.t
    c.wt(e, f"t_{pref}bc")
    e.tensor_copy(t["r_b"][:].rearrange("p (a n) -> p a n", a=4),
                  t["ps"][:, 0:4, :])
    ins = e.tensor_copy(t["mr_b"][:].rearrange("p (a n) -> p a n", a=4),
                        t["ps"][:, 4:8, :])
    c.inc(ins, "c_v")
    c.mark(f"v_{pref}bc", "c_v")


def em_A_scalar(c, e):
    t = c.t
    ps = t["ps"]
    for ch in range(NCH):
        if ch in (0, 3, 6):
            och = ch + 1
            if och >= 4:
                c.wt(e, f"t_A0st_{och - 2}")
            _xload(c, e, och, "s_x2")
        _wt_x1r(c, e, ch)
        if ch >= 2:
            c.wt(e, f"t_A0st_{ch - 2}")
        ins = e.activation(_pt(c, "A", ch)[:], t["xc2"][:, ch % 2, :],
                           AF.Square)
        c.inc(ins, "c_a")
        c.mark(f"a_A0sq_{ch}", "c_a")
    _em_rows_act(c, e, "A0")
    def sig(ch, hh, which):
        bk = 4 * (ch % 2) + (0 if which == "gx" else 2)
        dst = _pt(c, "A" if which == "gx" else "C", ch)
        sl = slice(hh * 1024, (hh + 1) * 1024)
        if which == "gx":
            c.wt(e, f"t_g{hh}_{ch}")
        elif hh == 0 and ch >= 2:
            c.wt(e, f"v_h_{ch - 2}")
        col = t["bcol_in" if which == "gx" else "bcol_a"][:, ch:ch + 1]
        ins = e.activation(dst[:, sl].rearrange("p (a n) -> p a n", a=2),
                           ps[:, bk:bk + 2, :], AF.Sigmoid, bias=col)
        c.inc(ins, "c_a")
        mk = {("gx", 0): f"a_gxh0_{ch}", ("ga", 0): f"a_gah0_{ch}",
              ("gx", 1): f"a_gx_{ch}", ("ga", 1): f"a_ga_{ch}"}[(which, hh)]
        c.mark(mk, "c_a")

    for q in range(NCH // 2):
        pair = (2 * q, 2 * q + 1)
        for ch in pair:
            sig(ch, 0, "gx")
        for ch in pair:
            sig(ch, 0, "ga")
        for ch in pair:
            sig(ch, 1, "gx")
        for ch in pair:
            sig(ch, 1, "ga")
        for ch in pair:
            c.wt(e, f"v_lac_{ch}")
            if ch >= 2:
                c.wt(e, f"p_xt_{ch - 2}")
            ins = e.activation(_pt(c, "D", ch)[:], _pt(c, "C", ch)[:],
                               AF.Exp, scale=2.0)
            c.inc(ins, "c_a")
            c.mark(f"a_aa_{ch}", "c_a")
        for ch in pair:
            c.wt(e, f"v_lcum_{ch}")
            if c.dbg == 2:
                c.wt(e, f"d3_dbg_a_{ch}")
            ins = e.activation(_pt(c, "C", ch)[:], _pt(c, "D", ch)[:],
                               AF.Sqrt, scale=-1.0, bias=1.0)
            c.inc(ins, "c_a")
            c.mark(f"a_mult_{ch}", "c_a")
        for ch in pair:
            B, Dd = _pt(c, "B", ch), _pt(c, "D", ch)
            c.wt(e, f"v_m1_{ch}")
            e.activation(B[:, 0:1], Dd[:, 0:1], AF.Copy, scale=0.0)
            ins = e.activation(B[:, 1:2], Dd[:, 0:1], AF.Copy)
            c.inc(ins, "c_a")
            c.mark(f"a_ed_{ch}", "c_a")
            c.wt(e, f"v_d_{ch}")
            ins = e.activation(t["h_all"][:, ch, :], B[:], AF.Exp)
            c.inc(ins, "c_a")
            c.mark(f"a_at_{ch}", "c_a")
    for ch in range(NCH):
        c.wt(e, f"v_h_{ch}")
        if ch >= 2:
            c.wt(e, f"t_A2st_{ch - 2}")
        else:
            c.wt(e, f"v_h_{6 + ch}")
        ins = e.activation(_pt(c, "A", ch)[:], t["h_all"][:, ch, :],
                           AF.Square)
        c.inc(ins, "c_a")
        c.mark(f"a_hsq_{ch}", "c_a")
    _em_rows_act(c, e, "A2")
    for ch in range(NCH):
        c.wt(e, f"v_x1_{ch}")
        if ch >= 2:
            c.wt(e, f"t_lnst_{ch - 2}")
        else:
            c.wt(e, f"t_A2st_{6 + ch}")
        ins = e.activation(_pt(c, "A", ch)[:], t["h_all"][:, ch, :],
                           AF.Square)
        c.inc(ins, "c_a")
        c.mark(f"a_x1sq_{ch}", "c_a")
    _em_rows_act(c, e, "A3")


def em_A_vector(c, e):
    t = c.t
    c.wt(e, "t_A0st_7")
    _em_rows_dve_1(c, e, "A0")
    _em_rows_dve_2(c, e, "A0")
    _em_bcast_evac_dve(c, e, "A0")
    def ln1s(ch):
        A, B = _pt(c, "A", ch), _pt(c, "B", ch)
        c.wt(e, f"p_xr_{ch}")
        if ch >= 2:
            c.wt(e, f"a_at_{ch - 2}")
        ins = e.tensor_sub(B[:], A[:], t["mr_b"][:])
        c.inc(ins, "c_v")
        c.mark(f"v_ln1s_{ch}", "c_v")

    def lac(ch):
        Cc = _pt(c, "C", ch)
        c.wt(e, f"a_ga_{ch}")
        ins = e.tensor_scalar(Cc[:], Cc[:],
                              t["nspcol"][:, ch:ch + 1], C1, OP.mult, OP.min)
        c.inc(ins, "c_v")
        c.mark(f"v_lac_{ch}", "c_v")

    def lcum(ch):
        B, C = _pt(c, "B", ch), _pt(c, "C", ch)
        c.wt(e, f"t_g1_{ch}")
        c.wt(e, f"p_gated_{ch}")
        if c.dbg == 2:
            c.wt(e, f"d3_dbg_ln1_{ch}")
        ins = e.tensor_tensor_scan(B[:], C[:], C[:], 0.0, OP.add, OP.bypass)
        c.inc(ins, "c_v")
        c.mark(f"v_lcum_{ch}", "c_v")

    def m1(ch):
        c.wt(e, f"a_mult_{ch}")
        ins = e.tensor_scalar(_pt(c, "D", ch)[:], _pt(c, "B", ch)[:],
                              0.0, LCL, OP.min, OP.max)
        c.inc(ins, "c_v")
        c.mark(f"v_m1_{ch}", "c_v")

    def d_normed(ch):
        A, B, C, D = (_pt(c, "A", ch), _pt(c, "B", ch),
                      _pt(c, "C", ch), _pt(c, "D", ch))
        if c.dbg == 2:
            c.wt(e, f"d3_dbg_cum_{ch}")
        ins = e.tensor_sub(B[:, 2:T], D[:, 1:T - 1], D[:, 0:T - 2])
        c.inc(ins, "c_v")
        c.mark(f"v_d_{ch}", "c_v")
        c.wt(e, f"a_ed_{ch}")
        ins = e.tensor_mul(D[:], A[:], C[:])
        c.inc(ins, "c_v")
        c.mark(f"v_normed_{ch}", "c_v")

    ln1s(0)
    ln1s(1)
    for q in range(NCH // 2):
        c0, c1 = 2 * q, 2 * q + 1
        lac(c0)
        lac(c1)
        lcum(c0)
        lcum(c1)
        m1(c0)
        m1(c1)
        d_normed(c0)
        d_normed(c1)
        if c1 + 2 < NCH:
            ln1s(c1 + 1)
            ln1s(c1 + 2)
        _em_hscan(c, e, c0)
        _em_hscan(c, e, c1)
    c.wt(e, "t_A2st_7")
    _em_rows_dve_1(c, e, "A2")
    _em_rows_dve_2(c, e, "A2")
    _em_bcast_evac_dve(c, e, "A2")
    for ch in range(NCH):
        p = ch % 2
        c.wt_raw(e, "s_x", 16 * (12 + ch))
        if getattr(c, "dbg", False):
            c.wt_raw(e, "s_dbg", 16 * (ch + 1))
        B, Cc = _pt(c, "B", ch), _pt(c, "C", ch)
        c.wt(e, f"p_x1m_{ch}")
        e.tensor_sub(Cc[:], B[:], t["mr_b"][:])
        ins = e.tensor_add(t["h_all"][:, ch, :], t["xc2"][:, p, :], Cc[:])
        c.inc(ins, "c_v")
        c.mark(f"v_x1_{ch}", "c_v")
    c.wt(e, "t_lnst_7")
    _em_rows_dve_1(c, e, "A3")
    _em_rows_dve_2(c, e, "A3")
    _em_bcast_evac_dve(c, e, "A3")
    for ch in range(NCH):
        if ch >= 2:
            c.wt_raw(e, "s_sp", 16 * (7 + ch))
        B = _pt(c, "B", ch)
        c.wt(e, f"p_l2m_{ch}")
        e.tensor_sub(t["ln2_t"][:, ch, :], B[:, 0:1024], t["mr_b"][:, 0:1024])
        ins = e.tensor_sub(t["l2c2"][:, ch % 2, :], B[:, 1024:T],
                           t["mr_b"][:, 1024:T])
        c.inc(ins, "c_v")
        c.mark(f"v_ln2_{ch}", "c_v")


def _em_hscan(c, e, ch):
    c.wt(e, f"a_at_{ch}")
    c.wt(e, f"p_xt_{ch}")
    h = c.t["h_all"][:, ch, :]
    ins = e.tensor_tensor_scan(h, h, _pt(c, "C", ch)[:], 0.0, OP.mult, OP.add)
    c.inc(ins, "c_v")
    c.mark(f"v_h_{ch}", "c_v")


def em_A_tensor(c, e):
    t = c.t
    ps = t["ps"]

    def stats8(ch, markpref, rhs_x):
        ins = None
        for n in range(4):
            ins = e.matmul(ps[0:1, n, :], t["cst_col"][:],
                           rhs_x[:, n * 512:(n + 1) * 512],
                           start=(ch == 0), stop=(ch == 7),
                           skip_group_check=True)
        sq = _pt(c, "A", ch)
        for n in range(4):
            ins = e.matmul(ps[0:1, 4 + n, :], t["cst_col"][:],
                           sq[:, n * 512:(n + 1) * 512],
                           start=(ch == 0), stop=(ch == 7),
                           skip_group_check=True)
        c.inc(ins, "c_t")
        c.mark(f"{markpref}_{ch}", "c_t")

    c.wt(e, "v_ms")
    for ch in range(NCH):
        _wt_x1r(c, e, ch)
        c.wt(e, f"a_A0sq_{ch}")
        stats8(ch, "t_A0st", t["xc2"][:, ch % 2, :])
    _em_bcast_pe(c, e, "A0")
    def gates_half(ch, hh):
        bk = 4 * (ch % 2)
        B = _pt(c, "B", ch)
        if hh == 0:
            c.wt(e, f"v_ln1s_{ch}")
            if ch >= 2:
                c.wt(e, f"a_ga_{ch - 2}")
            else:
                c.wt(e, "v_A0bc")
        else:
            c.wt(e, f"a_gah0_{ch}")
        ins = None
        for n in range(2):
            nn = 2 * hh + n
            ins = e.matmul(ps[:, bk + n, :], t["wg_in"][:, ch, :],
                           B[:, nn * 512:(nn + 1) * 512],
                           start=True, stop=True)
        for n in range(2):
            nn = 2 * hh + n
            ins = e.matmul(ps[:, bk + 2 + n, :], t["wg_a"][:, ch, :],
                           B[:, nn * 512:(nn + 1) * 512],
                           start=True, stop=True)
        c.inc(ins, "c_t")
        c.mark(f"t_g{hh}_{ch}", "c_t")

    for q in range(NCH // 2):
        for hh in (0, 1):
            gates_half(2 * q, hh)
            gates_half(2 * q + 1, hh)
    for ch in range(NCH):
        c.wt(e, f"a_hsq_{ch}")
        if ch == 0:
            c.wt(e, "a_ga_7")
        stats8(ch, "t_A2st", t["h_all"][:, ch, :])
    _em_bcast_pe(c, e, "A2")
    for ch in range(NCH):
        c.wt(e, f"a_x1sq_{ch}")
        if ch == 0:
            c.wt(e, "v_A2bc")
        stats8(ch, "t_lnst", t["h_all"][:, ch, :])
    _em_bcast_pe(c, e, "A3")



def em_B1_sync(c, e, tau):
    em_B1_sync_wloads(c, e, tau)


def em_B1_gpsimd(c, e, tau):
    lo = _L_A_END if tau == 0 else _L_B2_0_END
    hi = _L_B1_0_END if tau == 0 else _L_B1_1_END
    for g in range(lo, hi):
        if QID[g] == 0:
            _em_wload(c, e, g)


def em_B1_sync_wloads(c, e, tau):
    lo = _L_A_END if tau == 0 else _L_B2_0_END
    hi = _L_B1_0_END if tau == 0 else _L_B1_1_END
    for g in range(lo, hi):
        if QID[g] == 1:
            _em_wload(c, e, g)


def em_B1_tensor(c, e, tau):
    t = c.t
    ps = t["ps"]
    if tau == 0:
        c.wt(e, "v_A3bc")
    else:
        c.wt_raw(e, "s_l2", 16)
        c.wt(e, "ev_s4_0_7")
    for mg in range(16):
        for part in (0, 1):
            g = GIDX[("s1", tau, mg, part)]
            if tau == 1 and mg == 0 and part == 1:
                c.wt(e, "v_bc4_0")
            if mg >= 1:
                c.wt(e, [f"ev_s1a_{tau}_{mg - 1}", f"ev_s1b_{tau}_{mg - 1}"][part])
            _wt_w(c, e, g)
            ins = None
            for kk in range(8):
                if tau == 0 and mg == 0 and part == 0:
                    c.wt(e, f"v_ln2_{kk}")
                for mi in range(2):
                    lhsT = t["wbuf"][:, g % NBUF, kk, mi * 128:(mi + 1) * 128]
                    for n2 in range(2):
                        ins = e.matmul(ps[:, 4 * part + 2 * mi + n2, :], lhsT,
                                       t["ln2_t"][:, kk, n2 * 512:(n2 + 1) * 512],
                                       start=(kk == 0), stop=(kk == 7))
            c.inc(ins, "c_w")
            c.mark(f"pe_s1{'ab'[part]}_{tau}_{mg}", "c_w")
    for st, rhs in (("s2", "h1_t"), ("s3", "h2_t")):
        prev_ev = f"ev_s1b_{tau}_15" if st == "s2" else f"ev_s2_{tau}_15"
        c.wt(e, prev_ev)
        for mg in range(16):
            se = mg % 2
            if mg >= 2:
                c.wt(e, f"ev_{st}_{tau}_{mg - 2}")
            for kg in range(4):
                g = GIDX[(st, tau, mg, kg)]
                _wt_w(c, e, g)
                ins = None
                for kk in range(8):
                    k = kg * 8 + kk
                    for mi in range(2):
                        lhsT = t["wbuf"][:, g % NBUF, kk, mi * 128:(mi + 1) * 128]
                        for n2 in range(2):
                            ins = e.matmul(
                                ps[:, 4 * se + 2 * mi + n2, :], lhsT,
                                t[rhs][:, k, n2 * 512:(n2 + 1) * 512],
                                start=(k == 0), stop=(k == 31))
                c.inc(ins, "c_w")
            c.mark(f"pe_{st}_{tau}_{mg}", "c_w")


def em_B1_scalar(c, e, tau):
    t = c.t
    ps = t["ps"]
    if tau == 0:
        c.wt_raw(e, "s_sp", 16 * 16)
    else:
        c.wt(e, "v_C2_0")
    for mg in range(16):
        c.wt(e, f"pe_s1a_{tau}_{mg}")
        if mg >= 1:
            c.wt(e, f"ev_s1b_{tau}_{mg - 1}")
        ins = None
        for mi in range(2):
            m = 2 * mg + mi
            for n2 in range(2):
                ins = e.activation(t["t1_buf"][:, mi, n2 * 512:(n2 + 1) * 512],
                                   ps[:, 2 * mi + n2, :], AF.Silu,
                                   scale=t["s1c"][:, m:m + 1])
        c.inc(ins, "c_a")
        c.mark(f"ev_s1a_{tau}_{mg}", "c_a")
    for st, dst, scol in (("s2", "h2_t", "hs1c"), ("s3", "h1_t", "hs2c")):
        for mg in range(16):
            se = mg % 2
            c.wt(e, f"pe_{st}_{tau}_{mg}")
            if tau == 0 and st == "s2" and mg == 0:
                c.wt_raw(e, "s_sp", 16 * 16)
                src2 = t["ln2sp"].ap().rearrange("(a p) n -> p a n", p=128)
                c.inc(e.dma_start(t["ln2_t"][:], src2), "s_l2")
            if tau == 1 and st == "s2" and mg == 0:
                c.wt(e, "v_yc_0_7")
                c.wt_raw(e, "s_y", 16 * 8)
            ins = None
            for mi in range(2):
                m = 2 * mg + mi
                for n2 in range(2):
                    ins = e.activation(t[dst][:, m, n2 * 512:(n2 + 1) * 512],
                                       ps[:, 4 * se + 2 * mi + n2, :], AF.Gelu,
                                       scale=t[scol][:, m:m + 1])
            c.inc(ins, "c_a")
            c.mark(f"ev_{st}_{tau}_{mg}", "c_a")


def em_B1_vector(c, e, tau):
    t = c.t
    ps = t["ps"]
    if tau == 0:
        c.wt_raw(e, "s_sp", 16 * 16)
    else:
        c.wt(e, "pe_s4_0_7")
    for mg in range(16):
        c.wt(e, f"pe_s1b_{tau}_{mg}")
        c.wt(e, f"ev_s1a_{tau}_{mg}")
        ins = None
        for mi in range(2):
            m = 2 * mg + mi
            for n2 in range(2):
                ins = e.scalar_tensor_tensor(
                    t["h1_t"][:, m, n2 * 512:(n2 + 1) * 512],
                    ps[:, 4 + 2 * mi + n2, :], t["s2c"][:, m:m + 1],
                    t["t1_buf"][:, mi, n2 * 512:(n2 + 1) * 512],
                    OP.mult, OP.mult)
        c.inc(ins, "c_v")
        c.mark(f"ev_s1b_{tau}_{mg}", "c_v")



def em_B2_sync(c, e, tau):
    t = c.t
    em_B2_sync_wloads(c, e, tau)
    c.wt_raw(e, "s_sp", 16 * 8)
    if B2_LEVEL < 4:
        return
    c.wt(e, f"pe_s3_{tau}_15")
    for ch in range(NCH):
        if ch >= 2 and B2_LEVEL >= 6:
            c.wt(e, f"v_yc_{tau}_{ch - 2}")
            c.inc(e.dma_start(
                t["y_h"].ap()[(ch - 2) * 128:(ch - 1) * 128,
                              tau * 1024:(tau + 1) * 1024],
                t["y_t"][:, ch % 2, :]), "s_y")
        if tau == 1 and ch < 2:
            c.wt(e, f"v_yc_0_{6 + ch}")
        src = t["x1sp"].ap()[ch * 128:(ch + 1) * 128,
                             tau * 1024:(tau + 1) * 1024]
        c.inc(e.dma_start(t["x1r"][:, ch % 2, :], src), "s_x1r")
    if B2_LEVEL >= 6:
        for ch in (6, 7):
            c.wt(e, f"v_yc_{tau}_{ch}")
            c.inc(e.dma_start(
                t["y_h"].ap()[ch * 128:(ch + 1) * 128,
                              tau * 1024:(tau + 1) * 1024],
                t["y_t"][:, ch % 2, :]), "s_y")
        if tau == 1:
            c.wt_raw(e, "s_y", 16 * 16)


def em_B2_gpsimd(c, e, tau):
    lo = _L_B1_0_END if tau == 0 else _L_B1_1_END
    hi = _L_B2_0_END if tau == 0 else _L_B2_1_END
    for g in range(lo, hi):
        if QID[g] == 0:
            _em_wload(c, e, g)


def em_B2_sync_wloads(c, e, tau):
    lo = _L_B1_0_END if tau == 0 else _L_B1_1_END
    hi = _L_B2_0_END if tau == 0 else _L_B2_1_END
    for g in range(lo, hi):
        if QID[g] == 1:
            _em_wload(c, e, g)


def em_B2_tensor(c, e, tau):
    t = c.t
    ps = t["ps"]
    c.wt(e, f"ev_s3_{tau}_15")

    def stats(m):
        c.wt(e, f"ev_s4sq_{tau}_{m}")
        ins = None
        for n2 in range(2):
            ins = e.matmul(ps[0:1, 4 + n2, :], t["cst_col"][:],
                           t["ffn_t"][:, m, n2 * 512:(n2 + 1) * 512],
                           start=(m == 0), stop=(m == 7),
                           skip_group_check=True)
            ins = e.matmul(ps[0:1, 6 + n2, :], t["cst_col"][:],
                           t["sqf"][:, m % 2, n2 * 512:(n2 + 1) * 512],
                           start=(m == 0), stop=(m == 7),
                           skip_group_check=True)
        c.inc(ins, "c_t")
        c.mark(f"pe_s4stats_{tau}_{m}", "c_t")

    for m in range(8):
        if m >= 2:
            c.wt(e, f"ev_s4_{tau}_{m - 2}")
        for kg in range(4):
            g = GIDX[("s4", tau, m // 2, kg)]
            _wt_w(c, e, g)
            ins = None
            for kk in range(8):
                k = kg * 8 + kk
                lhsT = t["wbuf"][:, g % NBUF, kk,
                                 (m % 2) * 128:(m % 2 + 1) * 128]
                for n2 in range(2):
                    ins = e.matmul(ps[:, 2 * (m % 2) + n2, :], lhsT,
                                   t["h1_t"][:, k, n2 * 512:(n2 + 1) * 512],
                                   start=(k == 0), stop=(k == 31))
            if m % 2 == 1:
                c.inc(ins, "c_w")
            elif kg == 3:
                c.inc(ins, "c_t")
        c.mark(f"pe_s4_{tau}_{m}", "c_w" if m % 2 == 1 else "c_t")
        if m >= 1 and B2_LEVEL >= 2:
            stats(m - 1)
    if B2_LEVEL >= 2:
        stats(7)
    if B2_LEVEL < 3:
        return
    c.wt(e, f"v_C2_{tau}")
    r4 = t["rowb"][0:1, 0, :]
    r5 = t["rowb"][0:1, 1, :]
    ins = None
    for n2 in range(2):
        ins = e.matmul(ps[:, 4 + n2, :], t["ones_rowb"][:],
                       r4[0:1, n2 * 512:(n2 + 1) * 512], start=True, stop=True)
    for n2 in range(2):
        ins = e.matmul(ps[:, 6 + n2, :], t["ones_rowb"][:],
                       r5[0:1, n2 * 512:(n2 + 1) * 512], start=True, stop=True)
    c.inc(ins, "c_t")
    c.mark(f"t_bc4_{tau}", "c_t")


def em_B2_scalar(c, e, tau):
    t = c.t
    ps = t["ps"]
    if tau == 1:
        c.wt(e, "v_yc_0_7")
    for m in range(8):
        c.wt(e, f"pe_s4_{tau}_{m}")
        if m >= 2 and B2_LEVEL >= 2:
            c.wt(e, f"pe_s4stats_{tau}_{m - 2}")
        ins = None
        for n2 in range(2):
            ins = e.activation(t["ffn_t"][:, m, n2 * 512:(n2 + 1) * 512],
                               ps[:, 2 * (m % 2) + n2, :], AF.Copy,
                               scale=t["s3c"][:, m:m + 1])
        c.inc(ins, "c_a")
        c.mark(f"ev_s4_{tau}_{m}", "c_a")
        for n2 in range(2):
            ins = e.activation(t["sqf"][:, m % 2, n2 * 512:(n2 + 1) * 512],
                               t["ffn_t"][:, m, n2 * 512:(n2 + 1) * 512],
                               AF.Square)
        c.inc(ins, "c_a")
        c.mark(f"ev_s4sq_{tau}_{m}", "c_a")
    if B2_LEVEL < 3:
        return
    c.wt(e, f"v_C1_{tau}")
    tmp2 = t["sqf"][0:1, 1, :]
    r4 = t["y_t"][0:1, 0, :]
    s4 = t["t1_buf"][0:1, 1, :]
    c.ss(e.activation(tmp2, s4, AF.Ln), e, "c_a")
    c.ss(e.activation(r4, tmp2, AF.Exp, scale=-0.5), e, "c_a")
    ins = e.activation(t["rowb"][0:1, 0, :], r4, AF.Copy)
    c.inc(ins, "c_a")
    c.mark(f"a_C_{tau}", "c_a")


def em_B2_vector(c, e, tau):
    t = c.t
    ps = t["ps"]
    if B2_LEVEL < 3:
        return
    m4 = t["t1_buf"][0:1, 0, :]
    s4 = t["t1_buf"][0:1, 1, :]
    tmp = t["sqf"][0:1, 0, :]
    r5 = t["y_t"][0:1, 1, :]
    c.wt(e, f"pe_s4stats_{tau}_7")
    c.ss(e.tensor_copy(m4.rearrange("p (a n) -> p a n", a=2),
                       ps[0:1, 4:6, :]), e, "c_v")
    c.ss(e.tensor_mul(tmp, m4, m4), e, "c_v")
    c.ss(e.tensor_sub(s4.rearrange("p (a n) -> p a n", a=2),
                      ps[0:1, 6:8, :],
                      tmp.rearrange("p (a n) -> p a n", a=2)), e, "c_v")
    ins = e.tensor_scalar(s4, s4, 0.0, EPS_LN, OP.max, OP.add)
    c.inc(ins, "c_v")
    c.mark(f"v_C1_{tau}", "c_v")
    c.wt(e, f"a_C_{tau}")
    r4 = t["y_t"][0:1, 0, :]
    ins = e.tensor_mul(t["rowb"][0:1, 1, :], m4, r4)
    c.inc(ins, "c_v")
    c.mark(f"v_C2_{tau}", "c_v")
    c.wt(e, f"t_bc4_{tau}")
    e.tensor_copy(t["r3_b"][:].rearrange("p (a n) -> p a n", a=2),
                  ps[:, 4:6, :])
    ins = e.tensor_copy(t["mr3_b"][:].rearrange("p (a n) -> p a n", a=2),
                        ps[:, 6:8, :])
    c.inc(ins, "c_v")
    c.mark(f"v_bc4_{tau}", "c_v")
    if B2_LEVEL < 4:
        return
    for ch in range(NCH):
        c.wt_raw(e, "s_x1r", 16 * (8 * tau + ch + 1))
        if B2_LEVEL >= 6:
            c.wt_raw(e, "s_y", 16 * (8 * tau + ch - 1))
        e.tensor_mul(t["sqf"][:, 0, :], t["ffn_t"][:, ch, :], t["r3_b"][:])
        e.tensor_sub(t["sqf"][:, 1, :], t["sqf"][:, 0, :], t["mr3_b"][:])
        ins = e.tensor_add(t["y_t"][:, ch % 2, :], t["x1r"][:, ch % 2, :],
                           t["sqf"][:, 1, :])
        c.inc(ins, "c_v")
        c.mark(f"v_yc_{tau}_{ch}", "c_v")



def build_nc(dbg=False, n_blocks=6):
    nc = bass.Bass("TRN2", target_bir_lowering=False)
    tn = {}
    tn["x_h"] = nc.dram_tensor("x", [D, T], F32, kind="ExternalInput")
    tn["wgin_h"] = nc.dram_tensor("wgin", [8, 128, 128], F32, kind="ExternalInput")
    tn["wga_h"] = nc.dram_tensor("wga", [8, 128, 128], F32, kind="ExternalInput")
    tn["bcin_h"] = nc.dram_tensor("bcin", [128, 8], F32, kind="ExternalInput")
    tn["bca_h"] = nc.dram_tensor("bca", [128, 8], F32, kind="ExternalInput")
    tn["acol_h"] = nc.dram_tensor("acol", [128, 8], F32, kind="ExternalInput")
    tn["w1t_h"] = nc.dram_tensor("w1t", [1024, 4096], F32, kind="ExternalInput")
    tn["w2t_h"] = nc.dram_tensor("w2t", [1024, 4096], F32, kind="ExternalInput")
    tn["h1t_h"] = nc.dram_tensor("h1t", [4096, 4096], F32, kind="ExternalInput")
    tn["h2t_h"] = nc.dram_tensor("h2t", [4096, 4096], F32, kind="ExternalInput")
    tn["w3t_h"] = nc.dram_tensor("w3t", [4096, 1024], F32, kind="ExternalInput")
    tn["s1c_h"] = nc.dram_tensor("s1c", [128, 32], F32, kind="ExternalInput")
    tn["s2c_h"] = nc.dram_tensor("s2c", [128, 32], F32, kind="ExternalInput")
    tn["hs1c_h"] = nc.dram_tensor("hs1c", [128, 32], F32, kind="ExternalInput")
    tn["hs2c_h"] = nc.dram_tensor("hs2c", [128, 32], F32, kind="ExternalInput")
    tn["s3c_h"] = nc.dram_tensor("s3c", [128, 8], F32, kind="ExternalInput")
    tn["x1sp"] = nc.dram_tensor("x1sp", [D, T], F32,
                                kind="ExternalOutput" if dbg else "Internal")
    tn["ln2sp"] = nc.dram_tensor("ln2sp", [D, 1024], BF16,
                                 kind="ExternalOutput" if dbg else "Internal")
    tn["y_h"] = nc.dram_tensor("y", [D, T], F32, kind="ExternalOutput")
    if dbg:
        tn["dbg_h"] = nc.dram_tensor("dbg_h", [D, T], F32, kind="ExternalOutput")
    if dbg == 2:
        for nm in ("dbg_ln1", "dbg_a", "dbg_nrm", "dbg_cum", "dbg_gx", "dbg_mu"):
            tn[nm] = nc.dram_tensor(nm, [D, T], F32, kind="ExternalOutput")

    tn["wg_in"] = nc.alloc_sbuf_tensor("wg_in", [128, 8, 128], F32)
    tn["wg_a"] = nc.alloc_sbuf_tensor("wg_a", [128, 8, 128], F32)
    tn["bcol_in"] = nc.alloc_sbuf_tensor("bcol_in", [128, 8], F32)
    tn["bcol_a"] = nc.alloc_sbuf_tensor("bcol_a", [128, 8], F32)
    tn["acol"] = nc.alloc_sbuf_tensor("acol_sb", [128, 8], F32)
    tn["nspcol"] = nc.alloc_sbuf_tensor("nspcol", [128, 8], F32)
    tn["tiny"] = nc.alloc_sbuf_tensor("tiny", [128, 32], F32)
    tn["s1c"] = nc.alloc_sbuf_tensor("s1c_sb", [128, 32], F32)
    tn["s2c"] = nc.alloc_sbuf_tensor("s2c_sb", [128, 32], F32)
    tn["hs1c"] = nc.alloc_sbuf_tensor("hs1c_sb", [128, 32], F32)
    tn["hs2c"] = nc.alloc_sbuf_tensor("hs2c_sb", [128, 32], F32)
    tn["s3c"] = nc.alloc_sbuf_tensor("s3c_sb", [128, 8], F32)
    tn["ones_row"] = nc.alloc_sbuf_tensor("ones_row", [1, 128], F32)
    tn["ones_rowb"] = nc.alloc_sbuf_tensor("ones_rowb", [1, 128], BF16)
    tn["cst_col"] = nc.alloc_sbuf_tensor("cst_col", [128, 1], F32)
    tn["wbuf"] = nc.alloc_sbuf_tensor("wbuf", [128, NBUF, 8, 256], BF16)
    tn["ln2_t"] = nc.alloc_sbuf_tensor("ln2_t", [128, 8, 1024], BF16)
    tn["ps"] = nc.alloc_psum_tensor("ps", [128, 8, 512], F32)

    semnames = ["s_c", "s_x", "s_x2", "s_x3", "s_w", "s_w2", "s_sp", "s_l2", "s_x1r", "s_y", "s_dbg",
                "s_d3v", "s_d3a", "c_v", "c_a", "c_p", "c_t", "c_w", "bar"]
    sems = {n: nc.alloc_semaphore(n) for n in semnames}

    ctx = Ctx()
    ctx.sems = sems
    ctx.dbg = dbg

    def emit_all(c):
        blocks = []
        blocks.append({
            "sync": lambda e: em_S0_sync(c, e),
            "vector": lambda e: em_S0_vector(c, e),
            "scalar": lambda e: em_S0_scalar(c, e),
            "gpsimd": lambda e: None,
            "tensor": lambda e: None,
        })
        blocks.append({
            "sync": lambda e: em_A_sync(c, e),
            "gpsimd": lambda e: em_A_pool(c, e, dbg),
            "scalar": lambda e: em_A_scalar(c, e),
            "vector": lambda e: em_A_vector(c, e),
            "tensor": lambda e: em_A_tensor(c, e),
        })
        for tau in (0, 1):
            blocks.append({
                "sync": lambda e, tau=tau: em_B1_sync(c, e, tau),
                "gpsimd": lambda e, tau=tau: em_B1_gpsimd(c, e, tau),
                "tensor": lambda e, tau=tau: em_B1_tensor(c, e, tau),
                "scalar": lambda e, tau=tau: em_B1_scalar(c, e, tau),
                "vector": lambda e, tau=tau: em_B1_vector(c, e, tau),
            })
            blocks.append({
                "sync": lambda e, tau=tau: em_B2_sync(c, e, tau),
                "gpsimd": lambda e, tau=tau: em_B2_gpsimd(c, e, tau),
                "tensor": lambda e, tau=tau: em_B2_tensor(c, e, tau),
                "scalar": lambda e, tau=tau: em_B2_scalar(c, e, tau),
                "vector": lambda e, tau=tau: em_B2_vector(c, e, tau),
            })
        return blocks

    mock = _MockEng()
    mockt = _MockTensor()
    ctx.t = {k: mockt for k in list(tn.keys()) + [
        "xc2", "pA0", "pB0", "pC0", "pD0", "pA1", "pB1", "pC1", "pD1",
        "T0", "T1", "T2", "T3", "T4", "T5",
        "r_b", "mr_b", "h_all", "l2c2", "ln2_t", "h1_t", "h2_t", "t1_buf",
        "ffn_t", "sqf", "x1r", "y_t", "r3_b", "mr3_b", "rowb"]}
    for blk in emit_all(ctx)[:n_blocks]:
        for en in ("sync", "gpsimd", "scalar", "vector", "tensor"):
            blk[en](mock)
    missing = ctx.refs - set(ctx.M.keys())
    assert not missing, f"unresolved marks: {sorted(missing)}"

    ctx.counting = False
    ctx.t = tn
    ctx._ssn = 0
    blocks = emit_all(ctx)
    engs = ("sync", "gpsimd", "scalar", "vector", "tensor")

    with nc.Block() as block:
        def chain(blk):
            for en in engs:
                getattr(block, en)(lambda eng, en=en, blk=blk: blk[en](eng))

        chain(blocks[0])
        if n_blocks >= 2:
            sa = _ScopeA(nc, tn)
            sa.__enter__()
            chain(blocks[1])
            sa.__exit__(None, None, None)
        if n_blocks >= 3:
            _ScopeBOuter(nc, tn).__enter__()
            for tau in (0, 1):
                if n_blocks >= 2 + 2 * tau + 1:
                    b1 = _ScopeB1(nc, tn)
                    b1.__enter__()
                    chain(blocks[2 + 2 * tau])
                    b1.__exit__(None, None, None)
                if n_blocks >= 3 + 2 * tau + 1:
                    b2 = _ScopeB2(nc, tn)
                    b2.__enter__()
                    chain(blocks[3 + 2 * tau])
                    b2.__exit__(None, None, None)
    return nc


class _ScopeA:
    def __init__(self, nc, tn):
        self.nc, self.tn = nc, tn
        self.keys = []

    def __enter__(self):
        nc, tn = self.nc, self.tn
        self.cms = []
        def add(key, shape, dt):
            cm = nc.sbuf_tensor(key, shape, dt)
            tn[key] = cm.__enter__()
            self.cms.append(cm)
            self.keys.append(key)
        add("xc2", [128, 2, T], F32)
        for p in (0, 1):
            for nm in "ABCD":
                add(f"p{nm}{p}", [128, T], F32)
        add("r_b", [128, T], F32)
        add("mr_b", [128, T], F32)
        add("h_all", [128, 8, T], F32)
        add("l2c2", [128, 2, 1024], BF16)
        for alias, real in (("T0", "pB0"), ("T1", "pC0"), ("T2", "pD0"),
                            ("T3", "pB1"), ("T4", "pC1"), ("T5", "pD1")):
            tn[alias] = tn[real]
        return self

    def __exit__(self, *a):
        for cm in reversed(self.cms):
            cm.__exit__(None, None, None)
        return False


class _ScopeBOuter:

    def __init__(self, nc, tn):
        self.nc, self.tn = nc, tn

    def __enter__(self):
        nc, tn = self.nc, self.tn
        tn["h1_t"] = nc.alloc_sbuf_tensor("h1_t", [128, 32, 1024], BF16)
        tn["t1_buf"] = nc.alloc_sbuf_tensor("t1_buf", [128, 2, 1024], F32)
        return self

    def __exit__(self, *a):
        return False


class _ScopeB1:
    _n = [0]

    def __init__(self, nc, tn):
        self.nc, self.tn = nc, tn

    def __enter__(self):
        self._n[0] += 1
        self.cm = self.nc.sbuf_tensor(f"h2_t_{self._n[0]}", [128, 32, 1024], BF16)
        self.tn["h2_t"] = self.cm.__enter__()
        return self

    def __exit__(self, *a):
        self.cm.__exit__(None, None, None)
        return False


class _ScopeB2:
    _n = [0]

    def __init__(self, nc, tn):
        self.nc, self.tn = nc, tn

    def __enter__(self):
        nc, tn = self.nc, self.tn
        self._n[0] += 1
        self.cms = []
        for key, shape, dt in [("ffn_t", [128, 8, 1024], F32),
                               ("sqf", [128, 2, 1024], F32),
                               ("x1r", [128, 2, 1024], F32),
                               ("y_t", [128, 2, 1024], F32),
                               ("r3_b", [128, 1024], F32),
                               ("mr3_b", [128, 1024], F32),
                               ("rowb", [1, 2, 1024], BF16)]:
            cm = nc.sbuf_tensor(f"{key}_{self._n[0]}", shape, dt)
            tn[key] = cm.__enter__()
            self.cms.append(cm)
        return self

    def __exit__(self, *a):
        for cm in reversed(self.cms):
            cm.__exit__(None, None, None)
        return False



_NC_CACHE = {}


def _get_nc(dbg=False):
    if dbg not in _NC_CACHE:
        _NC_CACHE[dbg] = build_nc(dbg)
    return _NC_CACHE[dbg]


def _layout_inputs(inputs):
    f = np.ascontiguousarray
    x = np.asarray(inputs["L0_attn_x"], dtype=np.float32)
    common = {
        "wgin": f(np.asarray(inputs["L0_attn_w_in"], np.float32)),
        "wga": f(np.asarray(inputs["L0_attn_w_a"], np.float32)),
        "bcin": f(np.asarray(inputs["L0_attn_b_in"], np.float32).T),
        "bca": f(np.asarray(inputs["L0_attn_b_a"], np.float32).T),
        "acol": f(np.asarray(inputs["L0_attn_a_param"], np.float32)
                  .reshape(8, 128).T),
        "w1t": f(np.asarray(inputs["L0_ffn_w1_w"], np.float32).T),
        "w2t": f(np.asarray(inputs["L0_ffn_w2_w"], np.float32).T),
        "h1t": f(np.asarray(inputs["L0_ffn_h1_w"], np.float32).T),
        "h2t": f(np.asarray(inputs["L0_ffn_h2_w"], np.float32).T),
        "w3t": f(np.asarray(inputs["L0_ffn_w3_w"], np.float32).T),
        "s1c": f(np.asarray(inputs["L0_ffn_w1_s"], np.float32)
                 .reshape(32, 128).T),
        "s2c": f(np.asarray(inputs["L0_ffn_w2_s"], np.float32)
                 .reshape(32, 128).T),
        "hs1c": f(np.asarray(inputs["L0_ffn_h1_s"], np.float32)
                  .reshape(32, 128).T),
        "hs2c": f(np.asarray(inputs["L0_ffn_h2_s"], np.float32)
                  .reshape(32, 128).T),
        "s3c": f(np.asarray(inputs["L0_ffn_w3_s"], np.float32)
                 .reshape(8, 128).T),
    }
    in_maps = []
    for b in range(8):
        m = dict(common)
        m["x"] = f(x[b].T)
        in_maps.append(m)
    return in_maps


def kernel(**inputs):
    nc = _get_nc(dbg=False)
    in_maps = _layout_inputs(inputs)
    res = run_bass_kernel_spmd(nc, in_maps, core_ids=list(range(8)))
    out = np.stack([np.asarray(r["y"]).T for r in res.results], axis=0)
    return np.ascontiguousarray(out, dtype=np.float32)


# version 24 (final)
# speedup vs baseline: 1.0331x; 1.0005x over previous
import numpy as np
from contextlib import ExitStack

import concourse.bass as bass
import concourse.mybir as mybir
from concourse.bass_utils import run_bass_kernel_spmd

F32 = mybir.dt.float32
BF16 = mybir.dt.bfloat16
AF = mybir.ActivationFunctionType
OP = mybir.AluOpType

D = 1024
T = 2048
NCH = 8
NBUF = 4
EPS_LN = 1e-5
MIN_DECAY = 1e-3
SCAN_EPS = 1e-7
CLAMP_A = 1.0 - 1e-5


def _weight_loads():
    WL = []
    for tau in (0, 1):
        for mg in range(16):
            for part in (0, 1):
                WL.append(("s1", tau, mg, part))
        for st in ("s2", "s3"):
            for mg in range(16):
                for kg in range(4):
                    WL.append((st, tau, mg, kg))
        for mg in range(4):
            for kg in range(4):
                WL.append(("s4", tau, mg, kg))
    G = {k: g for g, k in enumerate(WL)}
    return WL, G


WL, GIDX = _weight_loads()
NLOADS = len(WL)
QID = [0 for g in range(NLOADS)]
QORD = []
_cnt = [0, 0]
for _g in range(NLOADS):
    QORD.append(_cnt[QID[_g]])
    _cnt[QID[_g]] += 1
B2_LEVEL = 6
_L_A_END = NBUF
_L_B1_0_END = GIDX[("s4", 0, 0, 0)]
_L_B2_0_END = GIDX[("s1", 1, 0, 0)]
_L_B1_1_END = GIDX[("s4", 1, 0, 0)]
_L_B2_1_END = NLOADS


class _MockEng:
    def __getattr__(self, name):
        return lambda *a, **k: None


class _MockTensor:
    def __getitem__(self, *a):
        return self

    def rearrange(self, *a, **k):
        return self

    def ap(self):
        return self


class Ctx:

    def __init__(self):
        self.counting = True
        self.cnt = {}
        self.M = {}
        self.refs = set()
        self.sems = None
        self.t = None
        self._ssn = 0

    def inc(self, ins, sem, n=1):
        amt = 16 * n if sem.startswith("s_") else n
        if self.counting:
            self.cnt[sem] = self.cnt.get(sem, 0) + amt
        else:
            ins.then_inc(self.sems[sem], amt)

    def mark(self, name, sem):
        if self.counting:
            assert name not in self.M, f"duplicate mark {name}"
            self.M[name] = (sem, self.cnt.get(sem, 0))

    def wt(self, eng, name):
        if self.counting:
            self.refs.add(name)
            return
        semname, v = self.M[name]
        if v > 0:
            eng.wait_ge(self.sems[semname], v)

    def wt_raw(self, eng, sem, v):
        if not self.counting and v > 0:
            eng.wait_ge(self.sems[sem], v)

    def ss(self, ins, eng, sem):
        name = f"_ss_{sem}_{self._ssn}"
        self._ssn += 1
        self.inc(ins, sem)
        self.mark(name, sem)
        self.wt(eng, name)



N_CONST = 10
C1 = float(np.log1p(-1e-5))
LCL = float(np.log(1e-7))


def em_S0_sync(c, e):
    t = c.t
    for name, dst in [("wgin_h", "wg_in"), ("wga_h", "wg_a"),
                      ("bcin_h", "bcol_in"), ("bca_h", "bcol_a"),
                      ("acol_h", "acol"), ("s1c_h", "s1c"), ("s2c_h", "s2c"),
                      ("hs1c_h", "hs1c"), ("hs2c_h", "hs2c"), ("s3c_h", "s3c")]:
        if name in ("wgin_h", "wga_h"):
            src = t[name].ap().rearrange("h i j -> i h j")
        else:
            src = t[name].ap()
        c.inc(e.dma_start(t[dst][:], src), "s_c")


def em_S0_vector(c, e):
    t = c.t
    e.memset(t["ones_row"][:], 1.0)
    e.memset(t["ones_rowb"][:], 1.0)
    ins = e.memset(t["cst_col"][:], 1.0 / D)
    c.inc(ins, "c_v")
    c.mark("v_ms", "c_v")
    c.wt_raw(e, "s_c", 16 * N_CONST)
    tr = t["tiny"][:, 0:8]
    tb = t["tiny"][:, 8:16]
    tl = t["tiny"][:, 24:32]
    ins = e.tensor_single_scalar(tr, t["acol"][:], 0.0, OP.max)
    c.inc(ins, "c_v")
    c.mark("v_pro1", "c_v")
    c.wt(e, "a_pro1")
    c.ss(e.scalar_tensor_tensor(tb, tl, MIN_DECAY, tr, OP.add, OP.add), e, "c_v")
    ins = e.tensor_single_scalar(t["nspcol"][:], tb, -1.0, OP.mult)
    c.inc(ins, "c_v")
    c.mark("v_pro2", "c_v")


def em_S0_scalar(c, e):
    t = c.t
    tb = t["tiny"][:, 8:16]
    te = t["tiny"][:, 16:24]
    tl = t["tiny"][:, 24:32]
    c.wt_raw(e, "s_c", 16 * N_CONST)
    c.ss(e.activation(tb, t["acol"][:], AF.Abs), e, "c_a")
    c.ss(e.activation(te, tb, AF.Exp, scale=-1.0), e, "c_a")
    ins = e.activation(tl, te, AF.Ln, bias=1.0)
    c.inc(ins, "c_a")
    c.mark("a_pro1", "c_a")


def em_S0_gpsimd(c, e):
    pass


def em_S0_tensor(c, e):
    pass




def _pt(c, name, ch):
    return c.t[f"p{name}{ch % 2}"]


def _xload(c, e, ch, sem):
    src = c.t["x_h"].ap()[ch * 128:(ch + 1) * 128, :]
    c.inc(e.dma_start(c.t["xc2"][:, ch % 2, :], src), sem)


XQ = {0: ("s_x", 1), 3: ("s_x", 2), 6: ("s_x", 3),
      1: ("s_x2", 1), 4: ("s_x2", 2), 7: ("s_x2", 3),
      2: ("s_x3", 1), 5: ("s_x3", 2)}


def _wt_x1r(c, e, ch):
    sem, o = XQ[ch]
    c.wt_raw(e, sem, 16 * o)


def em_A_sync(c, e):
    for ch in (0, 3, 6):
        if ch >= 2:
            c.wt(e, f"t_A0st_{ch - 2}")
        _xload(c, e, ch, "s_x")
    for ch in range(8):
        c.wt(e, [f"t_A0st_{6 + ch}", f"p_xr_{ch - 2}"][ch >= 2])
        _xload(c, e, ch, "s_x")
    for ch in range(8):
        c.wt(e, [f"p_xr_{6 + ch}", f"v_x1_{ch - 2}"][ch >= 2])
        _xload(c, e, ch, "s_x")


def em_A_pool(c, e, dbg=False):
    t = c.t
    dbg2 = dbg == 2

    def dump(ch, name, tile, markname):
        if not dbg2:
            return
        c.wt(e, markname)
        c.inc(e.dma_start(t[name].ap()[ch * 128:(ch + 1) * 128, :],
                          _pt(c, tile, ch)[:]), "s_d3v")
        c.mark(f"d3_{name}_{ch}", "s_d3v")

    def gated(ch, hh):
        sl = slice(hh * 1024, (hh + 1) * 1024)
        if hh == 0:
            dump(ch, "dbg_ln1", "B", f"v_ln1s_{ch}")
            c.wt(e, f"a_gxh0_{ch}")
            c.wt(e, f"v_ln1s_{ch}")
        else:
            c.wt(e, f"a_gx_{ch}")
        A = _pt(c, "A", ch)
        ins = e.tensor_mul(A[:, sl], A[:, sl], _pt(c, "B", ch)[:, sl])
        c.inc(ins, "c_p")
        c.mark([f"p_g0_{ch}", f"p_gated_{ch}"][hh], "c_p")
        if hh == 1:
            dump(ch, "dbg_gx", "A", f"p_gated_{ch}")

    def xt(ch):
        dump(ch, "dbg_mu", "C", f"a_mult_{ch}")
        dump(ch, "dbg_nrm", "D", f"v_normed_{ch}")
        c.wt(e, f"v_normed_{ch}")
        ins = e.tensor_scalar(_pt(c, "C", ch)[:], _pt(c, "D", ch)[:],
                              SCAN_EPS, 1.0, OP.add, OP.mult)
        c.inc(ins, "c_p")
        c.mark(f"p_xt_{ch}", "c_p")

    def xr(ch):
        c.wt_raw(e, "s_x", 16 * (4 + ch))
        if ch < 2:
            c.wt(e, "v_A0bc")
        else:
            c.wt(e, f"v_normed_{ch - 2}")
        ins = e.tensor_mul(_pt(c, "A", ch)[:], t["xc2"][:, ch % 2, :],
                           t["r_b"][:])
        c.inc(ins, "c_p")
        c.mark(f"p_xr_{ch}", "c_p")

    c.wt(e, "t_A0st_0")
    _xload(c, e, 2, "s_x3")
    c.wt(e, "t_A0st_3")
    _xload(c, e, 5, "s_x3")
    xr(0)
    xr(1)
    for q in range(NCH // 2):
        c0, c1 = 2 * q, 2 * q + 1
        gated(c0, 0)
        gated(c1, 0)
        gated(c0, 1)
        gated(c1, 1)
        if dbg2:
            dump(c0, "dbg_a", "C", f"v_lac_{c0}")
            dump(c0, "dbg_cum", "B", f"v_lcum_{c0}")
            dump(c1, "dbg_a", "C", f"v_lac_{c1}")
            dump(c1, "dbg_cum", "B", f"v_lcum_{c1}")
        xt(c0)
        if c1 + 1 < NCH:
            xr(c1 + 1)
        xt(c1)
        if c1 + 2 < NCH:
            xr(c1 + 2)
    if dbg:
        for ch in range(NCH):
            c.wt(e, f"v_h_{ch}")
            c.inc(e.dma_start(t["dbg_h"].ap()[ch * 128:(ch + 1) * 128, :],
                              t["h_all"][:, ch, :]), "s_dbg")
    for ch in range(NCH):
        if ch == 0:
            c.wt(e, "v_A2bc")
        elif ch >= 2:
            c.wt(e, f"v_x1_{ch - 2}")
        ins = e.tensor_mul(_pt(c, "B", ch)[:], t["h_all"][:, ch, :],
                           t["r_b"][:])
        c.inc(ins, "c_p")
        c.mark(f"p_x1m_{ch}", "c_p")
    for ch in range(NCH):
        c.wt(e, f"v_x1_{ch}")
        c.inc(e.dma_start(t["x1sp"].ap()[ch * 128:(ch + 1) * 128, :],
                          t["h_all"][:, ch, :]), "s_sp")
    def spill_ln2(ch):
        c.wt(e, f"v_l2c_{ch}")
        c.inc(e.dma_start(t["ln2sp"].ap()[ch * 128:(ch + 1) * 128, :],
                          t["l2c2"][:, ch % 2, :]), "s_sp")

    for ch in range(NCH):
        if ch == 0:
            c.wt(e, "v_A3bc")
        elif ch >= 2:
            c.wt(e, f"v_l2c_{ch - 2}")
        ins = e.tensor_mul(_pt(c, "B", ch)[:], t["h_all"][:, ch, :],
                           t["r_b"][:])
        c.inc(ins, "c_p")
        c.mark(f"p_l2m_{ch}", "c_p")
        if ch == 0:
            for g in range(_L_A_END):
                _em_wload(c, e, g)
        if ch >= 1:
            spill_ln2(ch - 1)
    spill_ln2(7)


def _wsrc(c, g):
    t = c.t
    st, tau, mg, x = WL[g]
    if st == "s1":
        wname = "w1t_h" if x == 0 else "w2t_h"
        src = c.t[wname].ap()[0:1024, mg * 256:(mg + 1) * 256]
    elif st == "s2":
        src = t["h1t_h"].ap()[x * 1024:(x + 1) * 1024, mg * 256:(mg + 1) * 256]
    elif st == "s3":
        src = t["h2t_h"].ap()[x * 1024:(x + 1) * 1024, mg * 256:(mg + 1) * 256]
    else:
        src = t["w3t_h"].ap()[x * 1024:(x + 1) * 1024, mg * 256:(mg + 1) * 256]
    return src.rearrange("(a p) n -> p a n", p=128)


def _em_wload(c, e, g):
    if g >= NBUF:
        c.wt_raw(e, "c_w", g - (NBUF - 1))
    ins = e.dma_start(c.t["wbuf"][:, g % NBUF, :, :], _wsrc(c, g))
    c.inc(ins, "s_w2" if QID[g] else "s_w")


def _wt_w(c, e, g):
    c.wt_raw(e, "s_w2" if QID[g] else "s_w", 16 * (QORD[g] + 1))


def _rows(c, name, n2048=True):
    t = c.t
    return t[name][0:1, :]


def _em_rows_dve_1(c, e, pref):
    t = c.t
    r0, r1, r2, r3 = (t["T0"][0:1, :], t["T1"][0:1, :],
                      t["T2"][0:1, :], t["T3"][0:1, :])
    c.ss(e.tensor_copy(r0.rearrange("p (a n) -> p a n", a=4),
                       t["ps"][0:1, 0:4, :]), e, "c_v")
    c.ss(e.tensor_copy(r1.rearrange("p (a n) -> p a n", a=4),
                       t["ps"][0:1, 4:8, :]), e, "c_v")
    c.ss(e.tensor_mul(r2, r0, r0), e, "c_v")
    c.ss(e.tensor_sub(r3, r1, r2), e, "c_v")
    c.ss(e.tensor_single_scalar(r2, r3, 0.0, OP.max), e, "c_v")
    ins = e.tensor_scalar_add(r3, r2, EPS_LN)
    c.inc(ins, "c_v")
    c.mark(f"v_{pref}r1", "c_v")


def _em_rows_act(c, e, pref):
    t = c.t
    r2, r3, r4 = t["T2"][0:1, :], t["T3"][0:1, :], t["T4"][0:1, :]
    c.wt(e, f"v_{pref}r1")
    c.ss(e.activation(r2, r3, AF.Ln), e, "c_a")
    ins = e.activation(r4, r2, AF.Exp, scale=-0.5)
    c.inc(ins, "c_a")
    c.mark(f"a_{pref}r", "c_a")


def _em_rows_dve_2(c, e, pref):
    t = c.t
    r0, r4, r5 = t["T0"][0:1, :], t["T4"][0:1, :], t["T5"][0:1, :]
    c.wt(e, f"a_{pref}r")
    ins = e.tensor_mul(r5, r0, r4)
    c.inc(ins, "c_v")
    c.mark(f"v_{pref}r2", "c_v")


def _em_bcast_pe(c, e, pref):
    t = c.t
    c.wt(e, f"v_{pref}r2")
    r4, r5 = t["T4"][0:1, :], t["T5"][0:1, :]
    ins = None
    for n in range(4):
        ins = e.matmul(t["ps"][:, n, :], t["ones_row"][:],
                       r4[0:1, n * 512:(n + 1) * 512], start=True, stop=True)
    for n in range(4):
        ins = e.matmul(t["ps"][:, 4 + n, :], t["ones_row"][:],
                       r5[0:1, n * 512:(n + 1) * 512], start=True, stop=True)
    c.inc(ins, "c_t")
    c.mark(f"t_{pref}bc", "c_t")


def _em_bcast_evac_dve(c, e, pref):
    t = c.t
    c.wt(e, f"t_{pref}bc")
    ins = e.tensor_copy(t["r_b"][:].rearrange("p (a n) -> p a n", a=4),
                        t["ps"][:, 0:4, :])
    c.inc(ins, "c_v")
    c.mark(f"v_{pref}bc03", "c_v")
    ins = e.tensor_copy(t["mr_b"][:].rearrange("p (a n) -> p a n", a=4),
                        t["ps"][:, 4:8, :])
    c.inc(ins, "c_v")
    c.mark(f"v_{pref}bc", "c_v")


def em_A_scalar(c, e):
    t = c.t
    ps = t["ps"]
    for ch in range(NCH):
        if ch in (0, 3, 6):
            och = ch + 1
            if och >= 4:
                c.wt(e, f"t_A0st_{och - 2}")
            _xload(c, e, och, "s_x2")
        _wt_x1r(c, e, ch)
        if ch >= 2:
            c.wt(e, f"t_A0st_{ch - 2}")
        ins = e.activation(_pt(c, "A", ch)[:], t["xc2"][:, ch % 2, :],
                           AF.Square)
        c.inc(ins, "c_a")
        c.mark(f"a_A0sq_{ch}", "c_a")
    _em_rows_act(c, e, "A0")
    def sig(ch, hh, which):
        bk = 4 * (ch % 2) + (0 if which == "gx" else 2)
        dst = _pt(c, "A" if which == "gx" else "C", ch)
        sl = slice(hh * 1024, (hh + 1) * 1024)
        if which == "gx":
            c.wt(e, f"t_g{hh}_{ch}")
        elif hh == 0 and ch >= 2:
            c.wt(e, f"v_h_{ch - 2}")
        col = t["bcol_in" if which == "gx" else "bcol_a"][:, ch:ch + 1]
        ins = e.activation(dst[:, sl].rearrange("p (a n) -> p a n", a=2),
                           ps[:, bk:bk + 2, :], AF.Sigmoid, bias=col)
        c.inc(ins, "c_a")
        mk = {("gx", 0): f"a_gxh0_{ch}", ("ga", 0): f"a_gah0_{ch}",
              ("gx", 1): f"a_gx_{ch}", ("ga", 1): f"a_ga_{ch}"}[(which, hh)]
        c.mark(mk, "c_a")

    for q in range(NCH // 2):
        pair = (2 * q, 2 * q + 1)
        for ch in pair:
            sig(ch, 0, "gx")
        for ch in pair:
            sig(ch, 0, "ga")
        for ch in pair:
            sig(ch, 1, "gx")
        for ch in pair:
            sig(ch, 1, "ga")
        for ch in pair:
            c.wt(e, f"v_lac_{ch}")
            if ch >= 2:
                c.wt(e, f"p_xt_{ch - 2}")
            ins = e.activation(_pt(c, "D", ch)[:], _pt(c, "C", ch)[:],
                               AF.Exp, scale=2.0)
            c.inc(ins, "c_a")
            c.mark(f"a_aa_{ch}", "c_a")
        for ch in pair:
            c.wt(e, f"v_lcum_{ch}")
            if c.dbg == 2:
                c.wt(e, f"d3_dbg_a_{ch}")
            ins = e.activation(_pt(c, "C", ch)[:], _pt(c, "D", ch)[:],
                               AF.Sqrt, scale=-1.0, bias=1.0)
            c.inc(ins, "c_a")
            c.mark(f"a_mult_{ch}", "c_a")
        for ch in pair:
            B, Dd = _pt(c, "B", ch), _pt(c, "D", ch)
            c.wt(e, f"v_m1_{ch}")
            e.activation(B[:, 0:1], Dd[:, 0:1], AF.Copy, scale=0.0)
            ins = e.activation(B[:, 1:2], Dd[:, 0:1], AF.Copy)
            c.inc(ins, "c_a")
            c.mark(f"a_ed_{ch}", "c_a")
            c.wt(e, f"v_d_{ch}")
            ins = e.activation(t["h_all"][:, ch, :], B[:], AF.Exp)
            c.inc(ins, "c_a")
            c.mark(f"a_at_{ch}", "c_a")
    for ch in range(NCH):
        c.wt(e, f"v_h_{ch}")
        if ch >= 2:
            c.wt(e, f"t_A2st_{ch - 2}")
        else:
            c.wt(e, f"v_h_{6 + ch}")
        ins = e.activation(_pt(c, "A", ch)[:], t["h_all"][:, ch, :],
                           AF.Square)
        c.inc(ins, "c_a")
        c.mark(f"a_hsq_{ch}", "c_a")
    _em_rows_act(c, e, "A2")
    for ch in range(NCH):
        c.wt(e, f"v_x1_{ch}")
        if ch >= 2:
            c.wt(e, f"t_lnst_{ch - 2}")
        else:
            c.wt(e, f"t_A2st_{6 + ch}")
        ins = e.activation(_pt(c, "A", ch)[:], t["h_all"][:, ch, :],
                           AF.Square)
        c.inc(ins, "c_a")
        c.mark(f"a_x1sq_{ch}", "c_a")
    _em_rows_act(c, e, "A3")


def em_A_vector(c, e):
    t = c.t
    c.wt(e, "t_A0st_7")
    _em_rows_dve_1(c, e, "A0")
    _em_rows_dve_2(c, e, "A0")
    _em_bcast_evac_dve(c, e, "A0")
    def ln1s(ch):
        A, B = _pt(c, "A", ch), _pt(c, "B", ch)
        c.wt(e, f"p_xr_{ch}")
        if ch >= 2:
            c.wt(e, f"a_at_{ch - 2}")
        ins = e.tensor_sub(B[:], A[:], t["mr_b"][:])
        c.inc(ins, "c_v")
        c.mark(f"v_ln1s_{ch}", "c_v")

    def lac(ch):
        Cc = _pt(c, "C", ch)
        c.wt(e, f"a_ga_{ch}")
        ins = e.tensor_scalar(Cc[:], Cc[:],
                              t["nspcol"][:, ch:ch + 1], C1, OP.mult, OP.min)
        c.inc(ins, "c_v")
        c.mark(f"v_lac_{ch}", "c_v")

    def lcum(ch):
        B, C = _pt(c, "B", ch), _pt(c, "C", ch)
        c.wt(e, f"t_g1_{ch}")
        c.wt(e, f"p_gated_{ch}")
        if c.dbg == 2:
            c.wt(e, f"d3_dbg_ln1_{ch}")
        ins = e.tensor_tensor_scan(B[:], C[:], C[:], 0.0, OP.add, OP.bypass)
        c.inc(ins, "c_v")
        c.mark(f"v_lcum_{ch}", "c_v")

    def m1(ch):
        c.wt(e, f"a_mult_{ch}")
        ins = e.tensor_scalar(_pt(c, "D", ch)[:], _pt(c, "B", ch)[:],
                              0.0, LCL, OP.min, OP.max)
        c.inc(ins, "c_v")
        c.mark(f"v_m1_{ch}", "c_v")

    def d_normed(ch):
        A, B, C, D = (_pt(c, "A", ch), _pt(c, "B", ch),
                      _pt(c, "C", ch), _pt(c, "D", ch))
        if c.dbg == 2:
            c.wt(e, f"d3_dbg_cum_{ch}")
        ins = e.tensor_sub(B[:, 2:T], D[:, 1:T - 1], D[:, 0:T - 2])
        c.inc(ins, "c_v")
        c.mark(f"v_d_{ch}", "c_v")
        c.wt(e, f"a_ed_{ch}")
        ins = e.tensor_mul(D[:], A[:], C[:])
        c.inc(ins, "c_v")
        c.mark(f"v_normed_{ch}", "c_v")

    ln1s(0)
    ln1s(1)
    for q in range(NCH // 2):
        c0, c1 = 2 * q, 2 * q + 1
        lac(c0)
        lac(c1)
        lcum(c0)
        lcum(c1)
        m1(c0)
        m1(c1)
        d_normed(c0)
        d_normed(c1)
        if c1 + 2 < NCH:
            ln1s(c1 + 1)
            ln1s(c1 + 2)
        _em_hscan(c, e, c0)
        _em_hscan(c, e, c1)
    c.wt(e, "t_A2st_7")
    _em_rows_dve_1(c, e, "A2")
    _em_rows_dve_2(c, e, "A2")
    _em_bcast_evac_dve(c, e, "A2")
    for ch in range(NCH):
        p = ch % 2
        c.wt_raw(e, "s_x", 16 * (12 + ch))
        if getattr(c, "dbg", False):
            c.wt_raw(e, "s_dbg", 16 * (ch + 1))
        B, Cc = _pt(c, "B", ch), _pt(c, "C", ch)
        c.wt(e, f"p_x1m_{ch}")
        e.tensor_sub(Cc[:], B[:], t["mr_b"][:])
        ins = e.tensor_add(t["h_all"][:, ch, :], t["xc2"][:, p, :], Cc[:])
        c.inc(ins, "c_v")
        c.mark(f"v_x1_{ch}", "c_v")
    c.wt(e, "t_lnst_7")
    _em_rows_dve_1(c, e, "A3")
    _em_rows_dve_2(c, e, "A3")
    _em_bcast_evac_dve(c, e, "A3")
    for ch in range(NCH):
        if ch >= 2:
            c.wt_raw(e, "s_sp", 16 * (7 + ch))
        B = _pt(c, "B", ch)
        c.wt(e, f"p_l2m_{ch}")
        ins = e.tensor_sub(t["ln2_t"][:, ch, :], B[:, 0:1024],
                           t["mr_b"][:, 0:1024])
        c.inc(ins, "c_v")
        c.mark(f"v_ln2_{ch}", "c_v")
        ins = e.tensor_sub(t["l2c2"][:, ch % 2, :], B[:, 1024:T],
                           t["mr_b"][:, 1024:T])
        c.inc(ins, "c_v")
        c.mark(f"v_l2c_{ch}", "c_v")


def _em_hscan(c, e, ch):
    c.wt(e, f"a_at_{ch}")
    c.wt(e, f"p_xt_{ch}")
    h = c.t["h_all"][:, ch, :]
    ins = e.tensor_tensor_scan(h, h, _pt(c, "C", ch)[:], 0.0, OP.mult, OP.add)
    c.inc(ins, "c_v")
    c.mark(f"v_h_{ch}", "c_v")


def em_A_tensor(c, e):
    t = c.t
    ps = t["ps"]

    def stats8(ch, markpref, rhs_x):
        ins = None
        for n in range(4):
            ins = e.matmul(ps[0:1, n, :], t["cst_col"][:],
                           rhs_x[:, n * 512:(n + 1) * 512],
                           start=(ch == 0), stop=(ch == 7),
                           skip_group_check=True)
        sq = _pt(c, "A", ch)
        for n in range(4):
            ins = e.matmul(ps[0:1, 4 + n, :], t["cst_col"][:],
                           sq[:, n * 512:(n + 1) * 512],
                           start=(ch == 0), stop=(ch == 7),
                           skip_group_check=True)
        c.inc(ins, "c_t")
        c.mark(f"{markpref}_{ch}", "c_t")

    c.wt(e, "v_ms")
    for ch in range(NCH):
        _wt_x1r(c, e, ch)
        c.wt(e, f"a_A0sq_{ch}")
        stats8(ch, "t_A0st", t["xc2"][:, ch % 2, :])
    _em_bcast_pe(c, e, "A0")
    def gates_half(ch, hh):
        bk = 4 * (ch % 2)
        B = _pt(c, "B", ch)
        if hh == 0:
            c.wt(e, f"v_ln1s_{ch}")
            if ch >= 2:
                c.wt(e, f"a_ga_{ch - 2}")
            else:
                c.wt(e, "v_A0bc")
        else:
            c.wt(e, f"a_gah0_{ch}")
        ins = None
        for n in range(2):
            nn = 2 * hh + n
            ins = e.matmul(ps[:, bk + n, :], t["wg_in"][:, ch, :],
                           B[:, nn * 512:(nn + 1) * 512],
                           start=True, stop=True)
        for n in range(2):
            nn = 2 * hh + n
            ins = e.matmul(ps[:, bk + 2 + n, :], t["wg_a"][:, ch, :],
                           B[:, nn * 512:(nn + 1) * 512],
                           start=True, stop=True)
        c.inc(ins, "c_t")
        c.mark(f"t_g{hh}_{ch}", "c_t")

    for q in range(NCH // 2):
        for hh in (0, 1):
            gates_half(2 * q, hh)
            gates_half(2 * q + 1, hh)
    for ch in range(NCH):
        c.wt(e, f"a_hsq_{ch}")
        if ch == 0:
            c.wt(e, "a_ga_7")
        stats8(ch, "t_A2st", t["h_all"][:, ch, :])
    _em_bcast_pe(c, e, "A2")
    for ch in range(NCH):
        c.wt(e, f"a_x1sq_{ch}")
        if ch == 0:
            c.wt(e, "v_A2bc")
        stats8(ch, "t_lnst", t["h_all"][:, ch, :])
    _em_bcast_pe(c, e, "A3")



def em_B1_sync(c, e, tau):
    em_B1_sync_wloads(c, e, tau)


def em_B1_gpsimd(c, e, tau):
    lo = _L_A_END if tau == 0 else _L_B2_0_END
    hi = _L_B1_0_END if tau == 0 else _L_B1_1_END
    for g in range(lo, hi):
        if QID[g] == 0:
            _em_wload(c, e, g)


def em_B1_sync_wloads(c, e, tau):
    lo = _L_A_END if tau == 0 else _L_B2_0_END
    hi = _L_B1_0_END if tau == 0 else _L_B1_1_END
    for g in range(lo, hi):
        if QID[g] == 1:
            _em_wload(c, e, g)


def em_B1_tensor(c, e, tau):
    t = c.t
    ps = t["ps"]
    if tau == 0:
        c.wt(e, "v_A3bc03")
    else:
        c.wt_raw(e, "s_l2", 16)
        c.wt(e, "ev_s4_0_7")
    for mg in range(16):
        for part in (0, 1):
            g = GIDX[("s1", tau, mg, part)]
            if tau == 1 and mg == 0 and part == 1:
                c.wt(e, "v_bc4_0")
            if tau == 0 and mg == 0 and part == 1:
                c.wt(e, "v_A3bc")
            if mg >= 1:
                c.wt(e, [f"ev_s1a_{tau}_{mg - 1}", f"ev_s1b_{tau}_{mg - 1}"][part])
            _wt_w(c, e, g)
            ins = None
            for kk in range(8):
                if tau == 0 and mg == 0 and part == 0:
                    c.wt(e, f"v_ln2_{kk}")
                for mi in range(2):
                    lhsT = t["wbuf"][:, g % NBUF, kk, mi * 128:(mi + 1) * 128]
                    for n2 in range(2):
                        ins = e.matmul(ps[:, 4 * part + 2 * mi + n2, :], lhsT,
                                       t["ln2_t"][:, kk, n2 * 512:(n2 + 1) * 512],
                                       start=(kk == 0), stop=(kk == 7))
            c.inc(ins, "c_w")
            c.mark(f"pe_s1{'ab'[part]}_{tau}_{mg}", "c_w")
    for st, rhs in (("s2", "h1_t"), ("s3", "h2_t")):
        prev_ev = f"ev_s1b_{tau}_15" if st == "s2" else f"ev_s2_{tau}_15"
        c.wt(e, prev_ev)
        for mg in range(16):
            se = mg % 2
            if mg >= 2:
                c.wt(e, f"ev_{st}_{tau}_{mg - 2}")
            for kg in range(4):
                g = GIDX[(st, tau, mg, kg)]
                _wt_w(c, e, g)
                ins = None
                for kk in range(8):
                    k = kg * 8 + kk
                    for mi in range(2):
                        lhsT = t["wbuf"][:, g % NBUF, kk, mi * 128:(mi + 1) * 128]
                        for n2 in range(2):
                            ins = e.matmul(
                                ps[:, 4 * se + 2 * mi + n2, :], lhsT,
                                t[rhs][:, k, n2 * 512:(n2 + 1) * 512],
                                start=(k == 0), stop=(k == 31))
                c.inc(ins, "c_w")
            c.mark(f"pe_{st}_{tau}_{mg}", "c_w")


def em_B1_scalar(c, e, tau):
    t = c.t
    ps = t["ps"]
    if tau == 0:
        c.wt_raw(e, "s_sp", 16 * 16)
    else:
        c.wt(e, "v_C2_0")
    for mg in range(16):
        c.wt(e, f"pe_s1a_{tau}_{mg}")
        if mg >= 1:
            c.wt(e, f"ev_s1b_{tau}_{mg - 1}")
        ins = None
        for mi in range(2):
            m = 2 * mg + mi
            for n2 in range(2):
                ins = e.activation(t["t1_buf"][:, mi, n2 * 512:(n2 + 1) * 512],
                                   ps[:, 2 * mi + n2, :], AF.Silu,
                                   scale=t["s1c"][:, m:m + 1])
        c.inc(ins, "c_a")
        c.mark(f"ev_s1a_{tau}_{mg}", "c_a")
    for st, dst, scol in (("s2", "h2_t", "hs1c"), ("s3", "h1_t", "hs2c")):
        for mg in range(16):
            se = mg % 2
            c.wt(e, f"pe_{st}_{tau}_{mg}")
            if tau == 0 and st == "s2" and mg == 0:
                c.wt_raw(e, "s_sp", 16 * 16)
                src2 = t["ln2sp"].ap().rearrange("(a p) n -> p a n", p=128)
                c.inc(e.dma_start(t["ln2_t"][:], src2), "s_l2")
            if tau == 1 and st == "s2" and mg == 0:
                c.wt(e, "v_yc_0_7")
                c.wt_raw(e, "s_y", 16 * 8)
            ins = None
            for mi in range(2):
                m = 2 * mg + mi
                for n2 in range(2):
                    ins = e.activation(t[dst][:, m, n2 * 512:(n2 + 1) * 512],
                                       ps[:, 4 * se + 2 * mi + n2, :], AF.Gelu,
                                       scale=t[scol][:, m:m + 1])
            c.inc(ins, "c_a")
            c.mark(f"ev_{st}_{tau}_{mg}", "c_a")


def em_B1_vector(c, e, tau):
    t = c.t
    ps = t["ps"]
    if tau == 0:
        c.wt_raw(e, "s_sp", 16 * 16)
    else:
        c.wt(e, "pe_s4_0_7")
    for mg in range(16):
        c.wt(e, f"pe_s1b_{tau}_{mg}")
        c.wt(e, f"ev_s1a_{tau}_{mg}")
        ins = None
        for mi in range(2):
            m = 2 * mg + mi
            for n2 in range(2):
                ins = e.scalar_tensor_tensor(
                    t["h1_t"][:, m, n2 * 512:(n2 + 1) * 512],
                    ps[:, 4 + 2 * mi + n2, :], t["s2c"][:, m:m + 1],
                    t["t1_buf"][:, mi, n2 * 512:(n2 + 1) * 512],
                    OP.mult, OP.mult)
        c.inc(ins, "c_v")
        c.mark(f"ev_s1b_{tau}_{mg}", "c_v")



def em_B2_sync(c, e, tau):
    t = c.t
    em_B2_sync_wloads(c, e, tau)
    c.wt_raw(e, "s_sp", 16 * 8)
    if B2_LEVEL < 4:
        return
    c.wt(e, f"pe_s3_{tau}_15")
    for ch in range(NCH):
        if ch >= 2 and B2_LEVEL >= 6:
            c.wt(e, f"v_yc_{tau}_{ch - 2}")
            c.inc(e.dma_start(
                t["y_h"].ap()[(ch - 2) * 128:(ch - 1) * 128,
                              tau * 1024:(tau + 1) * 1024],
                t["y_t"][:, ch % 2, :]), "s_y")
        if tau == 1 and ch < 2:
            c.wt(e, f"v_yc_0_{6 + ch}")
        src = t["x1sp"].ap()[ch * 128:(ch + 1) * 128,
                             tau * 1024:(tau + 1) * 1024]
        c.inc(e.dma_start(t["x1r"][:, ch % 2, :], src), "s_x1r")
    if B2_LEVEL >= 6:
        for ch in (6, 7):
            c.wt(e, f"v_yc_{tau}_{ch}")
            c.inc(e.dma_start(
                t["y_h"].ap()[ch * 128:(ch + 1) * 128,
                              tau * 1024:(tau + 1) * 1024],
                t["y_t"][:, ch % 2, :]), "s_y")
        if tau == 1:
            c.wt_raw(e, "s_y", 16 * 16)


def em_B2_gpsimd(c, e, tau):
    lo = _L_B1_0_END if tau == 0 else _L_B1_1_END
    hi = _L_B2_0_END if tau == 0 else _L_B2_1_END
    for g in range(lo, hi):
        if QID[g] == 0:
            _em_wload(c, e, g)


def em_B2_sync_wloads(c, e, tau):
    lo = _L_B1_0_END if tau == 0 else _L_B1_1_END
    hi = _L_B2_0_END if tau == 0 else _L_B2_1_END
    for g in range(lo, hi):
        if QID[g] == 1:
            _em_wload(c, e, g)


def em_B2_tensor(c, e, tau):
    t = c.t
    ps = t["ps"]
    c.wt(e, f"ev_s3_{tau}_15")

    def stats(m):
        c.wt(e, f"ev_s4sq_{tau}_{m}")
        ins = None
        for n2 in range(2):
            ins = e.matmul(ps[0:1, 4 + n2, :], t["cst_col"][:],
                           t["ffn_t"][:, m, n2 * 512:(n2 + 1) * 512],
                           start=(m == 0), stop=(m == 7),
                           skip_group_check=True)
            ins = e.matmul(ps[0:1, 6 + n2, :], t["cst_col"][:],
                           t["sqf"][:, m % 2, n2 * 512:(n2 + 1) * 512],
                           start=(m == 0), stop=(m == 7),
                           skip_group_check=True)
        c.inc(ins, "c_t")
        c.mark(f"pe_s4stats_{tau}_{m}", "c_t")

    for m in range(8):
        if m >= 2:
            c.wt(e, f"ev_s4_{tau}_{m - 2}")
        for kg in range(4):
            g = GIDX[("s4", tau, m // 2, kg)]
            _wt_w(c, e, g)
            ins = None
            for kk in range(8):
                k = kg * 8 + kk
                lhsT = t["wbuf"][:, g % NBUF, kk,
                                 (m % 2) * 128:(m % 2 + 1) * 128]
                for n2 in range(2):
                    ins = e.matmul(ps[:, 2 * (m % 2) + n2, :], lhsT,
                                   t["h1_t"][:, k, n2 * 512:(n2 + 1) * 512],
                                   start=(k == 0), stop=(k == 31))
            if m % 2 == 1:
                c.inc(ins, "c_w")
            elif kg == 3:
                c.inc(ins, "c_t")
        c.mark(f"pe_s4_{tau}_{m}", "c_w" if m % 2 == 1 else "c_t")
        if m >= 1 and B2_LEVEL >= 2:
            stats(m - 1)
    if B2_LEVEL >= 2:
        stats(7)
    if B2_LEVEL < 3:
        return
    c.wt(e, f"v_C2_{tau}")
    r4 = t["rowb"][0:1, 0, :]
    r5 = t["rowb"][0:1, 1, :]
    ins = None
    for n2 in range(2):
        ins = e.matmul(ps[:, 4 + n2, :], t["ones_rowb"][:],
                       r4[0:1, n2 * 512:(n2 + 1) * 512], start=True, stop=True)
    for n2 in range(2):
        ins = e.matmul(ps[:, 6 + n2, :], t["ones_rowb"][:],
                       r5[0:1, n2 * 512:(n2 + 1) * 512], start=True, stop=True)
    c.inc(ins, "c_t")
    c.mark(f"t_bc4_{tau}", "c_t")


def em_B2_scalar(c, e, tau):
    t = c.t
    ps = t["ps"]
    if tau == 1:
        c.wt(e, "v_yc_0_7")
    for m in range(8):
        c.wt(e, f"pe_s4_{tau}_{m}")
        if m >= 2 and B2_LEVEL >= 2:
            c.wt(e, f"pe_s4stats_{tau}_{m - 2}")
        ins = None
        for n2 in range(2):
            ins = e.activation(t["ffn_t"][:, m, n2 * 512:(n2 + 1) * 512],
                               ps[:, 2 * (m % 2) + n2, :], AF.Copy,
                               scale=t["s3c"][:, m:m + 1])
        c.inc(ins, "c_a")
        c.mark(f"ev_s4_{tau}_{m}", "c_a")
        for n2 in range(2):
            ins = e.activation(t["sqf"][:, m % 2, n2 * 512:(n2 + 1) * 512],
                               t["ffn_t"][:, m, n2 * 512:(n2 + 1) * 512],
                               AF.Square)
        c.inc(ins, "c_a")
        c.mark(f"ev_s4sq_{tau}_{m}", "c_a")
    if B2_LEVEL < 3:
        return
    c.wt(e, f"v_C1_{tau}")
    tmp2 = t["sqf"][0:1, 1, :]
    r4 = t["y_t"][0:1, 0, :]
    s4 = t["t1_buf"][0:1, 1, :]
    c.ss(e.activation(tmp2, s4, AF.Ln), e, "c_a")
    c.ss(e.activation(r4, tmp2, AF.Exp, scale=-0.5), e, "c_a")
    ins = e.activation(t["rowb"][0:1, 0, :], r4, AF.Copy)
    c.inc(ins, "c_a")
    c.mark(f"a_C_{tau}", "c_a")


def em_B2_vector(c, e, tau):
    t = c.t
    ps = t["ps"]
    if B2_LEVEL < 3:
        return
    m4 = t["t1_buf"][0:1, 0, :]
    s4 = t["t1_buf"][0:1, 1, :]
    tmp = t["sqf"][0:1, 0, :]
    r5 = t["y_t"][0:1, 1, :]
    c.wt(e, f"pe_s4stats_{tau}_7")
    c.ss(e.tensor_copy(m4.rearrange("p (a n) -> p a n", a=2),
                       ps[0:1, 4:6, :]), e, "c_v")
    c.ss(e.tensor_mul(tmp, m4, m4), e, "c_v")
    c.ss(e.tensor_sub(s4.rearrange("p (a n) -> p a n", a=2),
                      ps[0:1, 6:8, :],
                      tmp.rearrange("p (a n) -> p a n", a=2)), e, "c_v")
    ins = e.tensor_scalar(s4, s4, 0.0, EPS_LN, OP.max, OP.add)
    c.inc(ins, "c_v")
    c.mark(f"v_C1_{tau}", "c_v")
    c.wt(e, f"a_C_{tau}")
    r4 = t["y_t"][0:1, 0, :]
    ins = e.tensor_mul(t["rowb"][0:1, 1, :], m4, r4)
    c.inc(ins, "c_v")
    c.mark(f"v_C2_{tau}", "c_v")
    c.wt(e, f"t_bc4_{tau}")
    e.tensor_copy(t["r3_b"][:].rearrange("p (a n) -> p a n", a=2),
                  ps[:, 4:6, :])
    ins = e.tensor_copy(t["mr3_b"][:].rearrange("p (a n) -> p a n", a=2),
                        ps[:, 6:8, :])
    c.inc(ins, "c_v")
    c.mark(f"v_bc4_{tau}", "c_v")
    if B2_LEVEL < 4:
        return
    for ch in range(NCH):
        c.wt_raw(e, "s_x1r", 16 * (8 * tau + ch + 1))
        if B2_LEVEL >= 6:
            c.wt_raw(e, "s_y", 16 * (8 * tau + ch - 1))
        e.tensor_mul(t["sqf"][:, 0, :], t["ffn_t"][:, ch, :], t["r3_b"][:])
        e.tensor_sub(t["sqf"][:, 1, :], t["sqf"][:, 0, :], t["mr3_b"][:])
        ins = e.tensor_add(t["y_t"][:, ch % 2, :], t["x1r"][:, ch % 2, :],
                           t["sqf"][:, 1, :])
        c.inc(ins, "c_v")
        c.mark(f"v_yc_{tau}_{ch}", "c_v")



def build_nc(dbg=False, n_blocks=6):
    nc = bass.Bass("TRN2", target_bir_lowering=False)
    tn = {}
    tn["x_h"] = nc.dram_tensor("x", [D, T], F32, kind="ExternalInput")
    tn["wgin_h"] = nc.dram_tensor("wgin", [8, 128, 128], F32, kind="ExternalInput")
    tn["wga_h"] = nc.dram_tensor("wga", [8, 128, 128], F32, kind="ExternalInput")
    tn["bcin_h"] = nc.dram_tensor("bcin", [128, 8], F32, kind="ExternalInput")
    tn["bca_h"] = nc.dram_tensor("bca", [128, 8], F32, kind="ExternalInput")
    tn["acol_h"] = nc.dram_tensor("acol", [128, 8], F32, kind="ExternalInput")
    tn["w1t_h"] = nc.dram_tensor("w1t", [1024, 4096], F32, kind="ExternalInput")
    tn["w2t_h"] = nc.dram_tensor("w2t", [1024, 4096], F32, kind="ExternalInput")
    tn["h1t_h"] = nc.dram_tensor("h1t", [4096, 4096], F32, kind="ExternalInput")
    tn["h2t_h"] = nc.dram_tensor("h2t", [4096, 4096], F32, kind="ExternalInput")
    tn["w3t_h"] = nc.dram_tensor("w3t", [4096, 1024], F32, kind="ExternalInput")
    tn["s1c_h"] = nc.dram_tensor("s1c", [128, 32], F32, kind="ExternalInput")
    tn["s2c_h"] = nc.dram_tensor("s2c", [128, 32], F32, kind="ExternalInput")
    tn["hs1c_h"] = nc.dram_tensor("hs1c", [128, 32], F32, kind="ExternalInput")
    tn["hs2c_h"] = nc.dram_tensor("hs2c", [128, 32], F32, kind="ExternalInput")
    tn["s3c_h"] = nc.dram_tensor("s3c", [128, 8], F32, kind="ExternalInput")
    tn["x1sp"] = nc.dram_tensor("x1sp", [D, T], F32,
                                kind="ExternalOutput" if dbg else "Internal")
    tn["ln2sp"] = nc.dram_tensor("ln2sp", [D, 1024], BF16,
                                 kind="ExternalOutput" if dbg else "Internal")
    tn["y_h"] = nc.dram_tensor("y", [D, T], F32, kind="ExternalOutput")
    if dbg:
        tn["dbg_h"] = nc.dram_tensor("dbg_h", [D, T], F32, kind="ExternalOutput")
    if dbg == 2:
        for nm in ("dbg_ln1", "dbg_a", "dbg_nrm", "dbg_cum", "dbg_gx", "dbg_mu"):
            tn[nm] = nc.dram_tensor(nm, [D, T], F32, kind="ExternalOutput")

    tn["wg_in"] = nc.alloc_sbuf_tensor("wg_in", [128, 8, 128], F32)
    tn["wg_a"] = nc.alloc_sbuf_tensor("wg_a", [128, 8, 128], F32)
    tn["bcol_in"] = nc.alloc_sbuf_tensor("bcol_in", [128, 8], F32)
    tn["bcol_a"] = nc.alloc_sbuf_tensor("bcol_a", [128, 8], F32)
    tn["acol"] = nc.alloc_sbuf_tensor("acol_sb", [128, 8], F32)
    tn["nspcol"] = nc.alloc_sbuf_tensor("nspcol", [128, 8], F32)
    tn["tiny"] = nc.alloc_sbuf_tensor("tiny", [128, 32], F32)
    tn["s1c"] = nc.alloc_sbuf_tensor("s1c_sb", [128, 32], F32)
    tn["s2c"] = nc.alloc_sbuf_tensor("s2c_sb", [128, 32], F32)
    tn["hs1c"] = nc.alloc_sbuf_tensor("hs1c_sb", [128, 32], F32)
    tn["hs2c"] = nc.alloc_sbuf_tensor("hs2c_sb", [128, 32], F32)
    tn["s3c"] = nc.alloc_sbuf_tensor("s3c_sb", [128, 8], F32)
    tn["ones_row"] = nc.alloc_sbuf_tensor("ones_row", [1, 128], F32)
    tn["ones_rowb"] = nc.alloc_sbuf_tensor("ones_rowb", [1, 128], BF16)
    tn["cst_col"] = nc.alloc_sbuf_tensor("cst_col", [128, 1], F32)
    tn["wbuf"] = nc.alloc_sbuf_tensor("wbuf", [128, NBUF, 8, 256], BF16)
    tn["ln2_t"] = nc.alloc_sbuf_tensor("ln2_t", [128, 8, 1024], BF16)
    tn["ps"] = nc.alloc_psum_tensor("ps", [128, 8, 512], F32)

    semnames = ["s_c", "s_x", "s_x2", "s_x3", "s_w", "s_w2", "s_sp", "s_l2", "s_x1r", "s_y", "s_dbg",
                "s_d3v", "s_d3a", "c_v", "c_a", "c_p", "c_t", "c_w", "bar"]
    sems = {n: nc.alloc_semaphore(n) for n in semnames}

    ctx = Ctx()
    ctx.sems = sems
    ctx.dbg = dbg

    def emit_all(c):
        blocks = []
        blocks.append({
            "sync": lambda e: em_S0_sync(c, e),
            "vector": lambda e: em_S0_vector(c, e),
            "scalar": lambda e: em_S0_scalar(c, e),
            "gpsimd": lambda e: None,
            "tensor": lambda e: None,
        })
        blocks.append({
            "sync": lambda e: em_A_sync(c, e),
            "gpsimd": lambda e: em_A_pool(c, e, dbg),
            "scalar": lambda e: em_A_scalar(c, e),
            "vector": lambda e: em_A_vector(c, e),
            "tensor": lambda e: em_A_tensor(c, e),
        })
        for tau in (0, 1):
            blocks.append({
                "sync": lambda e, tau=tau: em_B1_sync(c, e, tau),
                "gpsimd": lambda e, tau=tau: em_B1_gpsimd(c, e, tau),
                "tensor": lambda e, tau=tau: em_B1_tensor(c, e, tau),
                "scalar": lambda e, tau=tau: em_B1_scalar(c, e, tau),
                "vector": lambda e, tau=tau: em_B1_vector(c, e, tau),
            })
            blocks.append({
                "sync": lambda e, tau=tau: em_B2_sync(c, e, tau),
                "gpsimd": lambda e, tau=tau: em_B2_gpsimd(c, e, tau),
                "tensor": lambda e, tau=tau: em_B2_tensor(c, e, tau),
                "scalar": lambda e, tau=tau: em_B2_scalar(c, e, tau),
                "vector": lambda e, tau=tau: em_B2_vector(c, e, tau),
            })
        return blocks

    mock = _MockEng()
    mockt = _MockTensor()
    ctx.t = {k: mockt for k in list(tn.keys()) + [
        "xc2", "pA0", "pB0", "pC0", "pD0", "pA1", "pB1", "pC1", "pD1",
        "T0", "T1", "T2", "T3", "T4", "T5",
        "r_b", "mr_b", "h_all", "l2c2", "ln2_t", "h1_t", "h2_t", "t1_buf",
        "ffn_t", "sqf", "x1r", "y_t", "r3_b", "mr3_b", "rowb"]}
    for blk in emit_all(ctx)[:n_blocks]:
        for en in ("sync", "gpsimd", "scalar", "vector", "tensor"):
            blk[en](mock)
    missing = ctx.refs - set(ctx.M.keys())
    assert not missing, f"unresolved marks: {sorted(missing)}"

    ctx.counting = False
    ctx.t = tn
    ctx._ssn = 0
    blocks = emit_all(ctx)
    engs = ("sync", "gpsimd", "scalar", "vector", "tensor")

    with nc.Block() as block:
        def chain(blk):
            for en in engs:
                getattr(block, en)(lambda eng, en=en, blk=blk: blk[en](eng))

        chain(blocks[0])
        if n_blocks >= 2:
            sa = _ScopeA(nc, tn)
            sa.__enter__()
            chain(blocks[1])
            sa.__exit__(None, None, None)
        if n_blocks >= 3:
            _ScopeBOuter(nc, tn).__enter__()
            for tau in (0, 1):
                if n_blocks >= 2 + 2 * tau + 1:
                    b1 = _ScopeB1(nc, tn)
                    b1.__enter__()
                    chain(blocks[2 + 2 * tau])
                    b1.__exit__(None, None, None)
                if n_blocks >= 3 + 2 * tau + 1:
                    b2 = _ScopeB2(nc, tn)
                    b2.__enter__()
                    chain(blocks[3 + 2 * tau])
                    b2.__exit__(None, None, None)
    return nc


class _ScopeA:
    def __init__(self, nc, tn):
        self.nc, self.tn = nc, tn
        self.keys = []

    def __enter__(self):
        nc, tn = self.nc, self.tn
        self.cms = []
        def add(key, shape, dt):
            cm = nc.sbuf_tensor(key, shape, dt)
            tn[key] = cm.__enter__()
            self.cms.append(cm)
            self.keys.append(key)
        add("xc2", [128, 2, T], F32)
        for p in (0, 1):
            for nm in "ABCD":
                add(f"p{nm}{p}", [128, T], F32)
        add("r_b", [128, T], F32)
        add("mr_b", [128, T], F32)
        add("h_all", [128, 8, T], F32)
        add("l2c2", [128, 2, 1024], BF16)
        for alias, real in (("T0", "pB0"), ("T1", "pC0"), ("T2", "pD0"),
                            ("T3", "pB1"), ("T4", "pC1"), ("T5", "pD1")):
            tn[alias] = tn[real]
        return self

    def __exit__(self, *a):
        for cm in reversed(self.cms):
            cm.__exit__(None, None, None)
        return False


class _ScopeBOuter:

    def __init__(self, nc, tn):
        self.nc, self.tn = nc, tn

    def __enter__(self):
        nc, tn = self.nc, self.tn
        tn["h1_t"] = nc.alloc_sbuf_tensor("h1_t", [128, 32, 1024], BF16)
        tn["t1_buf"] = nc.alloc_sbuf_tensor("t1_buf", [128, 2, 1024], F32)
        return self

    def __exit__(self, *a):
        return False


class _ScopeB1:
    _n = [0]

    def __init__(self, nc, tn):
        self.nc, self.tn = nc, tn

    def __enter__(self):
        self._n[0] += 1
        self.cm = self.nc.sbuf_tensor(f"h2_t_{self._n[0]}", [128, 32, 1024], BF16)
        self.tn["h2_t"] = self.cm.__enter__()
        return self

    def __exit__(self, *a):
        self.cm.__exit__(None, None, None)
        return False


class _ScopeB2:
    _n = [0]

    def __init__(self, nc, tn):
        self.nc, self.tn = nc, tn

    def __enter__(self):
        nc, tn = self.nc, self.tn
        self._n[0] += 1
        self.cms = []
        for key, shape, dt in [("ffn_t", [128, 8, 1024], F32),
                               ("sqf", [128, 2, 1024], F32),
                               ("x1r", [128, 2, 1024], F32),
                               ("y_t", [128, 2, 1024], F32),
                               ("r3_b", [128, 1024], F32),
                               ("mr3_b", [128, 1024], F32),
                               ("rowb", [1, 2, 1024], BF16)]:
            cm = nc.sbuf_tensor(f"{key}_{self._n[0]}", shape, dt)
            tn[key] = cm.__enter__()
            self.cms.append(cm)
        return self

    def __exit__(self, *a):
        for cm in reversed(self.cms):
            cm.__exit__(None, None, None)
        return False



_NC_CACHE = {}


def _get_nc(dbg=False):
    if dbg not in _NC_CACHE:
        _NC_CACHE[dbg] = build_nc(dbg)
    return _NC_CACHE[dbg]


def _layout_inputs(inputs):
    f = np.ascontiguousarray
    x = np.asarray(inputs["L0_attn_x"], dtype=np.float32)
    common = {
        "wgin": f(np.asarray(inputs["L0_attn_w_in"], np.float32)),
        "wga": f(np.asarray(inputs["L0_attn_w_a"], np.float32)),
        "bcin": f(np.asarray(inputs["L0_attn_b_in"], np.float32).T),
        "bca": f(np.asarray(inputs["L0_attn_b_a"], np.float32).T),
        "acol": f(np.asarray(inputs["L0_attn_a_param"], np.float32)
                  .reshape(8, 128).T),
        "w1t": f(np.asarray(inputs["L0_ffn_w1_w"], np.float32).T),
        "w2t": f(np.asarray(inputs["L0_ffn_w2_w"], np.float32).T),
        "h1t": f(np.asarray(inputs["L0_ffn_h1_w"], np.float32).T),
        "h2t": f(np.asarray(inputs["L0_ffn_h2_w"], np.float32).T),
        "w3t": f(np.asarray(inputs["L0_ffn_w3_w"], np.float32).T),
        "s1c": f(np.asarray(inputs["L0_ffn_w1_s"], np.float32)
                 .reshape(32, 128).T),
        "s2c": f(np.asarray(inputs["L0_ffn_w2_s"], np.float32)
                 .reshape(32, 128).T),
        "hs1c": f(np.asarray(inputs["L0_ffn_h1_s"], np.float32)
                  .reshape(32, 128).T),
        "hs2c": f(np.asarray(inputs["L0_ffn_h2_s"], np.float32)
                  .reshape(32, 128).T),
        "s3c": f(np.asarray(inputs["L0_ffn_w3_s"], np.float32)
                 .reshape(8, 128).T),
    }
    in_maps = []
    for b in range(8):
        m = dict(common)
        m["x"] = f(x[b].T)
        in_maps.append(m)
    return in_maps


def kernel(**inputs):
    nc = _get_nc(dbg=False)
    in_maps = _layout_inputs(inputs)
    res = run_bass_kernel_spmd(nc, in_maps, core_ids=list(range(8)))
    out = np.stack([np.asarray(r["y"]).T for r in res.results], axis=0)
    return np.ascontiguousarray(out, dtype=np.float32)
